# version 21
# speedup vs baseline: 1.1873x; 1.0198x over previous
import contextlib

import numpy as np
import ml_dtypes

import concourse.bass as bass
import concourse.mybir as mybir
from concourse.bass_utils import run_bass_kernel_spmd

F32 = mybir.dt.float32
BF16 = mybir.dt.bfloat16
I32 = mybir.dt.int32
AF = mybir.ActivationFunctionType
ALU = mybir.AluOpType
AX = mybir.AxisListType

BPC, T, C, H, DH, FF = 2, 2048, 1024, 16, 64, 3072
NCH, FCH, NTIL, NST = 8, 24, 16, 4

ENGINES = ("sync", "vec", "act", "pe", "pool")


class Ev:
    __slots__ = ("eng", "narrow", "amt", "sem", "val", "key")

    def __init__(self, eng, narrow, amt, key=None):
        self.eng, self.narrow, self.amt = eng, narrow, amt
        self.sem = None
        self.val = None
        self.key = key


class Res:
    __slots__ = ("wev", "revs")

    def __init__(self):
        self.wev = None
        self.revs = []


class Sched:
    def __init__(self):
        self.per_eng = {e: [] for e in ENGINES}
        self.last = {e: None for e in ENGINES}
        self.last_comp = {e: None for e in ENGINES}
        self.last_key = {}

    def step(self, eng, f, reads=(), writes=(), extra=(), narrow=False, dma=False,
             key=None):
        deps = []
        for r in reads:
            if r.wev is not None:
                deps.append(r.wev)
        for w in writes:
            deps.extend(w.revs)
            if w.wev is not None:
                deps.append(w.wev)
        for ev0 in extra:
            if ev0 is not None:
                deps.append(ev0)
        if dma and key is None:
            key = id(writes[0]) if writes else "dma_misc"
        ev = Ev(eng, narrow or dma, 16 if dma else 1, key=key)
        self.per_eng[eng].append((f, deps, ev))
        self.last[eng] = ev
        if dma:
            self.last_key[key] = ev
        else:
            self.last_comp[eng] = ev
        for r in reads:
            r.revs.append(ev)
        for w in writes:
            w.wev = ev
            w.revs = []
        return ev

    def fence(self, resources):
        evs = [self.last_comp[e] for e in ENGINES
               if self.last_comp[e] is not None]
        evs += list(self.last_key.values())
        for ev in evs:
            ev.narrow = True
        for r in resources:
            r.wev = None
            r.revs = list(evs)

    def finalize(self):
        needed = set()
        for eng in ENGINES:
            for f, deps, ev in self.per_eng[eng]:
                for d in deps:
                    if d.eng != eng or d.narrow:
                        needed.add(id(d))
        counts = {}
        emit_order = []
        for eng in ENGINES:
            emit_order.extend(self.per_eng[eng])
        for eng in ENGINES:
            for f, deps, ev in self.per_eng[eng]:
                if ev.amt == 16:
                    sem = ("k", ev.key)
                    counts[sem] = counts.get(sem, 0) + 16
                    ev.sem = sem
                    ev.val = counts[sem]
                elif id(ev) in needed:
                    sem = ("e", eng)
                    counts[sem] = counts.get(sem, 0) + 1
                    ev.sem = sem
                    ev.val = counts[sem]
        self.sem_ids = sorted(set(counts.keys()), key=repr)
        out = {}
        for eng in ENGINES:
            lst = []
            hi = {}
            for f, deps, ev in self.per_eng[eng]:
                waits = []
                for d in deps:
                    if d.sem is None or (d.eng == eng and not d.narrow
                                         and d.amt != 16):
                        continue
                    if hi.get(d.sem, -1) >= d.val:
                        continue
                    hi[d.sem] = d.val
                    waits.append((d.sem, d.val))
                lst.append((waits, f, ev))
            out[eng] = lst
        return out


def _alibi_consts():
    slopes = np.array([2.0 ** (-(h + 1) / 2.0) for h in range(H)], np.float32)
    kidx = np.arange(T, dtype=np.float32)
    alF = (slopes[:, None] * kidx[None, :]).astype(ml_dtypes.bfloat16)
    p = np.arange(128, dtype=np.float32)
    alK = np.zeros((128, 256), np.float32)
    for h in range(H):
        for j in range(16):
            alK[:, h * 16 + j] = slopes[h] * (128.0 * j + p)
    pad = np.zeros((128, 272), ml_dtypes.bfloat16)
    pad[0, 0:128] = 1.0
    pad[64, 128:256] = 1.0
    pad[0:64, 256] = 1.0
    pad[64:128, 257] = 1.0
    return np.ascontiguousarray(alF), alK, np.ascontiguousarray(pad)


def build_program():
    import os
    nc = bass.Bass("TRN2", target_bir_lowering=False, detect_race_conditions=False)
    s = Sched()

    x_in = nc.dram_tensor("x", [BPC, T, C], F32, kind="ExternalInput").ap()
    w_attn = nc.dram_tensor("w_attn", [C, 3 * C], F32, kind="ExternalInput").ap()
    b_attn = nc.dram_tensor("b_attn", [3 * C], F32, kind="ExternalInput").ap()
    w_proj = nc.dram_tensor("w_proj", [C, C], F32, kind="ExternalInput").ap()
    b_proj = nc.dram_tensor("b_proj", [C], F32, kind="ExternalInput").ap()
    temp = nc.dram_tensor("temp", [1], F32, kind="ExternalInput").ap()
    w_gate = nc.dram_tensor("w_gate", [FF, C], F32, kind="ExternalInput").ap()
    w_up = nc.dram_tensor("w_up", [FF, C], F32, kind="ExternalInput").ap()
    w_down = nc.dram_tensor("w_down", [C, FF], F32, kind="ExternalInput").ap()
    c_in = nc.dram_tensor("cvec", [BPC, C], F32, kind="ExternalInput").ap()
    wmod = nc.dram_tensor("wmod", [C, 6 * C], F32, kind="ExternalInput").ap()
    out = nc.dram_tensor("out", [BPC, T, C], F32, kind="ExternalOutput").ap()

    x1sp = nc.dram_tensor("x1sp", [BPC, T, C], F32, kind="Internal").ap()
    h2sp = nc.dram_tensor("h2sp", [BPC, NCH, 128, T], BF16, kind="Internal").ap()
    upsp = nc.dram_tensor("upsp", [BPC, FCH, 128, T], BF16, kind="Internal").ap()
    gasp = nc.dram_tensor("gasp", [BPC, FCH, 128, T], BF16, kind="Internal").ap()
    gssp = nc.dram_tensor("gssp", [BPC, FCH, 128, T], BF16, kind="Internal").ap()
    osp = nc.dram_tensor("osp", [BPC, FCH, 128, T], BF16, kind="Internal").ap()

    alF_np, alK_np, pad_np = _alibi_consts()
    c_alF = nc.inline_tensor(alF_np.view(np.uint16), "alibiFc").ap().bitcast(BF16)
    c_alK = nc.inline_tensor(alK_np, "alibiKc").ap()
    c_pad = nc.inline_tensor(pad_np.view(np.uint16), "padonec").ap().bitcast(BF16)
    c_idb = nc.inline_tensor(np.eye(128, dtype=ml_dtypes.bfloat16).view(np.uint16),
                             "identbc").ap().bitcast(BF16)
    c_idf = nc.inline_tensor(np.eye(128, dtype=np.float32), "identfc").ap()
    tri_np = np.where(np.arange(128)[:, None] <= np.arange(128)[None, :],
                      np.float32(1e30), np.float32(-1e30)).astype(np.float32)
    c_tri = nc.inline_tensor(tri_np, "trimaskc").ap()

    ctx = contextlib.ExitStack()
    sbuf = lambda nm, sh, dt=BF16: ctx.enter_context(nc.sbuf_tensor(nm, sh, dt))
    psum = lambda nm, sh, dt=F32: ctx.enter_context(nc.psum_tensor(nm, sh, dt))

    with ctx:
        ctx.enter_context(nc.allow_low_precision(
            "bf16 matmul operands / bf16-rounded normalization by design; "
            "fp32 PSUM accumulation and fp32 statistics throughout"))
        arA = sbuf("arA", [128, 16384])
        arB = sbuf("arB", [128, 16384])
        arC = sbuf("arC", [128, 16384])
        arD = sbuf("arD", [128, 16384])
        wpj = sbuf("wpj", [128, 8192])
        fscr = sbuf("fscr", [128, 6144], F32)
        fscr2 = sbuf("fsc2", [128, 1024], F32)
        sqb = sbuf("sqb", [128, 2048])
        qpad = sbuf("qpad", [128, 2048])
        uzpad = sbuf("uzp", [128, 2048])
        padc = sbuf("padc", [128, 272])
        ibufC = sbuf("ibc", [128, 512], I32)
        alK = sbuf("alK", [128, 256], F32)
        wcast = sbuf("wc", [128, 2048])
        spst = sbuf("spst", [128, 2048])
        iscr = sbuf("iscr", [128, 1024], I32)
        scal = sbuf("scal", [128, 64], F32)
        srow = sbuf("srow", [68, 64], F32)
        modv = sbuf("modv", [128, 96], F32)
        gabv = sbuf("gabv", [128, 16], F32)
        battn = sbuf("batt", [128, 16], F32)
        bprojc = sbuf("bpjc", [128, 8], F32)
        bvrow = sbuf("bvr", [128, 1024])
        onescol = sbuf("onc", [128, 4])
        onesrow = sbuf("onr", [65, 128])
        onesrf = sbuf("onrf", [1, 128], F32)
        identb = sbuf("idb", [128, 128])
        identf = sbuf("idf", [128, 128], F32)
        trimask = sbuf("tri", [128, 128], F32)
        mxacc = sbuf("mxac", [128, 16], F32)
        rrow = sbuf("rrow", [128, 512])
        crow = fscr[0:BPC, 4096:5120]
        siluc = sbuf("silc", [128, 2 * NCH], F32)

        pS = [psum("pS0", [128, 512]), psum("pS1", [128, 512]), psum("pS2", [128, 512])]
        pY = [psum("pY0", [128, 512]), psum("pY1", [128, 512])]
        pR = psum("pR", [128, 512])
        pT = [psum("pT0", [128, 1024], BF16), psum("pT1", [128, 1024], BF16)]
        pS = pS + [pT[0].ap().bitcast(F32), pT[1].ap().bitcast(F32)]

        FS = [fscr[:, i * 1024:(i + 1) * 1024] for i in range(6)]
        sig2k = fscr[:, 4096:6144]
        mview = modv[:, :].rearrange("p (b2 c6 g) -> p b2 c6 g", b2=2, c6=6)

        def mcol(b, comp, g):
            i = b * 48 + comp * 8 + g
            return modv[:, i:i + 1]

        R = Res
        rFS = [R() for _ in range(6)]
        rF2 = R()
        rSQ = R()
        rSP = R()
        rDQ = R()
        rPT = [R(), R(), R(), R(), None]
        rIS = R()
        rSC = [R() for _ in range(64)]
        rSR = R()
        rMX = [R() for _ in range(16)]
        rMod, rGab, rBat, rBpc, rBv, rSil = R(), R(), R(), R(), R(), R()
        rCrow = rFS[4]
        rOnes, rId, rTri = R(), R(), R()
        rQP = [R() for _ in range(4)]
        rUZ = [[R() for _ in range(4)] for _ in range(2)]
        rPadc, rALK = R(), R()
        rA_, rB_, rC_, rD_, rWPJ = R(), R(), R(), R(), R()
        rXp = [R(), R()]
        rRR = R()
        rPSb = [R(), R(), R()]
        rPYb = [R(), R()]
        rPR = R()
        rPTb = [R(), R()]
        rPSb = rPSb + [rPTb[0], rPTb[1]]
        rBI, rBQ = R(), R()
        rBIh = [R(), R(), R(), R()]
        rF2h = [R(), R()]
        rDS = [R(), R()]
        rASt = [R(), R()]
        rH2sp = [R(), R()]
        rX1sp = [R(), R()]
        rUpsp = [R(), R()]
        rGasp = [R(), R()]
        rGssp = [R(), R()]
        rOsp = [R(), R()]

        ALL_RES = [v for v in list(locals().values()) if isinstance(v, Res)]
        for lst in (rFS, rPT, rSC, rMX, rPSb, rPYb, rPTb, rH2sp, rX1sp, rUpsp,
                    rGasp, rGssp, rOsp, rQP, rUZ[0], rUZ[1], rDS,
                    [rBI, rBQ], rBIh, rASt, rF2h):
            ALL_RES.extend(r for r in lst if r is not None)

        def dma(dst, src, q="sync", reads=(), writes=(), extra=(), slow=False):
            return s.step("sync" if q == "sync" else "pool",
                          lambda e: e.dma_start(out=dst, in_=src,
                                                allow_slow_non_contiguous=slow),
                          reads=reads, writes=writes, extra=extra, dma=True)

        def vec(f, **kw):
            return s.step("vec", f, **kw)

        def act(f, **kw):
            return s.step("act", f, **kw)

        def pe(f, **kw):
            return s.step("pe", f, **kw)

        def pool(f, **kw):
            return s.step("pool", f, **kw)

        dma(srow[0:1, 0:1], temp.rearrange("(a b) -> a b", b=1), writes=[rSR])
        dma(identb[:, :], c_idb, writes=[rId])
        dma(identf[:, :], c_idf, writes=[rId])
        dma(trimask[:, :], c_tri, writes=[rTri])
        dma(crow[:, :], c_in, writes=[rCrow])
        dma(fscr[0:1, 0:3072], b_attn.rearrange("(a n) -> a n", a=1),
            writes=[rFS[0], rFS[1], rFS[2]])

        pool(lambda e: e.memset(onescol[:, :], 1.0), writes=[rOnes], narrow=True)
        pool(lambda e: e.memset(onesrow[:, :], 1.0), writes=[rOnes], narrow=True)
        pool(lambda e: e.memset(onesrf[0:1, :], 1.0), writes=[rOnes], narrow=True)
        pool(lambda e: e.memset(qpad[:, :], 0.0), writes=rQP, narrow=True)
        pool(lambda e: e.memset(uzpad[:, :], 0.0),
             writes=rUZ[0] + rUZ[1], narrow=True)
        pool(lambda e: e.memset(bvrow[:, :], 0.0), writes=[rBv], narrow=True)
        pool(lambda e: e.memset(rrow[:, :], 0.0), writes=[rRR], narrow=True)
        dma(padc[:, :], c_pad, writes=[rPadc])
        dma(alK[:, :], c_alK, writes=[rALK])
        pool(lambda e: e.memset(mxacc[:, :], 0.0), writes=list(rMX), narrow=True)

        vec(lambda e: e.tensor_scalar(srow[0:1, 1:2], srow[0:1, 0:1], 8.0, None,
                                      ALU.mult), reads=[rSR], writes=[rSR], narrow=True)
        vec(lambda e: e.reciprocal(srow[0:1, 2:3], srow[0:1, 1:2]),
            reads=[rSR], writes=[rSR], narrow=True)

        def bcast(src_ap, col, rds):
            pe(lambda e: e.matmul(pR[:, 0:1], onesrf[0:1, :], src_ap,
                                  start=True, stop=True),
               reads=[rOnes] + rds, writes=[rPR], narrow=True)
            return act(lambda e: e.copy(scal[:, col:col + 1], pR[:, 0:1]),
                       reads=[rPR], writes=[rSC[col]], narrow=True)

        bcast(srow[0:1, 2:3], 0, [rSR])

        for j in range(16):
            pe(lambda e, j=j: e.transpose(pR[0:128, 0:1],
                                          fscr[0:1, j * 128:(j + 1) * 128],
                                          identf[0:1, 0:1]),
               reads=[rFS[j // 8], rFS[2], rId], writes=[rPR], narrow=True)
            act(lambda e, j=j: e.copy(battn[:, j:j + 1], pR[:, 0:1]),
                reads=[rPR], writes=[rBat], narrow=True)
        vec(lambda e: e.tensor_scalar(battn[:, 8:16], battn[:, 8:16], scal[:, 0:1],
                                      None, ALU.mult),
            reads=[rSC[0], rBat], writes=[rBat], narrow=True)
        vec(lambda e: e.tensor_copy(bvrow[0:1, :], fscr[0:1, 2048:3072]),
            reads=[rFS[2]], writes=[rBv], narrow=True)

        dma(fscr[0:1, 3072:4096], b_proj.rearrange("(a n) -> a n", a=1),
            writes=[rFS[3]])
        for j in range(8):
            pe(lambda e, j=j: e.transpose(pR[0:128, 0:1],
                                          fscr[0:1, 3072 + j * 128:3200 + j * 128],
                                          identf[0:1, 0:1]),
               reads=[rFS[3], rId], writes=[rPR], narrow=True)
            act(lambda e, j=j: e.copy(bprojc[:, j:j + 1], pR[:, 0:1]),
                reads=[rPR], writes=[rBpc], narrow=True)

        act(lambda e: e.activation(crow[:, :], crow[:, :], AF.Silu),
            reads=[rCrow], writes=[rCrow], narrow=True)
        for g in range(NCH):
            pe(lambda e, g=g: e.transpose(pR[0:128, 0:BPC],
                                          crow[0:BPC, g * 128:(g + 1) * 128],
                                          identf[0:BPC, 0:BPC]),
               reads=[rCrow, rId], writes=[rPR], narrow=True)
            vec(lambda e, g=g: e.tensor_copy(siluc[:, g * 2:g * 2 + 2], pR[:, 0:BPC]),
                reads=[rPR], writes=[rSil], narrow=True)

        rWCh = [R(), R()]
        rPT[4] = rWCh[0]
        for mo in range(48):
            bsl = mo % 2
            dma(FS[bsl].rearrange("p (g m) -> p g m", m=128),
                wmod[:, mo * 128:(mo + 1) * 128]
                .rearrange("(g p) m -> p g m", p=128), writes=[rFS[bsl]])
            for g in range(NCH):
                pe(lambda e, b2=bsl, g=g: e.matmul(
                    pY[0][:, 0:BPC],
                    FS[b2][:, g * 128:(g + 1) * 128],
                    siluc[:, g * 2:g * 2 + 2], start=(g == 0), stop=(g == NCH - 1)),
                   reads=[rFS[bsl], rSil],
                   writes=([rPYb[0]] if g in (0, NCH - 1) else []))
            comp, g8 = mo // 8, mo % 8
            dst = modv[:, :].rearrange("p (b2 cg) -> p b2 cg", b2=2)[:, :, comp * 8 + g8]
            act(lambda e, dst=dst: e.copy(dst, pY[0][:, 0:BPC]),
                reads=[rPYb[0]], writes=[rMod], narrow=True)

        for comp in (1, 4):
            vec(lambda e, comp=comp: e.tensor_scalar(
                mview[:, :, comp, :], mview[:, :, comp, :], 1.0, None, ALU.add),
                reads=[rMod], writes=[rMod], narrow=True)
        for comp in (2, 5):
            gsl = mview[:, :, comp, :]
            act(lambda e, gsl=gsl: e.activation(
                scal[:, 8:24].rearrange("p (b2 g) -> p b2 g", b2=2), gsl, AF.Sign),
                reads=[rMod], writes=[rSC[8]], narrow=True)
            act(lambda e, gsl=gsl: e.activation(
                scal[:, 24:40].rearrange("p (b2 g) -> p b2 g", b2=2), gsl, AF.Abs),
                reads=[rMod], writes=[rSC[24]], narrow=True)
            vec(lambda e: e.tensor_scalar(scal[:, 24:40], scal[:, 24:40], 0.25, None,
                                          ALU.max),
                reads=[rSC[24]], writes=[rSC[24]], narrow=True)
            vec(lambda e, gsl=gsl: e.tensor_tensor(
                gsl, scal[:, 8:24].rearrange("p (b2 g) -> p b2 g", b2=2),
                scal[:, 24:40].rearrange("p (b2 g) -> p b2 g", b2=2), ALU.mult),
                reads=[rSC[8], rSC[24]], writes=[rMod], narrow=True)
        for b in range(BPC):
            vec(lambda e, b=b: e.tensor_tensor(
                gabv[:, b * 8:(b + 1) * 8], mview[:, b, 2, :], bprojc[:, :], ALU.mult),
                reads=[rMod, rBpc], writes=[rGab], narrow=True)

        s.fence(ALL_RES + rWCh)

        LN_E = {"sb": 40, "sq": None, "sqr": None, "ln": None, "lnr": None}
        LN_SETS = None

        def layernorm(src_res, src_ap, tmp_ap, tmp_res, par=0):
            P_ = LN_SETS[par]
            sb = P_["sb"]
            vec(lambda e: e.reduce_sum(scal[:, sb:sb + 1], src_ap, AX.X),
                reads=[src_res], writes=[rSC[sb]], narrow=True)
            vec(lambda e: e.tensor_scalar(scal[:, sb + 1:sb + 2],
                                          scal[:, sb:sb + 1], 1.0 / C,
                                          None, ALU.mult),
                reads=[rSC[sb]], writes=[rSC[sb + 1]], narrow=True)
            vec(lambda e: e.tensor_scalar(tmp_ap, src_ap, scal[:, sb + 1:sb + 2],
                                          None, ALU.subtract),
                reads=[src_res, rSC[sb + 1]], writes=[tmp_res])
            vec(lambda e: e.tensor_tensor(P_["sq"], tmp_ap, tmp_ap, ALU.mult),
                reads=[tmp_res], writes=P_["sqr"])
            vec(lambda e: e.reduce_sum(scal[:, sb + 2:sb + 3], P_["sq"], AX.X),
                reads=P_["sqr"], writes=[rSC[sb + 2]], narrow=True)
            vec(lambda e: e.tensor_scalar(scal[:, sb + 3:sb + 4],
                                          scal[:, sb + 2:sb + 3], 1.0 / C,
                                          1e-5, ALU.mult, ALU.add),
                reads=[rSC[sb + 2]], writes=[rSC[sb + 3]], narrow=True)
            act(lambda e: e.activation(scal[:, sb + 4:sb + 5],
                                       scal[:, sb + 3:sb + 4], AF.Sqrt),
                reads=[rSC[sb + 3]], writes=[rSC[sb + 4]], narrow=True)
            vec(lambda e: e.reciprocal(scal[:, sb + 5:sb + 6],
                                       scal[:, sb + 4:sb + 5]),
                reads=[rSC[sb + 4]], writes=[rSC[sb + 5]], narrow=True)
            vec(lambda e: e.tensor_scalar(P_["ln"], tmp_ap,
                                          scal[:, sb + 5:sb + 6],
                                          None, ALU.mult),
                reads=[tmp_res, rSC[sb + 5]], writes=P_["lnr"])

        LN_SETS = [
            {"sb": 40, "sq": sqb[:, 0:1024], "sqr": [rSQ],
             "ln": sqb[:, 1024:2048], "lnr": [rSQ]},
            {"sb": 50, "sq": uzpad[:, 0:1024],
             "sqr": [rUZ[0][0], rUZ[0][1], rUZ[1][0], rUZ[1][1]],
             "ln": uzpad[:, 1024:2048],
             "lnr": [rUZ[0][2], rUZ[0][3], rUZ[1][2], rUZ[1][3]]},
        ]

        def ln1_qkv(b):
            for ti in range(NTIL):
                xs = 2 + ti % 2
                par = ti % 2
                tmp, rtmp = (FS[4], rFS[4]) if par == 0 else (FS[5], rFS[5])
                lnb, lnres = LN_SETS[par]["ln"], LN_SETS[par]["lnr"]
                dma(FS[xs], x_in[b, ti * 128:(ti + 1) * 128, :], writes=[rFS[xs]])
                layernorm(rFS[xs], FS[xs], tmp, rtmp, par)
                for g in range(NCH):
                    pp = g % 2
                    pe(lambda e, g=g, pp=pp, lnb=lnb: e.transpose(
                        pT[pp][:, 0:128], lnb[:, g * 128:(g + 1) * 128],
                        identb[:, :]),
                       reads=lnres + [rId], writes=[rPTb[pp]])
                    act(lambda e, b=b, g=g, ti=ti, pp=pp: e.activation(
                        arD[:, g * T + ti * 128:g * T + (ti + 1) * 128],
                        pT[pp][:, 0:128], AF.Identity,
                        bias=mcol(b, 0, g), scale=mcol(b, 1, g)),
                        reads=[rPTb[pp], rMod], writes=[rD_])
            for oc in range(16):
                bsl = oc % 2
                dma(FS[bsl].rearrange("p (g m) -> p g m", m=128),
                    w_attn[:, oc * 128:(oc + 1) * 128]
                    .rearrange("(g p) m -> p g m", p=128), writes=[rFS[bsl]])
                if oc >= 8:
                    vec(lambda e, b2=bsl: e.tensor_scalar(
                        wcast[:, b2 * 1024:(b2 + 1) * 1024], FS[b2], scal[:, 0:1],
                        None, ALU.mult),
                        reads=[rFS[bsl], rSC[0]], writes=[rWCh[bsl]])
                else:
                    vec(lambda e, b2=bsl: e.tensor_copy(
                        wcast[:, b2 * 1024:(b2 + 1) * 1024], FS[b2]),
                        reads=[rFS[bsl]], writes=[rWCh[bsl]])
                for st in range(NST):
                    sp2 = st % 2
                    for g in range(NCH):
                        pe(lambda e, b2=bsl, g=g, st=st, sp2=sp2: e.matmul(
                            pS[sp2][:, :],
                            wcast[:, b2 * 1024 + g * 128:b2 * 1024 + (g + 1) * 128],
                            arD[:, g * T + st * 512:g * T + (st + 1) * 512],
                            start=(g == 0), stop=(g == NCH - 1)),
                           reads=[rWCh[bsl], rD_],
                           writes=([rPSb[sp2]] if g in (0, NCH - 1) else []))
                    dbuf, dres = (arA, rA_) if oc < 8 else (arB, rB_)
                    og = oc % 8
                    act(lambda e, d=dbuf, og=og, st=st, sp2=sp2, oc=oc: e.activation(
                        d[:, og * T + st * 512:og * T + (st + 1) * 512],
                        pS[sp2][:, :], AF.Identity, bias=battn[:, oc:oc + 1], scale=1.0),
                        reads=[rPSb[sp2], rBat], writes=[dres])
            for vh in range(2):
                for g in range(NCH):
                    bsl = g % 2
                    dma(FS[bsl][:, 0:512], w_attn[g * 128:(g + 1) * 128,
                                                  2048 + vh * 512:2560 + vh * 512],
                        writes=[rFS[bsl]])
                    vec(lambda e, b2=bsl, g=g: e.tensor_copy(
                        wpj[:, g * 512:(g + 1) * 512], FS[b2][:, 0:512]),
                        reads=[rFS[bsl]], writes=[rWPJ])
                for tb in range(NTIL):
                    sp2 = tb % 2
                    for g in range(NCH):
                        pe(lambda e, g=g, tb=tb, sp2=sp2: e.matmul(
                            pS[sp2][:, :], arD[:, g * T + tb * 128:g * T + (tb + 1) * 128],
                            wpj[:, g * 512:(g + 1) * 512], start=(g == 0), stop=False),
                           reads=[rWPJ, rD_], writes=([rPSb[sp2]] if g == 0 else []))
                    pe(lambda e, vh=vh, sp2=sp2: e.matmul(
                        pS[sp2][:, :], padc[:, 0:128],
                        bvrow[0:128, vh * 512:(vh + 1) * 512], start=False, stop=True),
                       reads=[rPadc, rBv], writes=[rPSb[sp2]])
                    act(lambda e, tb=tb, vh=vh, sp2=sp2: e.copy(
                        arC[:, tb * 1024 + vh * 512:tb * 1024 + (vh + 1) * 512],
                        pS[sp2][:, :]), reads=[rPSb[sp2]], writes=[rC_])

        def attention(b):
            for g in range(NCH):
                bsl = g % 2
                dma(FS[bsl], w_proj[g * 128:(g + 1) * 128, :], writes=[rFS[bsl]])
                vec(lambda e, b2=bsl, g=g: e.tensor_copy(
                    wpj[:, g * 1024:(g + 1) * 1024], FS[b2]),
                    reads=[rFS[bsl]], writes=[rWPJ])

            pending_epi = [None]
            for h in range(H):
                hb = (h % 2) * 64
                hc = h // 2
                par = h % 2
                if par == 0:
                    act(lambda e, hc=hc: e.activation(
                        sqb[0:128, 0:2048], arB[0:128, hc * T:(hc + 1) * T],
                        AF.Square), reads=[rB_], writes=[rSQ])
                    for qc in range(4):
                        pe(lambda e, qc=qc: e.matmul(
                            pR[0:1, 0:512], padc[:, 256:257],
                            sqb[0:128, qc * 512:(qc + 1) * 512], start=True, stop=True,
                            tile_position=(0, 0), skip_group_check=True),
                           reads=[rPadc, rSQ], writes=[rPR])
                        pe(lambda e, qc=qc: e.matmul(
                            pR[64:65, 0:512], padc[:, 257:258],
                            sqb[0:128, qc * 512:(qc + 1) * 512], start=True, stop=True,
                            tile_position=(0, 64), skip_group_check=True),
                           reads=[rPadc, rSQ], writes=[rPR])
                        vec(lambda e, qc=qc: e.reduce_max(
                            srow[0:1, 4 + qc:5 + qc], pR[0:1, 0:512], AX.X),
                            reads=[rPR], writes=[rSR], narrow=True)
                        vec(lambda e, qc=qc: e.reduce_max(
                            srow[64:65, 4 + qc:5 + qc], pR[64:65, 0:512], AX.X),
                            reads=[rPR], writes=[rSR], narrow=True)
                    vec(lambda e: e.tensor_reduce(srow[0:1, 3:4], srow[0:1, 4:8], AX.X,
                                                  ALU.max),
                        reads=[rSR], writes=[rSR], narrow=True)
                    vec(lambda e: e.tensor_reduce(srow[64:65, 3:4], srow[64:65, 4:8],
                                                  AX.X, ALU.max),
                        reads=[rSR], writes=[rSR], narrow=True)
                    act(lambda e: e.activation(srow[0:1, 3:4], srow[0:1, 3:4], AF.Sqrt),
                        reads=[rSR], writes=[rSR], narrow=True)
                    act(lambda e: e.activation(srow[64:65, 3:4], srow[64:65, 3:4],
                                               AF.Sqrt),
                        reads=[rSR], writes=[rSR], narrow=True)
                    vec(lambda e: e.tensor_scalar(srow[0:1, 3:4], srow[0:1, 3:4], -1.0,
                                                  None, ALU.mult),
                        reads=[rSR], writes=[rSR], narrow=True)
                    vec(lambda e: e.tensor_scalar(srow[64:65, 3:4], srow[64:65, 3:4],
                                                  -1.0, None, ALU.mult),
                        reads=[rSR], writes=[rSR], narrow=True)
                    act(lambda e, hc=hc: e.activation(
                        sqb[0:128, 0:2048], arA[0:128, hc * T:(hc + 1) * T],
                        AF.Square), reads=[rA_], writes=[rSQ])

                for qc in range(4):
                    pe(lambda e, par=par, hb=hb, qc=qc: e.matmul(
                        pR[hb:hb + 1, 0:512], padc[:, 256 + par:257 + par],
                        sqb[0:128, qc * 512:(qc + 1) * 512], start=True, stop=True,
                        tile_position=(0, hb), skip_group_check=True),
                       reads=[rPadc, rSQ], writes=[rPR])
                    act(lambda e, hb=hb: e.activation(
                        fscr[hb:hb + 1, 5120:5632], pR[hb:hb + 1, 0:512], AF.Sqrt),
                        reads=[rPR], writes=[rFS[5]], narrow=True)
                    dma(uzpad[hb:hb + 1, qc * 512:(qc + 1) * 512],
                        c_alF[h:h + 1, qc * 512:(qc + 1) * 512],
                        writes=[rUZ[par][qc]])
                    vec(lambda e, par=par, hb=hb, qc=qc: e.scalar_tensor_tensor(
                        uzpad[hb:hb + 1, qc * 512:(qc + 1) * 512],
                        fscr[hb:hb + 1, 5120:5632], srow[hb:hb + 1, 3:4],
                        uzpad[hb:hb + 1, qc * 512:(qc + 1) * 512],
                        ALU.mult, ALU.subtract),
                        reads=[rFS[5], rSR, rUZ[par][qc]], writes=[rUZ[par][qc]],
                        narrow=True)

                yrow = hb
                sumrow = 64 - hb

                def emit_fill(q2, par=par, hb=hb, hc=hc):
                    qo2 = par * 1024 + (q2 % 2) * 512
                    vec(lambda e, q2=q2, qo2=qo2: e.tensor_copy(
                        qpad[hb:hb + 64, qo2:qo2 + 512],
                        arA[hb:hb + 64, hc * T + q2 * 512:hc * T + (q2 + 1) * 512]),
                        reads=[rA_], writes=[rQP[par * 2 + q2 % 2]])

                emit_fill(0)
                for qs in range(NST):
                    nk = qs * 4 + 4
                    yb = (h * NST + qs) % 2
                    qoff = par * 1024 + (qs % 2) * 512
                    qres = rQP[par * 2 + qs % 2]

                    def pslot(pp):
                        if pp < 4:
                            return spst[:, pp * 512:(pp + 1) * 512], rPT[pp]
                        return wcast[:, 0:512], rPT[4]

                    def emit_s(j, qs=qs, qoff=qoff, qres=qres):
                        co = max(0, j * 128 - qs * 512)
                        N = 512 - co
                        sp2 = j % 5
                        pe(lambda e, hc=hc, j=j, co=co, N=N, sp2=sp2, qoff=qoff:
                           e.matmul(pS[sp2][:, co:co + N],
                                    arB[0:128,
                                        hc * T + j * 128:hc * T + (j + 1) * 128],
                                    qpad[0:128, qoff + co:qoff + co + N],
                                    start=True, stop=False),
                           reads=[qres, rB_], writes=[rPSb[sp2]])
                        pe(lambda e, par=par, j=j, qs=qs, co=co, N=N, sp2=sp2:
                           e.matmul(pS[sp2][:, co:co + N],
                                    padc[:, par * 128:par * 128 + 128],
                                    uzpad[0:128, qs * 512 + co:qs * 512 + co + N],
                                    start=False, stop=True),
                           reads=[rPadc, rUZ[par][qs]], writes=[rPSb[sp2]])
                        if j * 128 == qs * 512 + co:
                            vec(lambda e, co=co, sp2=sp2: e.tensor_tensor(
                                pS[sp2][:, co:co + 128], pS[sp2][:, co:co + 128],
                                trimask[:, :], ALU.min),
                                reads=[rTri], writes=[rPSb[sp2]])
                        pb, pr = pslot(j % 5)
                        act(lambda e, h=h, j=j, co=co, N=N, sp2=sp2, pb=pb:
                            e.activation(
                                pb[:, 0:N], pS[sp2][:, co:co + N],
                                AF.Exp, bias=alK[:, h * 16 + j:h * 16 + j + 1]),
                            reads=[rPSb[sp2], rALK], writes=[pr])

                    def emit_pv(j, qs=qs, yb=yb, nk=nk):
                        co = max(0, j * 128 - qs * 512)
                        N = 512 - co
                        pb, pr = pslot(j % 5)
                        pe(lambda e, h=h, j=j, co=co, N=N, yb=yb, yrow=yrow, pb=pb,
                           nk=nk: e.matmul(
                               pY[yb][yrow:yrow + 64, co:co + N],
                               arC[:, j * 1024 + h * 64:j * 1024 + (h + 1) * 64],
                               pb[:, 0:N],
                               start=(j == 0), stop=(j == nk - 1),
                               tile_position=(0, yrow), skip_group_check=True),
                           reads=[rC_, pr], writes=[rPYb[yb]])
                        pe(lambda e, j=j, co=co, N=N, yb=yb, sumrow=sumrow, pb=pb,
                           nk=nk: e.matmul(
                               pY[yb][sumrow:sumrow + 1, co:co + N],
                               onescol[:, 1:2], pb[:, 0:N],
                               start=(j == 0), stop=(j == nk - 1),
                               tile_position=(0, sumrow), skip_group_check=True),
                           reads=[rOnes, pr], writes=[rPYb[yb]])

                    DEPTH = int(os.environ.get("KERNEL_DEPTH", "4"))
                    for j0 in range(min(DEPTH, nk)):
                        emit_s(j0)
                    if qs + 1 < NST:
                        emit_fill(qs + 1)
                    for j in range(nk):
                        if j + DEPTH < nk:
                            emit_s(j + DEPTH)
                        emit_pv(j)
                        if j == 1 and pending_epi[0] is not None:
                            pending_epi[0]()
                            pending_epi[0] = None
                    if pending_epi[0] is not None:
                        pending_epi[0]()
                        pending_epi[0] = None

                    def epilogue(yb=yb, sumrow=sumrow, yrow=yrow, hc=hc, qs=qs):
                        vec(lambda e, yb=yb, sumrow=sumrow: e.reciprocal(
                            rrow[sumrow:sumrow + 1, :],
                            pY[yb][sumrow:sumrow + 1, :]),
                            reads=[rPYb[yb]], writes=[rRR], narrow=True)
                        pe(lambda e, sumrow=sumrow, yrow=yrow: e.matmul(
                            pR[yrow:yrow + 64, :],
                            padc[:, (sumrow * 2):(sumrow * 2) + 64],
                            rrow[0:128, :], start=True, stop=True,
                            tile_position=(0, yrow), skip_group_check=True),
                           reads=[rPadc, rRR], writes=[rPR])
                        act(lambda e, yrow=yrow: e.copy(
                            fscr2[yrow:yrow + 64, 512:1024], pR[yrow:yrow + 64, :]),
                            reads=[rPR], writes=[rF2])
                        vec(lambda e, hc=hc, qs=qs, yb=yb, yrow=yrow: e.tensor_tensor(
                            arD[yrow:yrow + 64, hc * T + qs * 512:hc * T + (qs + 1) * 512],
                            pY[yb][yrow:yrow + 64, :], fscr2[yrow:yrow + 64, 512:1024],
                            ALU.mult), reads=[rPYb[yb], rF2], writes=[rD_])

                    if os.environ.get("KERNEL_NO_DEFER"):
                        epilogue()
                    else:
                        pending_epi[0] = epilogue

            if pending_epi[0] is not None:
                pending_epi[0]()
                pending_epi[0] = None

            for qs in range(NST):
                for oc in range(NCH):
                    sp2 = oc % 2
                    for g in range(NCH):
                        pe(lambda e, g=g, oc=oc, qs=qs, sp2=sp2: e.matmul(
                            pS[sp2][:, :],
                            wpj[:, g * 1024 + oc * 128:g * 1024 + (oc + 1) * 128],
                            arD[:, g * T + qs * 512:g * T + (qs + 1) * 512],
                            start=(g == 0), stop=(g == NCH - 1)),
                           reads=[rWPJ, rD_],
                           writes=([rPSb[sp2]] if g in (0, NCH - 1) else []))
                    tgt, trs = (wcast, [rWCh[0], rWCh[1]]) if oc < 4 else (spst, [rSP])
                    to = (oc % 4) * 512
                    act(lambda e, b=b, oc=oc, tgt=tgt, to=to, sp2=sp2: e.activation(
                        tgt[:, to:to + 512], pS[sp2][:, :], AF.Identity,
                        scale=mcol(b, 2, oc),
                        bias=gabv[:, b * 8 + oc:b * 8 + oc + 1]),
                        reads=[rPSb[sp2], rMod, rGab], writes=trs)
                for tb in range(4):
                    ti = qs * 4 + tb
                    xs = 2 + tb % 2
                    par = tb % 2
                    acc, racc = (FS[4], rFS[4]) if par == 0 else (FS[1], rFS[1])
                    tmp, rtmp = (FS[5], rFS[5]) if par == 0 else (FS[0], rFS[0])
                    lnb, lnres = LN_SETS[par]["ln"], LN_SETS[par]["lnr"]
                    h2b, h2res = LN_SETS[par]["sq"], LN_SETS[par]["sqr"]
                    mc = 46 + par * 10
                    dma(FS[xs], x_in[b, ti * 128:(ti + 1) * 128, :], writes=[rFS[xs]])
                    for oc in range(NCH):
                        tgt, trs = (wcast, [rWCh[0], rWCh[1]]) if oc < 4 else (spst, [rSP])
                        to = (oc % 4) * 512
                        pp = oc % 2
                        pe(lambda e, tgt=tgt, to=to, tb=tb, pp=pp: e.transpose(
                            pT[pp][:, 0:128], tgt[:, to + tb * 128:to + (tb + 1) * 128],
                            identb[:, :]),
                           reads=trs + [rId], writes=[rPTb[pp]])
                        vec(lambda e, oc=oc, xs=xs, pp=pp, acc=acc: e.tensor_tensor(
                            acc[:, oc * 128:(oc + 1) * 128], pT[pp][:, 0:128],
                            FS[xs][:, oc * 128:(oc + 1) * 128], ALU.add),
                            reads=[rPTb[pp], rFS[xs]], writes=[racc])
                    dma(x1sp[b, ti * 128:(ti + 1) * 128, :], acc, q="pool",
                        reads=[racc], writes=[rX1sp[b]])
                    layernorm(racc, acc, tmp, rtmp, par)
                    for g in range(NCH):
                        pp = g % 2
                        pe(lambda e, g=g, pp=pp, lnb=lnb: e.transpose(
                            pT[pp][:, 0:128], lnb[:, g * 128:(g + 1) * 128],
                            identb[:, :]),
                           reads=lnres + [rId], writes=[rPTb[pp]])
                        act(lambda e, b=b, g=g, pp=pp, h2b=h2b: e.activation(
                            h2b[:, g * 128:(g + 1) * 128], pT[pp][:, 0:128],
                            AF.Identity,
                            bias=mcol(b, 3, g), scale=mcol(b, 4, g)),
                            reads=[rPTb[pp], rMod], writes=h2res)
                    dma(h2sp[b, :, :, ti * 128:(ti + 1) * 128]
                        .rearrange("g p t -> p g t"),
                        h2b.rearrange("p (g t) -> p g t", t=128),
                        q="pool", reads=h2res, writes=[rH2sp[b]])
                    vec(lambda e, h2b=h2b, mc=mc: e.tensor_reduce(
                        scal[:, mc:mc + 1], h2b, AX.X, ALU.max,
                        apply_absolute_value=True),
                        reads=h2res, writes=[rSC[mc]], narrow=True)
                    vec(lambda e, b=b, mc=mc: e.tensor_tensor(
                        mxacc[:, b:b + 1], mxacc[:, b:b + 1], scal[:, mc:mc + 1],
                        ALU.max),
                        reads=[rSC[mc]], writes=[rMX[b]], narrow=True)

        def scale_pair(mxres, mx_ap, colm, coli):
            pe(lambda e: e.transpose(pR[0:1, 0:128], mx_ap, identf[:, :]),
               reads=[mxres, rId], writes=[rPR], narrow=True)
            vec(lambda e: e.reduce_max(srow[0:1, 8:9], pR[0:1, 0:128], AX.X),
                reads=[rPR], writes=[rSR], narrow=True)
            vec(lambda e: e.tensor_scalar(srow[0:1, 9:10], srow[0:1, 8:9],
                                          1.0 / 32767.0, 1e-12, ALU.mult, ALU.max),
                reads=[rSR], writes=[rSR], narrow=True)
            vec(lambda e: e.reciprocal(srow[0:1, 10:11], srow[0:1, 9:10]),
                reads=[rSR], writes=[rSR], narrow=True)
            bcast(srow[0:1, 9:10], colm, [rSR])
            bcast(srow[0:1, 10:11], coli, [rSR])

        def dqsl(w, g, oc):
            off = w * (NCH * FF) + g * FF + oc * 128
            if off < 16384:
                return arA[:, off:off + 128]
            if off < 32768:
                return arB[:, off - 16384:off - 16384 + 128]
            return arC[:, off - 32768:off - 32768 + 128]

        def lpbq_upgate(raw, w, oc):
            bsl = oc % 2
            dma(FS[bsl], raw[oc * 128:(oc + 1) * 128, :], writes=[rFS[bsl]])
            vec(lambda e, b2=bsl: e.tensor_reduce(
                scal[:, 32:64].rearrange("p (n w) -> p n w", w=1),
                FS[b2].rearrange("p (n w) -> p n w", w=32), AX.X, ALU.max,
                apply_absolute_value=True),
                reads=[rFS[bsl]], writes=[rSC[32]], narrow=True)
            vec(lambda e: e.tensor_scalar(scal[:, 32:64], scal[:, 32:64], 1.0 / 7.0,
                                          1e-12, ALU.mult, ALU.max),
                reads=[rSC[32]], writes=[rSC[32]], narrow=True)
            vec(lambda e: e.reciprocal(fscr2[:, 0:32], scal[:, 32:64]),
                reads=[rSC[32]], writes=[rF2], narrow=True)
            vec(lambda e, b2=bsl: e.tensor_tensor(
                iscr[:, 0:1024].rearrange("p (n w) -> p n w", w=32),
                FS[b2].rearrange("p (n w) -> p n w", w=32),
                fscr2[:, 0:32].unsqueeze(2).broadcast_to([128, 32, 32]), ALU.mult),
                reads=[rFS[bsl], rF2], writes=[rIS])
            vec(lambda e: e.tensor_tensor(
                sqb[:, 0:1024].rearrange("p (n w) -> p n w", w=32),
                iscr[:, 0:1024].rearrange("p (n w) -> p n w", w=32),
                scal[:, 32:64].unsqueeze(2).broadcast_to([128, 32, 32]), ALU.mult),
                reads=[rIS, rSC[32]], writes=[rSQ])
            for g in range(NCH):
                pp = g % 2
                pe(lambda e, g=g, pp=pp: e.transpose(
                    pT[pp][:, 0:128], sqb[:, g * 128:(g + 1) * 128], identb[:, :]),
                   reads=[rSQ, rId], writes=[rPTb[pp]])
                act(lambda e, w=w, g=g, oc=oc, pp=pp: e.copy(
                    dqsl(w, g, oc), pT[pp][:, 0:128]),
                    reads=[rPTb[pp]], writes=[rDQ])

        def ffn_A_load(b, st):
            xoff = (st % 2) * 4096
            rX = rXp[st % 2]
            dma(arD[:, xoff:xoff + 4096].rearrange("p (g t) -> p g t", t=512),
                h2sp[b, :, :, st * 512:(st + 1) * 512].rearrange("g p t -> p g t"),
                reads=[rH2sp[b]], writes=[rX])
            for half in range(4):
                ho = half * 1024
                vec(lambda e, xoff=xoff, ho=ho, rX=rX: e.tensor_scalar(
                    iscr[:, :], arD[:, xoff + ho:xoff + ho + 1024],
                    scal[:, 17:18], None, ALU.mult),
                    reads=[rX, rSC[17]], writes=[rIS])
                vec(lambda e, xoff=xoff, ho=ho, rX=rX: e.tensor_scalar(
                    arD[:, xoff + ho:xoff + ho + 1024], iscr[:, :],
                    scal[:, 16:17], None, ALU.mult),
                    reads=[rIS, rSC[16]], writes=[rX])

        def ffn_A_mm(b, w, st):
            xoff = (st % 2) * 4096
            rX = rXp[st % 2]
            dsp, rdsp, mcol = ((upsp, rUpsp, 2), (gasp, rGasp, 4))[w]
            for oc in range(FCH):
                sp2 = oc % 2
                for g in range(NCH):
                    pe(lambda e, w=w, g=g, oc=oc, xoff=xoff, sp2=sp2: e.matmul(
                        pS[sp2][:, :], dqsl(w, g, oc),
                        arD[:, xoff + g * 512:xoff + (g + 1) * 512],
                        start=(g == 0), stop=(g == NCH - 1)),
                       reads=[rDQ, rX],
                       writes=([rPSb[sp2]] if g in (0, NCH - 1) else []))
                sl = oc % 8
                gr = (oc // 4) % 2
                act(lambda e, sl=sl, sp2=sp2: e.copy(
                    arD[:, 8192 + sl * 512:8704 + sl * 512], pS[sp2][:, :]),
                    reads=[rPSb[sp2]], writes=[rASt[gr]])
                if sl % 4 == 3:
                    go = 8192 + (sl - 3) * 512
                    dma(dsp[b, oc - 3:oc + 1, :, st * 512:(st + 1) * 512]
                        .rearrange("o p t -> p o t"),
                        arD[:, go:go + 2048].rearrange("p (o t) -> p o t", t=512),
                        q="pool", reads=[rASt[gr]], writes=[rdsp[b]])
                    vec(lambda e, go=go: e.tensor_reduce(
                        scal[:, 46:47], arD[:, go:go + 2048], AX.X, ALU.max,
                        apply_absolute_value=True),
                        reads=[rASt[gr]], writes=[rSC[46]], narrow=True)
                    vec(lambda e, b=b, mcol=mcol: e.tensor_tensor(
                        mxacc[:, mcol + b:mcol + b + 1],
                        mxacc[:, mcol + b:mcol + b + 1],
                        scal[:, 46:47], ALU.max),
                        reads=[rSC[46]], writes=[rMX[mcol + b]], narrow=True)

        uzI = uzpad[:, :].bitcast(I32)
        spI = spst[:, :].bitcast(I32)

        def ibuf(h):
            q = h % 4
            src = uzI if q < 2 else spI
            return src[:, (q % 2) * 512:(q % 2) * 512 + 512]
        f2bf = fscr2[:, :].bitcast(BF16)[:, 1024:2048]
        f2bf2 = fscr2[:, :].bitcast(BF16)[:, 0:1024]

        def spB(b):
            scale_pair(rMX[2 + b], mxacc[:, 2 + b:3 + b], 18, 19)
            scale_pair(rMX[4 + b], mxacc[:, 4 + b:5 + b], 20, 21)

        def ffn_B(b, ocs=None):
            ocs = list(range(FCH)) if ocs is None else list(ocs)
            segs = [(oc, sg) for oc in ocs for sg in range(4)]
            n = len(segs)

            def b_load(oc, hf):
                dma(wcast[:, hf * 1024:(hf + 1) * 1024],
                    gasp[b, oc, :, hf * 1024:(hf + 1) * 1024],
                    reads=[rGasp[b]], writes=[rWCh[hf]])

            def b_int(g):
                oc, sg = segs[g]
                so = sg * 512
                ui, ru = ibuf(g), rBIh[g % 4]
                ex = [rSP] if g % 4 >= 2 else []
                vec(lambda e, so=so, ui=ui: e.tensor_scalar(
                    ui, wcast[:, so:so + 512], scal[:, 21:22], None,
                    ALU.mult), reads=[rWCh[sg // 2], rSC[21]] + ex, writes=[ru])

            def b_act(g):
                oc, sg = segs[g]
                so = sg * 512
                hf = sg // 2
                ui, ru = ibuf(g), rBIh[g % 4]
                act(lambda e, so=so, ui=ui: e.activation(
                    fscr[:, 4096 + so:4608 + so], ui, AF.Sigmoid,
                    scale=scal[:, 20:21]),
                    reads=[ru, rSC[20]], writes=[rFS[4 + hf]])
                act(lambda e, so=so, ui=ui: e.activation(
                    sqb[:, so:so + 512], ui, AF.Identity,
                    scale=scal[:, 20:21]),
                    reads=[ru, rSC[20]], writes=[rSQ])

            def b_fq(g):
                oc, sg = segs[g]
                so = sg * 512
                hf = sg // 2
                ui, ru = ibuf(g), rBIh[g % 4]
                vec(lambda e, so=so, ui=ui: e.tensor_scalar(
                    ui, fscr[:, 4096 + so:4608 + so], 65536.0, None,
                    ALU.mult), reads=[rFS[4 + hf]], writes=[ru])
                vec(lambda e, so=so, ui=ui: e.tensor_scalar(
                    qpad[:, so:so + 512], ui, 65535.0,
                    1.0 / 65536.0, ALU.min, ALU.mult),
                    reads=[ru], writes=[rBQ])

            def b_gs(oc, hf):
                ho = hf * 1024
                fb = (f2bf2, f2bf)[hf]
                vec(lambda e, ho=ho, fb=fb: e.tensor_tensor(
                    fb[:, :], sqb[:, ho:ho + 1024], qpad[:, ho:ho + 1024],
                    ALU.mult), reads=[rSQ, rBQ], writes=[rF2h[hf]])
                dma(gssp[b, oc, :, ho:ho + 1024], fb[:, :], q="pool",
                    reads=[rF2h[hf]], writes=[rGssp[b]])
                vec(lambda e, b=b, fb=fb: e.tensor_reduce(
                    mxacc[:, 10 + b:11 + b], fb[:, :], AX.X,
                    ALU.max, apply_absolute_value=True),
                    reads=[rF2h[hf]], writes=[rMX[10 + b]], narrow=True)
                vec(lambda e, b=b: e.tensor_tensor(
                    mxacc[:, 6 + b:7 + b], mxacc[:, 6 + b:7 + b],
                    mxacc[:, 10 + b:11 + b],
                    ALU.max), reads=[rMX[10 + b]], writes=[rMX[6 + b]], narrow=True)

            b_load(segs[0][0], 0)
            b_load(segs[0][0], 1)
            for g in range(n + 2):
                if g < n:
                    b_int(g)
                    oc, sg = segs[g]
                    if sg == 1 and g + 3 < n:
                        b_load(segs[g + 3][0], 0)
                    if sg == 3 and g + 1 < n:
                        b_load(segs[g + 1][0], 1)
                if g >= 1 and g - 1 < n:
                    b_act(g - 1)
                if g >= 2 and g - 2 < n:
                    b_fq(g - 2)
                    oc2, sg2 = segs[g - 2]
                    if sg2 == 1:
                        b_gs(oc2, 0)
                    elif sg2 == 3:
                        b_gs(oc2, 1)

        def spC(b):
            scale_pair(rMX[6 + b], mxacc[:, 6 + b:7 + b], 22, 23)

        fsbf = fscr[:, :].bitcast(BF16)[:, 8192:12288]
        rCG = [R(), R()]
        ALL_RES.extend(rCG)

        def ffn_C(b, ocs=None):
            ocs = list(range(FCH)) if ocs is None else list(ocs)

            def c_gload(i):
                dma(fsbf[:, (i % 2) * 2048:(i % 2) * 2048 + 2048],
                    gssp[b, ocs[i], :, :],
                    reads=[rGssp[b]], writes=[rCG[i % 2], rFS[4], rFS[5]])

            c_gload(0)
            for i, oc in enumerate(ocs):
                gb = fsbf[:, (i % 2) * 2048:(i % 2) * 2048 + 2048]
                rg = rCG[i % 2]
                for hf in range(2):
                    fb = (f2bf2, f2bf)[hf]
                    dma(fb[:, :], upsp[b, oc, :, hf * 1024:(hf + 1) * 1024],
                        reads=[rUpsp[b]], writes=[rF2h[hf], rF2])
                if i + 1 < len(ocs):
                    c_gload(i + 1)

                def cbuf(sg):
                    q = sg % 3
                    if q < 2:
                        return uzI[:, q * 512:q * 512 + 512], rBIh[q]
                    return ibufC[:, :], rBIh[2]

                def c_int1(sg):
                    so = sg * 512
                    ui, ru = cbuf(sg)
                    vec(lambda e, so=so, ui=ui, gb=gb: e.tensor_scalar(
                        ui, gb[:, so:so + 512], scal[:, 23:24], None,
                        ALU.mult), reads=[rg, rSC[23]],
                        writes=[ru])

                def c_act1(sg):
                    so = sg * 512
                    ui, ru = cbuf(sg)
                    vec(lambda e, so=so, ui=ui: e.tensor_scalar(
                        spst[:, so:so + 512], ui, scal[:, 22:23], None,
                        ALU.mult),
                        reads=[ru, rSC[22]], writes=[rSP])

                for sg in range(4):
                    c_int1(sg)
                    if sg >= 2:
                        c_act1(sg - 2)
                c_act1(2)
                c_act1(3)

                def c_int2(sg):
                    so = sg * 512
                    fb = (f2bf2, f2bf)[sg // 2]
                    ui, ru = cbuf(sg)
                    vec(lambda e, so=so, ui=ui, fb=fb, sg=sg: e.tensor_scalar(
                        ui, fb[:, (sg % 2) * 512:(sg % 2) * 512 + 512],
                        scal[:, 19:20], None,
                        ALU.mult), reads=[rF2h[sg // 2], rSC[19]],
                        writes=[ru])

                def c_act2(sg):
                    so = sg * 512
                    ui, ru = cbuf(sg)
                    vec(lambda e, so=so, ui=ui: e.tensor_scalar(
                        qpad[:, so:so + 512], ui, scal[:, 18:19], None,
                        ALU.mult),
                        reads=[ru, rSC[18]], writes=[rBQ])

                for sg in range(4):
                    c_int2(sg)
                    if sg >= 2:
                        c_act2(sg - 2)
                c_act2(2)
                c_act2(3)
                vec(lambda e: e.tensor_tensor(spst[:, :], spst[:, :], qpad[:, :],
                                              ALU.mult),
                    reads=[rBQ], writes=[rSP])
                dma(osp[b, oc, :, :], spst[:, :], q="pool",
                    reads=[rSP], writes=[rOsp[b]])
                vec(lambda e, b=b: e.tensor_reduce(
                    mxacc[:, 12 + b:13 + b], spst[:, :], AX.X,
                    ALU.max, apply_absolute_value=True),
                    reads=[rSP], writes=[rMX[12 + b]], narrow=True)
                vec(lambda e, b=b: e.tensor_tensor(
                    mxacc[:, 8 + b:9 + b], mxacc[:, 8 + b:9 + b],
                    mxacc[:, 12 + b:13 + b],
                    ALU.max), reads=[rMX[12 + b]], writes=[rMX[8 + b]], narrow=True)

        def dqdl(g, co):
            off = co * FF + g * 128
            if off < 16384:
                return arA[:, off:off + 128]
            return arB[:, off - 16384:off - 16384 + 128]

        def lpbq_down(cos=None):
            for co in (range(NCH) if cos is None else cos):
                for third in range(3):
                    tsl = third % 2
                    dma(FS[tsl], w_down[co * 128:(co + 1) * 128,
                                        third * 1024:(third + 1) * 1024],
                        writes=[rFS[tsl]])
                    vec(lambda e, t2=tsl: e.tensor_reduce(
                        scal[:, 32:64].rearrange("p (n w) -> p n w", w=1),
                        FS[t2].rearrange("p (n w) -> p n w", w=32), AX.X, ALU.max,
                        apply_absolute_value=True),
                        reads=[rFS[tsl]], writes=[rSC[32]], narrow=True)
                    vec(lambda e: e.tensor_scalar(scal[:, 32:64], scal[:, 32:64],
                                                  1.0 / 7.0, 1e-12, ALU.mult, ALU.max),
                        reads=[rSC[32]], writes=[rSC[32]], narrow=True)
                    vec(lambda e: e.reciprocal(fscr2[:, 0:32], scal[:, 32:64]),
                        reads=[rSC[32], rF2h[0], rF2h[1]], writes=[rF2],
                        narrow=True)
                    vec(lambda e, t2=tsl: e.tensor_tensor(
                        iscr[:, 0:1024].rearrange("p (n w) -> p n w", w=32),
                        FS[t2].rearrange("p (n w) -> p n w", w=32),
                        fscr2[:, 0:32].unsqueeze(2).broadcast_to([128, 32, 32]),
                        ALU.mult), reads=[rFS[tsl], rF2], writes=[rIS])
                    vec(lambda e: e.tensor_tensor(
                        f2bf[:, 0:1024].rearrange("p (n w) -> p n w", w=32),
                        iscr[:, 0:1024].rearrange("p (n w) -> p n w", w=32),
                        scal[:, 32:64].unsqueeze(2).broadcast_to([128, 32, 32]),
                        ALU.mult), reads=[rIS, rSC[32]], writes=[rF2])
                    for g8 in range(NCH):
                        g = third * 8 + g8
                        pp = g8 % 2
                        pe(lambda e, g8=g8, pp=pp: e.transpose(
                            pT[pp][:, 0:128], f2bf[:, g8 * 128:(g8 + 1) * 128],
                            identb[:, :]),
                           reads=[rF2, rId], writes=[rPTb[pp]])
                        act(lambda e, g=g, co=co, pp=pp: e.copy(
                            dqdl(g, co), pT[pp][:, 0:128]),
                            reads=[rPTb[pp]], writes=[rDQ])

        out_evs = []

        def spD(b):
            scale_pair(rMX[8 + b], mxacc[:, 8 + b:9 + b], 24, 25)

        def ffn_D(b, sts=None):
            for st in (range(NST) if sts is None else sts):
                for grp in range(6):
                    go = grp * 2048
                    rX = rXp[grp % 2]
                    dma(arD[:, go:go + 2048].rearrange("p (g t) -> p g t", t=512),
                        osp[b, grp * 4:(grp + 1) * 4, :, st * 512:(st + 1) * 512]
                        .rearrange("g p t -> p g t"),
                        reads=[rOsp[b], rASt[0], rASt[1]], writes=[rX])
                    for hf in range(2):
                        ho = go + hf * 1024
                        vec(lambda e, ho=ho, rX=rX: e.tensor_scalar(
                            iscr[:, :], arD[:, ho:ho + 1024], scal[:, 25:26], None,
                            ALU.mult), reads=[rX, rSC[25]], writes=[rIS])
                        vec(lambda e, ho=ho, rX=rX: e.tensor_scalar(
                            arD[:, ho:ho + 1024], iscr[:, :], scal[:, 24:25], None,
                            ALU.mult), reads=[rIS, rSC[24]], writes=[rX])
                for co in range(NCH):
                    sp2 = co % 2
                    for g in range(FCH):
                        pe(lambda e, g=g, co=co, sp2=sp2: e.matmul(
                            pS[sp2][:, :], dqdl(g, co),
                            arD[:, g * 512:(g + 1) * 512],
                            start=(g == 0), stop=(g == FCH - 1)),
                           reads=[rDQ, rXp[0], rXp[1]],
                           writes=([rPSb[sp2]] if g in (0, FCH - 1) else []))
                    do = 12288 + co * 512
                    act(lambda e, b=b, co=co, do=do, sp2=sp2: e.activation(
                        arD[:, do:do + 512], pS[sp2][:, :], AF.Identity,
                        scale=mcol(b, 5, co), bias=0.0),
                        reads=[rPSb[sp2], rMod], writes=[rDS[co % 2]])
                for tb in range(4):
                    ti = st * 4 + tb
                    xs = 2 + tb % 2
                    dma(FS[xs], x1sp[b, ti * 128:(ti + 1) * 128, :],
                        reads=[rX1sp[b]], writes=[rFS[xs]])
                    for co in range(NCH):
                        do = 12288 + co * 512
                        pp = co % 2
                        pe(lambda e, do=do, tb=tb, pp=pp, co=co: e.transpose(
                            pT[pp][:, 0:128],
                            arD[:, do + tb * 128:do + (tb + 1) * 128],
                            identb[:, :]),
                           reads=[rDS[co % 2], rId], writes=[rPTb[pp]])
                        vec(lambda e, co=co, xs=xs, pp=pp: e.tensor_tensor(
                            FS[0][:, co * 128:(co + 1) * 128], pT[pp][:, 0:128],
                            FS[xs][:, co * 128:(co + 1) * 128], ALU.add),
                            reads=[rPTb[pp], rFS[xs]], writes=[rFS[0]])
                    ev = dma(out[b, ti * 128:(ti + 1) * 128, :], FS[0], q="pool",
                             reads=[rFS[0]])
                    out_evs.append(ev)

        _dbg = os.environ.get("KERNEL_DEBUG_STOP", "")
        for b in range(BPC):
            ln1_qkv(b)
            attention(b)
            s.fence(ALL_RES + rWCh)
            if _dbg == "attn%d" % b:
                break

        if not _dbg:
          for oc in range(FCH):
            lpbq_upgate(w_up, 0, oc)
          for oc in range(FCH):
            lpbq_upgate(w_gate, 1, oc)
          s.fence(ALL_RES + rWCh)

          scale_pair(rMX[0], mxacc[:, 0:1], 16, 17)
          ffn_A_load(0, 0)
          ffn_A_load(0, 1)
          ffn_A_mm(0, 1, 0)
          ffn_A_load(0, 2)
          ffn_A_mm(0, 1, 1)
          ffn_A_load(0, 3)
          ffn_A_mm(0, 1, 2)
          ffn_A_mm(0, 1, 3)
          scale_pair(rMX[4], mxacc[:, 4:5], 20, 21)
          ffn_A_load(0, 0)
          ffn_A_load(0, 1)
          ffn_A_mm(0, 0, 0)
          ffn_B(0, range(0, 6))
          ffn_A_load(0, 2)
          ffn_A_mm(0, 0, 1)
          ffn_B(0, range(6, 12))
          ffn_A_load(0, 3)
          ffn_A_mm(0, 0, 2)
          ffn_B(0, range(12, 18))
          ffn_A_mm(0, 0, 3)
          ffn_B(0, range(18, 24))
          scale_pair(rMX[2], mxacc[:, 2:3], 18, 19)
          scale_pair(rMX[1], mxacc[:, 1:2], 16, 17)
          ffn_A_load(1, 0)
          ffn_A_load(1, 1)
          ffn_A_mm(1, 1, 0)
          spC(0)
          ffn_C(0, range(0, 6))
          ffn_A_load(1, 2)
          ffn_A_mm(1, 1, 1)
          ffn_C(0, range(6, 12))
          ffn_A_load(1, 3)
          ffn_A_mm(1, 1, 2)
          ffn_C(0, range(12, 18))
          ffn_A_mm(1, 1, 3)
          ffn_C(0, range(18, 24))
          scale_pair(rMX[5], mxacc[:, 5:6], 20, 21)
          ffn_A_load(1, 0)
          ffn_A_load(1, 1)
          ffn_A_mm(1, 0, 0)
          ffn_B(1, range(0, 6))
          ffn_A_load(1, 2)
          ffn_A_mm(1, 0, 1)
          ffn_B(1, range(6, 12))
          ffn_A_load(1, 3)
          ffn_A_mm(1, 0, 2)
          ffn_B(1, range(12, 18))
          ffn_A_mm(1, 0, 3)
          ffn_B(1, range(18, 24))
          scale_pair(rMX[3], mxacc[:, 3:4], 18, 19)
          lpbq_down()
          spD(0)
          ffn_D(0, [0])
          spC(1)
          ffn_C(1, range(0, 6))
          ffn_D(0, [1])
          ffn_C(1, range(6, 12))
          ffn_D(0, [2])
          ffn_C(1, range(12, 18))
          ffn_D(0, [3])
          ffn_C(1, range(18, 24))
          spD(1)
          ffn_D(1)

        final_deps = out_evs[-1:]
        s.step("sync", lambda e: e.nop(), extra=[s.last["pool"], s.last["vec"],
                                                 s.last["act"], s.last["pe"]] + final_deps)

        plans = s.finalize()
        sem_handles = {}
        for i, sid in enumerate(s.sem_ids):
            sem_handles[sid] = ctx.enter_context(nc.semaphore("c%d" % i))

        def runner(eng_name):
            def go(e):
                for waits, f, ev in plans[eng_name]:
                    for sem_eng, val in waits:
                        e.wait_ge(sem_handles[sem_eng], val)
                    ins = f(e)
                    if ev.sem is not None:
                        ins.then_inc(sem_handles[ev.sem], ev.amt)
            return go

        with nc.Block() as blk:
            blk.sync(runner("sync"))
            blk.vector(runner("vec"))
            blk.scalar(runner("act"))
            blk.tensor(runner("pe"))
            blk.gpsimd(runner("pool"))
    return nc


_prog_cache = [None]


def kernel(**inputs):
    x = np.ascontiguousarray(np.asarray(inputs["L0_attn_x"], np.float32))
    cv = np.ascontiguousarray(np.asarray(inputs["L0_c"], np.float32))
    full = {
        "w_attn": np.ascontiguousarray(np.asarray(inputs["L0_attn_W_attn"], np.float32)),
        "b_attn": np.ascontiguousarray(np.asarray(inputs["L0_attn_b_attn"], np.float32)),
        "w_proj": np.ascontiguousarray(np.asarray(inputs["L0_attn_W_proj"], np.float32)),
        "b_proj": np.ascontiguousarray(np.asarray(inputs["L0_attn_b_proj"], np.float32)),
        "temp": np.ascontiguousarray(np.asarray(inputs["L0_attn_temperature"], np.float32)),
        "w_gate": np.ascontiguousarray(np.asarray(inputs["L0_ffn_w_gate"], np.float32)),
        "w_up": np.ascontiguousarray(np.asarray(inputs["L0_ffn_w_up"], np.float32)),
        "w_down": np.ascontiguousarray(np.asarray(inputs["L0_ffn_w_down"], np.float32)),
        "wmod": np.ascontiguousarray(np.asarray(inputs["L0_wmod"], np.float32)),
    }
    if _prog_cache[0] is None:
        _prog_cache[0] = build_program()
    nc = _prog_cache[0]
    in_maps = []
    for core in range(8):
        m = dict(full)
        m["x"] = np.ascontiguousarray(x[core * BPC:(core + 1) * BPC])
        m["cvec"] = np.ascontiguousarray(cv[core * BPC:(core + 1) * BPC])
        in_maps.append(m)
    res = run_bass_kernel_spmd(nc, in_maps, core_ids=list(range(8)))
    return np.concatenate([np.asarray(r["out"], np.float32) for r in res.results],
                          axis=0)


# version 22
# speedup vs baseline: 1.1958x; 1.0072x over previous
import contextlib

import numpy as np
import ml_dtypes

import concourse.bass as bass
import concourse.mybir as mybir
from concourse.bass_utils import run_bass_kernel_spmd

F32 = mybir.dt.float32
BF16 = mybir.dt.bfloat16
I32 = mybir.dt.int32
AF = mybir.ActivationFunctionType
ALU = mybir.AluOpType
AX = mybir.AxisListType

BPC, T, C, H, DH, FF = 2, 2048, 1024, 16, 64, 3072
NCH, FCH, NTIL, NST = 8, 24, 16, 4

ENGINES = ("sync", "vec", "act", "pe", "pool")


class Ev:
    __slots__ = ("eng", "narrow", "amt", "sem", "val", "key")

    def __init__(self, eng, narrow, amt, key=None):
        self.eng, self.narrow, self.amt = eng, narrow, amt
        self.sem = None
        self.val = None
        self.key = key


class Res:
    __slots__ = ("wev", "revs")

    def __init__(self):
        self.wev = None
        self.revs = []


class Sched:
    def __init__(self):
        self.per_eng = {e: [] for e in ENGINES}
        self.last = {e: None for e in ENGINES}
        self.last_comp = {e: None for e in ENGINES}
        self.last_key = {}

    def step(self, eng, f, reads=(), writes=(), extra=(), narrow=False, dma=False,
             key=None):
        deps = []
        for r in reads:
            if r.wev is not None:
                deps.append(r.wev)
        for w in writes:
            deps.extend(w.revs)
            if w.wev is not None:
                deps.append(w.wev)
        for ev0 in extra:
            if ev0 is not None:
                deps.append(ev0)
        if dma and key is None:
            key = id(writes[0]) if writes else "dma_misc"
        ev = Ev(eng, narrow or dma, 16 if dma else 1, key=key)
        self.per_eng[eng].append((f, deps, ev))
        self.last[eng] = ev
        if dma:
            self.last_key[key] = ev
        else:
            self.last_comp[eng] = ev
        for r in reads:
            r.revs.append(ev)
        for w in writes:
            w.wev = ev
            w.revs = []
        return ev

    def fence(self, resources):
        evs = [self.last_comp[e] for e in ENGINES
               if self.last_comp[e] is not None]
        evs += list(self.last_key.values())
        for ev in evs:
            ev.narrow = True
        for r in resources:
            r.wev = None
            r.revs = list(evs)

    def finalize(self):
        needed = set()
        for eng in ENGINES:
            for f, deps, ev in self.per_eng[eng]:
                for d in deps:
                    if d.eng != eng or d.narrow:
                        needed.add(id(d))
        counts = {}
        emit_order = []
        for eng in ENGINES:
            emit_order.extend(self.per_eng[eng])
        for eng in ENGINES:
            for f, deps, ev in self.per_eng[eng]:
                if ev.amt == 16:
                    sem = ("k", ev.key)
                    counts[sem] = counts.get(sem, 0) + 16
                    ev.sem = sem
                    ev.val = counts[sem]
                elif id(ev) in needed:
                    sem = ("e", eng)
                    counts[sem] = counts.get(sem, 0) + 1
                    ev.sem = sem
                    ev.val = counts[sem]
        self.sem_ids = sorted(set(counts.keys()), key=repr)
        out = {}
        for eng in ENGINES:
            lst = []
            hi = {}
            for f, deps, ev in self.per_eng[eng]:
                waits = []
                for d in deps:
                    if d.sem is None or (d.eng == eng and not d.narrow
                                         and d.amt != 16):
                        continue
                    if hi.get(d.sem, -1) >= d.val:
                        continue
                    hi[d.sem] = d.val
                    waits.append((d.sem, d.val))
                lst.append((waits, f, ev))
            out[eng] = lst
        return out


def _alibi_consts():
    slopes = np.array([2.0 ** (-(h + 1) / 2.0) for h in range(H)], np.float32)
    kidx = np.arange(T, dtype=np.float32)
    alF = (slopes[:, None] * kidx[None, :]).astype(ml_dtypes.bfloat16)
    p = np.arange(128, dtype=np.float32)
    alK = np.zeros((128, 256), np.float32)
    for h in range(H):
        for j in range(16):
            alK[:, h * 16 + j] = slopes[h] * (128.0 * j + p)
    pad = np.zeros((128, 272), ml_dtypes.bfloat16)
    pad[0, 0:128] = 1.0
    pad[64, 128:256] = 1.0
    pad[0:64, 256] = 1.0
    pad[64:128, 257] = 1.0
    return np.ascontiguousarray(alF), alK, np.ascontiguousarray(pad)


def build_program():
    import os
    nc = bass.Bass("TRN2", target_bir_lowering=False, detect_race_conditions=False)
    s = Sched()

    x_in = nc.dram_tensor("x", [BPC, T, C], F32, kind="ExternalInput").ap()
    w_attn = nc.dram_tensor("w_attn", [C, 3 * C], F32, kind="ExternalInput").ap()
    b_attn = nc.dram_tensor("b_attn", [3 * C], F32, kind="ExternalInput").ap()
    w_proj = nc.dram_tensor("w_proj", [C, C], F32, kind="ExternalInput").ap()
    b_proj = nc.dram_tensor("b_proj", [C], F32, kind="ExternalInput").ap()
    temp = nc.dram_tensor("temp", [1], F32, kind="ExternalInput").ap()
    w_gate = nc.dram_tensor("w_gate", [FF, C], F32, kind="ExternalInput").ap()
    w_up = nc.dram_tensor("w_up", [FF, C], F32, kind="ExternalInput").ap()
    w_down = nc.dram_tensor("w_down", [C, FF], F32, kind="ExternalInput").ap()
    c_in = nc.dram_tensor("cvec", [BPC, C], F32, kind="ExternalInput").ap()
    wmod = nc.dram_tensor("wmod", [C, 6 * C], F32, kind="ExternalInput").ap()
    out = nc.dram_tensor("out", [BPC, T, C], F32, kind="ExternalOutput").ap()

    x1sp = nc.dram_tensor("x1sp", [BPC, T, C], F32, kind="Internal").ap()
    h2sp = nc.dram_tensor("h2sp", [BPC, NCH, 128, T], BF16, kind="Internal").ap()
    upsp = nc.dram_tensor("upsp", [BPC, FCH, 128, T], BF16, kind="Internal").ap()
    gasp = nc.dram_tensor("gasp", [BPC, FCH, 128, T], BF16, kind="Internal").ap()
    gssp = nc.dram_tensor("gssp", [BPC, FCH, 128, T], BF16, kind="Internal").ap()
    osp = nc.dram_tensor("osp", [BPC, FCH, 128, T], BF16, kind="Internal").ap()

    alF_np, alK_np, pad_np = _alibi_consts()
    c_alF = nc.inline_tensor(alF_np.view(np.uint16), "alibiFc").ap().bitcast(BF16)
    c_alK = nc.inline_tensor(alK_np, "alibiKc").ap()
    c_pad = nc.inline_tensor(pad_np.view(np.uint16), "padonec").ap().bitcast(BF16)
    c_idb = nc.inline_tensor(np.eye(128, dtype=ml_dtypes.bfloat16).view(np.uint16),
                             "identbc").ap().bitcast(BF16)
    c_idf = nc.inline_tensor(np.eye(128, dtype=np.float32), "identfc").ap()
    tri_np = np.where(np.arange(128)[:, None] <= np.arange(128)[None, :],
                      np.float32(1e30), np.float32(-1e30)).astype(np.float32)
    c_tri = nc.inline_tensor(tri_np, "trimaskc").ap()

    ctx = contextlib.ExitStack()
    sbuf = lambda nm, sh, dt=BF16: ctx.enter_context(nc.sbuf_tensor(nm, sh, dt))
    psum = lambda nm, sh, dt=F32: ctx.enter_context(nc.psum_tensor(nm, sh, dt))

    with ctx:
        ctx.enter_context(nc.allow_low_precision(
            "bf16 matmul operands / bf16-rounded normalization by design; "
            "fp32 PSUM accumulation and fp32 statistics throughout"))
        arA = sbuf("arA", [128, 16384])
        arB = sbuf("arB", [128, 16384])
        arC = sbuf("arC", [128, 16384])
        arD = sbuf("arD", [128, 16384])
        wpj = sbuf("wpj", [128, 8192])
        fscr = sbuf("fscr", [128, 6144], F32)
        fscr2 = sbuf("fsc2", [128, 1024], F32)
        sqb = sbuf("sqb", [128, 2048])
        qpad = sbuf("qpad", [128, 2048])
        uzpad = sbuf("uzp", [128, 2048])
        padc = sbuf("padc", [128, 272])
        ibufC = sbuf("ibc", [128, 512], I32)
        alK = sbuf("alK", [128, 256], F32)
        wcast = sbuf("wc", [128, 2048])
        spst = sbuf("spst", [128, 2048])
        iscr = sbuf("iscr", [128, 1024], I32)
        scal = sbuf("scal", [128, 64], F32)
        srow = sbuf("srow", [68, 64], F32)
        modv = sbuf("modv", [128, 96], F32)
        gabv = sbuf("gabv", [128, 16], F32)
        battn = sbuf("batt", [128, 16], F32)
        bprojc = sbuf("bpjc", [128, 8], F32)
        bvrow = sbuf("bvr", [128, 1024])
        onescol = sbuf("onc", [128, 4])
        onesrow = sbuf("onr", [65, 128])
        onesrf = sbuf("onrf", [1, 128], F32)
        identb = sbuf("idb", [128, 128])
        identf = sbuf("idf", [128, 128], F32)
        trimask = sbuf("tri", [128, 128], F32)
        mxacc = sbuf("mxac", [128, 16], F32)
        rrow = sbuf("rrow", [128, 512])
        crow = fscr[0:BPC, 4096:5120]
        siluc = sbuf("silc", [128, 2 * NCH], F32)

        pS = [psum("pS0", [128, 512]), psum("pS1", [128, 512]), psum("pS2", [128, 512])]
        pY = [psum("pY0", [128, 512]), psum("pY1", [128, 512])]
        pR = psum("pR", [128, 512])
        pT = [psum("pT0", [128, 1024], BF16), psum("pT1", [128, 1024], BF16)]
        pS = pS + [pT[0].ap().bitcast(F32), pT[1].ap().bitcast(F32)]

        FS = [fscr[:, i * 1024:(i + 1) * 1024] for i in range(6)]
        sig2k = fscr[:, 4096:6144]
        mview = modv[:, :].rearrange("p (b2 c6 g) -> p b2 c6 g", b2=2, c6=6)

        def mcol(b, comp, g):
            i = b * 48 + comp * 8 + g
            return modv[:, i:i + 1]

        R = Res
        rFS = [R() for _ in range(6)]
        rF2 = R()
        rSQ = R()
        rSP = R()
        rDQ = R()
        rPT = [R(), R(), R(), R(), None]
        rIS = R()
        rSC = [R() for _ in range(64)]
        rSR = R()
        rMX = [R() for _ in range(16)]
        rMod, rGab, rBat, rBpc, rBv, rSil = R(), R(), R(), R(), R(), R()
        rCrow = rFS[4]
        rOnes, rId, rTri = R(), R(), R()
        rQP = [R() for _ in range(4)]
        rUZ = [[R() for _ in range(4)] for _ in range(2)]
        rPadc, rALK = R(), R()
        rA_, rB_, rC_, rD_, rWPJ = R(), R(), R(), R(), R()
        rXp = [R(), R()]
        rRR = R()
        rPSb = [R(), R(), R()]
        rPYb = [R(), R()]
        rPR = R()
        rPTb = [R(), R()]
        rPSb = rPSb + [rPTb[0], rPTb[1]]
        rBI, rBQ = R(), R()
        rBIh = [R(), R(), R(), R()]
        rF2h = [R(), R()]
        rDS = [R(), R()]
        rASt = [R(), R()]
        rH2sp = [R(), R()]
        rX1sp = [R(), R()]
        rUpsp = [R(), R()]
        rGasp = [R(), R()]
        rGssp = [R(), R()]
        rOsp = [R(), R()]

        ALL_RES = [v for v in list(locals().values()) if isinstance(v, Res)]
        for lst in (rFS, rPT, rSC, rMX, rPSb, rPYb, rPTb, rH2sp, rX1sp, rUpsp,
                    rGasp, rGssp, rOsp, rQP, rUZ[0], rUZ[1], rDS,
                    [rBI, rBQ], rBIh, rASt, rF2h):
            ALL_RES.extend(r for r in lst if r is not None)

        def dma(dst, src, q="sync", reads=(), writes=(), extra=(), slow=False):
            return s.step("sync" if q == "sync" else "pool",
                          lambda e: e.dma_start(out=dst, in_=src,
                                                allow_slow_non_contiguous=slow),
                          reads=reads, writes=writes, extra=extra, dma=True)

        def vec(f, **kw):
            return s.step("vec", f, **kw)

        def act(f, **kw):
            return s.step("act", f, **kw)

        def pe(f, **kw):
            return s.step("pe", f, **kw)

        def pool(f, **kw):
            return s.step("pool", f, **kw)

        dma(srow[0:1, 0:1], temp.rearrange("(a b) -> a b", b=1), writes=[rSR])
        dma(identb[:, :], c_idb, writes=[rId])
        dma(identf[:, :], c_idf, writes=[rId])
        dma(trimask[:, :], c_tri, writes=[rTri])
        dma(crow[:, :], c_in, writes=[rCrow])
        dma(fscr[0:1, 0:3072], b_attn.rearrange("(a n) -> a n", a=1),
            writes=[rFS[0], rFS[1], rFS[2]])

        pool(lambda e: e.memset(onescol[:, :], 1.0), writes=[rOnes], narrow=True)
        pool(lambda e: e.memset(onesrow[:, :], 1.0), writes=[rOnes], narrow=True)
        pool(lambda e: e.memset(onesrf[0:1, :], 1.0), writes=[rOnes], narrow=True)
        pool(lambda e: e.memset(qpad[:, :], 0.0), writes=rQP, narrow=True)
        pool(lambda e: e.memset(uzpad[:, :], 0.0),
             writes=rUZ[0] + rUZ[1], narrow=True)
        pool(lambda e: e.memset(bvrow[:, :], 0.0), writes=[rBv], narrow=True)
        pool(lambda e: e.memset(rrow[:, :], 0.0), writes=[rRR], narrow=True)
        dma(padc[:, :], c_pad, writes=[rPadc])
        dma(alK[:, :], c_alK, writes=[rALK])
        pool(lambda e: e.memset(mxacc[:, :], 0.0), writes=list(rMX), narrow=True)

        vec(lambda e: e.tensor_scalar(srow[0:1, 1:2], srow[0:1, 0:1], 8.0, None,
                                      ALU.mult), reads=[rSR], writes=[rSR], narrow=True)
        vec(lambda e: e.reciprocal(srow[0:1, 2:3], srow[0:1, 1:2]),
            reads=[rSR], writes=[rSR], narrow=True)

        def bcast(src_ap, col, rds):
            pe(lambda e: e.matmul(pR[:, 0:1], onesrf[0:1, :], src_ap,
                                  start=True, stop=True),
               reads=[rOnes] + rds, writes=[rPR], narrow=True)
            return act(lambda e: e.copy(scal[:, col:col + 1], pR[:, 0:1]),
                       reads=[rPR], writes=[rSC[col]], narrow=True)

        bcast(srow[0:1, 2:3], 0, [rSR])

        for j in range(16):
            pe(lambda e, j=j: e.transpose(pR[0:128, 0:1],
                                          fscr[0:1, j * 128:(j + 1) * 128],
                                          identf[0:1, 0:1]),
               reads=[rFS[j // 8], rFS[2], rId], writes=[rPR], narrow=True)
            act(lambda e, j=j: e.copy(battn[:, j:j + 1], pR[:, 0:1]),
                reads=[rPR], writes=[rBat], narrow=True)
        vec(lambda e: e.tensor_scalar(battn[:, 8:16], battn[:, 8:16], scal[:, 0:1],
                                      None, ALU.mult),
            reads=[rSC[0], rBat], writes=[rBat], narrow=True)
        vec(lambda e: e.tensor_copy(bvrow[0:1, :], fscr[0:1, 2048:3072]),
            reads=[rFS[2]], writes=[rBv], narrow=True)

        dma(fscr[0:1, 3072:4096], b_proj.rearrange("(a n) -> a n", a=1),
            writes=[rFS[3]])
        for j in range(8):
            pe(lambda e, j=j: e.transpose(pR[0:128, 0:1],
                                          fscr[0:1, 3072 + j * 128:3200 + j * 128],
                                          identf[0:1, 0:1]),
               reads=[rFS[3], rId], writes=[rPR], narrow=True)
            act(lambda e, j=j: e.copy(bprojc[:, j:j + 1], pR[:, 0:1]),
                reads=[rPR], writes=[rBpc], narrow=True)

        act(lambda e: e.activation(crow[:, :], crow[:, :], AF.Silu),
            reads=[rCrow], writes=[rCrow], narrow=True)
        for g in range(NCH):
            pe(lambda e, g=g: e.transpose(pR[0:128, 0:BPC],
                                          crow[0:BPC, g * 128:(g + 1) * 128],
                                          identf[0:BPC, 0:BPC]),
               reads=[rCrow, rId], writes=[rPR], narrow=True)
            vec(lambda e, g=g: e.tensor_copy(siluc[:, g * 2:g * 2 + 2], pR[:, 0:BPC]),
                reads=[rPR], writes=[rSil], narrow=True)

        rWCh = [R(), R()]
        rPT[4] = rWCh[0]
        for mo in range(48):
            bsl = mo % 2
            dma(FS[bsl].rearrange("p (g m) -> p g m", m=128),
                wmod[:, mo * 128:(mo + 1) * 128]
                .rearrange("(g p) m -> p g m", p=128), writes=[rFS[bsl]])
            for g in range(NCH):
                pe(lambda e, b2=bsl, g=g: e.matmul(
                    pY[0][:, 0:BPC],
                    FS[b2][:, g * 128:(g + 1) * 128],
                    siluc[:, g * 2:g * 2 + 2], start=(g == 0), stop=(g == NCH - 1)),
                   reads=[rFS[bsl], rSil],
                   writes=([rPYb[0]] if g in (0, NCH - 1) else []))
            comp, g8 = mo // 8, mo % 8
            dst = modv[:, :].rearrange("p (b2 cg) -> p b2 cg", b2=2)[:, :, comp * 8 + g8]
            act(lambda e, dst=dst: e.copy(dst, pY[0][:, 0:BPC]),
                reads=[rPYb[0]], writes=[rMod], narrow=True)

        for comp in (1, 4):
            vec(lambda e, comp=comp: e.tensor_scalar(
                mview[:, :, comp, :], mview[:, :, comp, :], 1.0, None, ALU.add),
                reads=[rMod], writes=[rMod], narrow=True)
        for comp in (2, 5):
            gsl = mview[:, :, comp, :]
            act(lambda e, gsl=gsl: e.activation(
                scal[:, 8:24].rearrange("p (b2 g) -> p b2 g", b2=2), gsl, AF.Sign),
                reads=[rMod], writes=[rSC[8]], narrow=True)
            act(lambda e, gsl=gsl: e.activation(
                scal[:, 24:40].rearrange("p (b2 g) -> p b2 g", b2=2), gsl, AF.Abs),
                reads=[rMod], writes=[rSC[24]], narrow=True)
            vec(lambda e: e.tensor_scalar(scal[:, 24:40], scal[:, 24:40], 0.25, None,
                                          ALU.max),
                reads=[rSC[24]], writes=[rSC[24]], narrow=True)
            vec(lambda e, gsl=gsl: e.tensor_tensor(
                gsl, scal[:, 8:24].rearrange("p (b2 g) -> p b2 g", b2=2),
                scal[:, 24:40].rearrange("p (b2 g) -> p b2 g", b2=2), ALU.mult),
                reads=[rSC[8], rSC[24]], writes=[rMod], narrow=True)
        for b in range(BPC):
            vec(lambda e, b=b: e.tensor_tensor(
                gabv[:, b * 8:(b + 1) * 8], mview[:, b, 2, :], bprojc[:, :], ALU.mult),
                reads=[rMod, rBpc], writes=[rGab], narrow=True)

        s.fence(ALL_RES + rWCh)

        LN_E = {"sb": 40, "sq": None, "sqr": None, "ln": None, "lnr": None}
        LN_SETS = None

        def layernorm(src_res, src_ap, tmp_ap, tmp_res, par=0):
            P_ = LN_SETS[par]
            sb = P_["sb"]
            vec(lambda e: e.reduce_sum(scal[:, sb:sb + 1], src_ap, AX.X),
                reads=[src_res], writes=[rSC[sb]], narrow=True)
            vec(lambda e: e.tensor_scalar(scal[:, sb + 1:sb + 2],
                                          scal[:, sb:sb + 1], 1.0 / C,
                                          None, ALU.mult),
                reads=[rSC[sb]], writes=[rSC[sb + 1]], narrow=True)
            vec(lambda e: e.tensor_scalar(tmp_ap, src_ap, scal[:, sb + 1:sb + 2],
                                          None, ALU.subtract),
                reads=[src_res, rSC[sb + 1]], writes=[tmp_res])
            vec(lambda e: e.tensor_tensor(P_["sq"], tmp_ap, tmp_ap, ALU.mult),
                reads=[tmp_res], writes=P_["sqr"])
            vec(lambda e: e.reduce_sum(scal[:, sb + 2:sb + 3], P_["sq"], AX.X),
                reads=P_["sqr"], writes=[rSC[sb + 2]], narrow=True)
            vec(lambda e: e.tensor_scalar(scal[:, sb + 3:sb + 4],
                                          scal[:, sb + 2:sb + 3], 1.0 / C,
                                          1e-5, ALU.mult, ALU.add),
                reads=[rSC[sb + 2]], writes=[rSC[sb + 3]], narrow=True)
            act(lambda e: e.activation(scal[:, sb + 4:sb + 5],
                                       scal[:, sb + 3:sb + 4], AF.Sqrt),
                reads=[rSC[sb + 3]], writes=[rSC[sb + 4]], narrow=True)
            vec(lambda e: e.reciprocal(scal[:, sb + 5:sb + 6],
                                       scal[:, sb + 4:sb + 5]),
                reads=[rSC[sb + 4]], writes=[rSC[sb + 5]], narrow=True)
            vec(lambda e: e.tensor_scalar(P_["ln"], tmp_ap,
                                          scal[:, sb + 5:sb + 6],
                                          None, ALU.mult),
                reads=[tmp_res, rSC[sb + 5]], writes=P_["lnr"])

        LN_SETS = [
            {"sb": 40, "sq": sqb[:, 0:1024], "sqr": [rSQ],
             "ln": sqb[:, 1024:2048], "lnr": [rSQ]},
            {"sb": 50, "sq": uzpad[:, 0:1024],
             "sqr": [rUZ[0][0], rUZ[0][1], rUZ[1][0], rUZ[1][1]],
             "ln": uzpad[:, 1024:2048],
             "lnr": [rUZ[0][2], rUZ[0][3], rUZ[1][2], rUZ[1][3]]},
        ]

        def ln1_qkv(b):
            for ti in range(NTIL):
                xs = 2 + ti % 2
                par = ti % 2
                tmp, rtmp = (FS[4], rFS[4]) if par == 0 else (FS[5], rFS[5])
                lnb, lnres = LN_SETS[par]["ln"], LN_SETS[par]["lnr"]
                dma(FS[xs], x_in[b, ti * 128:(ti + 1) * 128, :], writes=[rFS[xs]])
                layernorm(rFS[xs], FS[xs], tmp, rtmp, par)
                for g in range(NCH):
                    pp = g % 2
                    pe(lambda e, g=g, pp=pp, lnb=lnb: e.transpose(
                        pT[pp][:, 0:128], lnb[:, g * 128:(g + 1) * 128],
                        identb[:, :]),
                       reads=lnres + [rId], writes=[rPTb[pp]])
                    act(lambda e, b=b, g=g, ti=ti, pp=pp: e.activation(
                        arD[:, g * T + ti * 128:g * T + (ti + 1) * 128],
                        pT[pp][:, 0:128], AF.Identity,
                        bias=mcol(b, 0, g), scale=mcol(b, 1, g)),
                        reads=[rPTb[pp], rMod], writes=[rD_])
            for oc in range(16):
                bsl = oc % 2
                dma(FS[bsl].rearrange("p (g m) -> p g m", m=128),
                    w_attn[:, oc * 128:(oc + 1) * 128]
                    .rearrange("(g p) m -> p g m", p=128), writes=[rFS[bsl]])
                if oc >= 8:
                    vec(lambda e, b2=bsl: e.tensor_scalar(
                        wcast[:, b2 * 1024:(b2 + 1) * 1024], FS[b2], scal[:, 0:1],
                        None, ALU.mult),
                        reads=[rFS[bsl], rSC[0]], writes=[rWCh[bsl]])
                else:
                    vec(lambda e, b2=bsl: e.tensor_copy(
                        wcast[:, b2 * 1024:(b2 + 1) * 1024], FS[b2]),
                        reads=[rFS[bsl]], writes=[rWCh[bsl]])
                for st in range(NST):
                    sp2 = st % 4
                    for g in range(NCH):
                        pe(lambda e, b2=bsl, g=g, st=st, sp2=sp2: e.matmul(
                            pS[sp2][:, :],
                            wcast[:, b2 * 1024 + g * 128:b2 * 1024 + (g + 1) * 128],
                            arD[:, g * T + st * 512:g * T + (st + 1) * 512],
                            start=(g == 0), stop=(g == NCH - 1)),
                           reads=[rWCh[bsl], rD_],
                           writes=([rPSb[sp2]] if g in (0, NCH - 1) else []))
                    dbuf, dres = (arA, rA_) if oc < 8 else (arB, rB_)
                    og = oc % 8
                    act(lambda e, d=dbuf, og=og, st=st, sp2=sp2, oc=oc: e.activation(
                        d[:, og * T + st * 512:og * T + (st + 1) * 512],
                        pS[sp2][:, :], AF.Identity, bias=battn[:, oc:oc + 1], scale=1.0),
                        reads=[rPSb[sp2], rBat], writes=[dres])
            for vh in range(2):
                for g in range(NCH):
                    bsl = g % 2
                    dma(FS[bsl][:, 0:512], w_attn[g * 128:(g + 1) * 128,
                                                  2048 + vh * 512:2560 + vh * 512],
                        writes=[rFS[bsl]])
                    vec(lambda e, b2=bsl, g=g: e.tensor_copy(
                        wpj[:, g * 512:(g + 1) * 512], FS[b2][:, 0:512]),
                        reads=[rFS[bsl]], writes=[rWPJ])
                for tb in range(NTIL):
                    sp2 = tb % 4
                    for g in range(NCH):
                        pe(lambda e, g=g, tb=tb, sp2=sp2: e.matmul(
                            pS[sp2][:, :], arD[:, g * T + tb * 128:g * T + (tb + 1) * 128],
                            wpj[:, g * 512:(g + 1) * 512], start=(g == 0), stop=False),
                           reads=[rWPJ, rD_], writes=([rPSb[sp2]] if g == 0 else []))
                    pe(lambda e, vh=vh, sp2=sp2: e.matmul(
                        pS[sp2][:, :], padc[:, 0:128],
                        bvrow[0:128, vh * 512:(vh + 1) * 512], start=False, stop=True),
                       reads=[rPadc, rBv], writes=[rPSb[sp2]])
                    act(lambda e, tb=tb, vh=vh, sp2=sp2: e.copy(
                        arC[:, tb * 1024 + vh * 512:tb * 1024 + (vh + 1) * 512],
                        pS[sp2][:, :]), reads=[rPSb[sp2]], writes=[rC_])

        def attention(b):
            for g in range(NCH):
                bsl = g % 2
                dma(FS[bsl], w_proj[g * 128:(g + 1) * 128, :], writes=[rFS[bsl]])
                vec(lambda e, b2=bsl, g=g: e.tensor_copy(
                    wpj[:, g * 1024:(g + 1) * 1024], FS[b2]),
                    reads=[rFS[bsl]], writes=[rWPJ])

            pending_epi = [None]
            for h in range(H):
                hb = (h % 2) * 64
                hc = h // 2
                par = h % 2
                if par == 0:
                    act(lambda e, hc=hc: e.activation(
                        sqb[0:128, 0:2048], arB[0:128, hc * T:(hc + 1) * T],
                        AF.Square), reads=[rB_], writes=[rSQ])
                    for qc in range(4):
                        pe(lambda e, qc=qc: e.matmul(
                            pR[0:1, 0:512], padc[:, 256:257],
                            sqb[0:128, qc * 512:(qc + 1) * 512], start=True, stop=True,
                            tile_position=(0, 0), skip_group_check=True),
                           reads=[rPadc, rSQ], writes=[rPR])
                        pe(lambda e, qc=qc: e.matmul(
                            pR[64:65, 0:512], padc[:, 257:258],
                            sqb[0:128, qc * 512:(qc + 1) * 512], start=True, stop=True,
                            tile_position=(0, 64), skip_group_check=True),
                           reads=[rPadc, rSQ], writes=[rPR])
                        vec(lambda e, qc=qc: e.reduce_max(
                            srow[0:1, 4 + qc:5 + qc], pR[0:1, 0:512], AX.X),
                            reads=[rPR], writes=[rSR], narrow=True)
                        vec(lambda e, qc=qc: e.reduce_max(
                            srow[64:65, 4 + qc:5 + qc], pR[64:65, 0:512], AX.X),
                            reads=[rPR], writes=[rSR], narrow=True)
                    vec(lambda e: e.tensor_reduce(srow[0:1, 3:4], srow[0:1, 4:8], AX.X,
                                                  ALU.max),
                        reads=[rSR], writes=[rSR], narrow=True)
                    vec(lambda e: e.tensor_reduce(srow[64:65, 3:4], srow[64:65, 4:8],
                                                  AX.X, ALU.max),
                        reads=[rSR], writes=[rSR], narrow=True)
                    act(lambda e: e.activation(srow[0:1, 3:4], srow[0:1, 3:4], AF.Sqrt),
                        reads=[rSR], writes=[rSR], narrow=True)
                    act(lambda e: e.activation(srow[64:65, 3:4], srow[64:65, 3:4],
                                               AF.Sqrt),
                        reads=[rSR], writes=[rSR], narrow=True)
                    vec(lambda e: e.tensor_scalar(srow[0:1, 3:4], srow[0:1, 3:4], -1.0,
                                                  None, ALU.mult),
                        reads=[rSR], writes=[rSR], narrow=True)
                    vec(lambda e: e.tensor_scalar(srow[64:65, 3:4], srow[64:65, 3:4],
                                                  -1.0, None, ALU.mult),
                        reads=[rSR], writes=[rSR], narrow=True)
                    act(lambda e, hc=hc: e.activation(
                        sqb[0:128, 0:2048], arA[0:128, hc * T:(hc + 1) * T],
                        AF.Square), reads=[rA_], writes=[rSQ])

                for qc in range(4):
                    pe(lambda e, par=par, hb=hb, qc=qc: e.matmul(
                        pR[hb:hb + 1, 0:512], padc[:, 256 + par:257 + par],
                        sqb[0:128, qc * 512:(qc + 1) * 512], start=True, stop=True,
                        tile_position=(0, hb), skip_group_check=True),
                       reads=[rPadc, rSQ], writes=[rPR])
                    act(lambda e, hb=hb: e.activation(
                        fscr[hb:hb + 1, 5120:5632], pR[hb:hb + 1, 0:512], AF.Sqrt),
                        reads=[rPR], writes=[rFS[5]], narrow=True)
                    dma(uzpad[hb:hb + 1, qc * 512:(qc + 1) * 512],
                        c_alF[h:h + 1, qc * 512:(qc + 1) * 512],
                        writes=[rUZ[par][qc]])
                    vec(lambda e, par=par, hb=hb, qc=qc: e.scalar_tensor_tensor(
                        uzpad[hb:hb + 1, qc * 512:(qc + 1) * 512],
                        fscr[hb:hb + 1, 5120:5632], srow[hb:hb + 1, 3:4],
                        uzpad[hb:hb + 1, qc * 512:(qc + 1) * 512],
                        ALU.mult, ALU.subtract),
                        reads=[rFS[5], rSR, rUZ[par][qc]], writes=[rUZ[par][qc]],
                        narrow=True)

                yrow = hb
                sumrow = 64 - hb

                def emit_fill(q2, par=par, hb=hb, hc=hc):
                    qo2 = par * 1024 + (q2 % 2) * 512
                    vec(lambda e, q2=q2, qo2=qo2: e.tensor_copy(
                        qpad[hb:hb + 64, qo2:qo2 + 512],
                        arA[hb:hb + 64, hc * T + q2 * 512:hc * T + (q2 + 1) * 512]),
                        reads=[rA_], writes=[rQP[par * 2 + q2 % 2]])

                emit_fill(0)
                for qs in range(NST):
                    nk = qs * 4 + 4
                    yb = (h * NST + qs) % 2
                    qoff = par * 1024 + (qs % 2) * 512
                    qres = rQP[par * 2 + qs % 2]

                    def pslot(pp):
                        if pp < 4:
                            return spst[:, pp * 512:(pp + 1) * 512], rPT[pp]
                        return wcast[:, 0:512], rPT[4]

                    def emit_s(j, qs=qs, qoff=qoff, qres=qres):
                        co = max(0, j * 128 - qs * 512)
                        N = 512 - co
                        sp2 = j % 5
                        pe(lambda e, hc=hc, j=j, co=co, N=N, sp2=sp2, qoff=qoff:
                           e.matmul(pS[sp2][:, co:co + N],
                                    arB[0:128,
                                        hc * T + j * 128:hc * T + (j + 1) * 128],
                                    qpad[0:128, qoff + co:qoff + co + N],
                                    start=True, stop=False),
                           reads=[qres, rB_], writes=[rPSb[sp2]])
                        pe(lambda e, par=par, j=j, qs=qs, co=co, N=N, sp2=sp2:
                           e.matmul(pS[sp2][:, co:co + N],
                                    padc[:, par * 128:par * 128 + 128],
                                    uzpad[0:128, qs * 512 + co:qs * 512 + co + N],
                                    start=False, stop=True),
                           reads=[rPadc, rUZ[par][qs]], writes=[rPSb[sp2]])
                        if j * 128 == qs * 512 + co:
                            vec(lambda e, co=co, sp2=sp2: e.tensor_tensor(
                                pS[sp2][:, co:co + 128], pS[sp2][:, co:co + 128],
                                trimask[:, :], ALU.min),
                                reads=[rTri], writes=[rPSb[sp2]])
                        pb, pr = pslot(j % 5)
                        act(lambda e, h=h, j=j, co=co, N=N, sp2=sp2, pb=pb:
                            e.activation(
                                pb[:, 0:N], pS[sp2][:, co:co + N],
                                AF.Exp, bias=alK[:, h * 16 + j:h * 16 + j + 1]),
                            reads=[rPSb[sp2], rALK], writes=[pr])

                    def emit_pv(j, qs=qs, yb=yb, nk=nk):
                        co = max(0, j * 128 - qs * 512)
                        N = 512 - co
                        pb, pr = pslot(j % 5)
                        pe(lambda e, h=h, j=j, co=co, N=N, yb=yb, yrow=yrow, pb=pb,
                           nk=nk: e.matmul(
                               pY[yb][yrow:yrow + 64, co:co + N],
                               arC[:, j * 1024 + h * 64:j * 1024 + (h + 1) * 64],
                               pb[:, 0:N],
                               start=(j == 0), stop=(j == nk - 1),
                               tile_position=(0, yrow), skip_group_check=True),
                           reads=[rC_, pr], writes=[rPYb[yb]])
                        pe(lambda e, j=j, co=co, N=N, yb=yb, sumrow=sumrow, pb=pb,
                           nk=nk: e.matmul(
                               pY[yb][sumrow:sumrow + 1, co:co + N],
                               onescol[:, 1:2], pb[:, 0:N],
                               start=(j == 0), stop=(j == nk - 1),
                               tile_position=(0, sumrow), skip_group_check=True),
                           reads=[rOnes, pr], writes=[rPYb[yb]])

                    DEPTH = int(os.environ.get("KERNEL_DEPTH", "4"))
                    for j0 in range(min(DEPTH, nk)):
                        emit_s(j0)
                    if qs + 1 < NST:
                        emit_fill(qs + 1)
                    for j in range(nk):
                        if j + DEPTH < nk:
                            emit_s(j + DEPTH)
                        emit_pv(j)
                        if j == 1 and pending_epi[0] is not None:
                            pending_epi[0]()
                            pending_epi[0] = None
                    if pending_epi[0] is not None:
                        pending_epi[0]()
                        pending_epi[0] = None

                    def epilogue(yb=yb, sumrow=sumrow, yrow=yrow, hc=hc, qs=qs):
                        vec(lambda e, yb=yb, sumrow=sumrow: e.reciprocal(
                            rrow[sumrow:sumrow + 1, :],
                            pY[yb][sumrow:sumrow + 1, :]),
                            reads=[rPYb[yb]], writes=[rRR], narrow=True)
                        pe(lambda e, sumrow=sumrow, yrow=yrow: e.matmul(
                            pR[yrow:yrow + 64, :],
                            padc[:, (sumrow * 2):(sumrow * 2) + 64],
                            rrow[0:128, :], start=True, stop=True,
                            tile_position=(0, yrow), skip_group_check=True),
                           reads=[rPadc, rRR], writes=[rPR])
                        act(lambda e, yrow=yrow: e.copy(
                            fscr2[yrow:yrow + 64, 512:1024], pR[yrow:yrow + 64, :]),
                            reads=[rPR], writes=[rF2])
                        vec(lambda e, hc=hc, qs=qs, yb=yb, yrow=yrow: e.tensor_tensor(
                            arD[yrow:yrow + 64, hc * T + qs * 512:hc * T + (qs + 1) * 512],
                            pY[yb][yrow:yrow + 64, :], fscr2[yrow:yrow + 64, 512:1024],
                            ALU.mult), reads=[rPYb[yb], rF2], writes=[rD_])

                    if os.environ.get("KERNEL_NO_DEFER"):
                        epilogue()
                    else:
                        pending_epi[0] = epilogue

            if pending_epi[0] is not None:
                pending_epi[0]()
                pending_epi[0] = None

            for qs in range(NST):
                for oc in range(NCH):
                    sp2 = oc % 3
                    for g in range(NCH):
                        pe(lambda e, g=g, oc=oc, qs=qs, sp2=sp2: e.matmul(
                            pS[sp2][:, :],
                            wpj[:, g * 1024 + oc * 128:g * 1024 + (oc + 1) * 128],
                            arD[:, g * T + qs * 512:g * T + (qs + 1) * 512],
                            start=(g == 0), stop=(g == NCH - 1)),
                           reads=[rWPJ, rD_],
                           writes=([rPSb[sp2]] if g in (0, NCH - 1) else []))
                    tgt, trs = (wcast, [rWCh[0], rWCh[1]]) if oc < 4 else (spst, [rSP])
                    to = (oc % 4) * 512
                    act(lambda e, b=b, oc=oc, tgt=tgt, to=to, sp2=sp2: e.activation(
                        tgt[:, to:to + 512], pS[sp2][:, :], AF.Identity,
                        scale=mcol(b, 2, oc),
                        bias=gabv[:, b * 8 + oc:b * 8 + oc + 1]),
                        reads=[rPSb[sp2], rMod, rGab], writes=trs)
                for tb in range(4):
                    ti = qs * 4 + tb
                    xs = 2 + tb % 2
                    par = tb % 2
                    acc, racc = (FS[4], rFS[4]) if par == 0 else (FS[1], rFS[1])
                    tmp, rtmp = (FS[5], rFS[5]) if par == 0 else (FS[0], rFS[0])
                    lnb, lnres = LN_SETS[par]["ln"], LN_SETS[par]["lnr"]
                    h2b, h2res = LN_SETS[par]["sq"], LN_SETS[par]["sqr"]
                    mc = 46 + par * 10
                    dma(FS[xs], x_in[b, ti * 128:(ti + 1) * 128, :], writes=[rFS[xs]])
                    for oc in range(NCH):
                        tgt, trs = (wcast, [rWCh[0], rWCh[1]]) if oc < 4 else (spst, [rSP])
                        to = (oc % 4) * 512
                        pp = oc % 2
                        pe(lambda e, tgt=tgt, to=to, tb=tb, pp=pp: e.transpose(
                            pT[pp][:, 0:128], tgt[:, to + tb * 128:to + (tb + 1) * 128],
                            identb[:, :]),
                           reads=trs + [rId], writes=[rPTb[pp]])
                        vec(lambda e, oc=oc, xs=xs, pp=pp, acc=acc: e.tensor_tensor(
                            acc[:, oc * 128:(oc + 1) * 128], pT[pp][:, 0:128],
                            FS[xs][:, oc * 128:(oc + 1) * 128], ALU.add),
                            reads=[rPTb[pp], rFS[xs]], writes=[racc])
                    dma(x1sp[b, ti * 128:(ti + 1) * 128, :], acc, q="pool",
                        reads=[racc], writes=[rX1sp[b]])
                    layernorm(racc, acc, tmp, rtmp, par)
                    for g in range(NCH):
                        pp = g % 2
                        pe(lambda e, g=g, pp=pp, lnb=lnb: e.transpose(
                            pT[pp][:, 0:128], lnb[:, g * 128:(g + 1) * 128],
                            identb[:, :]),
                           reads=lnres + [rId], writes=[rPTb[pp]])
                        act(lambda e, b=b, g=g, pp=pp, h2b=h2b: e.activation(
                            h2b[:, g * 128:(g + 1) * 128], pT[pp][:, 0:128],
                            AF.Identity,
                            bias=mcol(b, 3, g), scale=mcol(b, 4, g)),
                            reads=[rPTb[pp], rMod], writes=h2res)
                    dma(h2sp[b, :, :, ti * 128:(ti + 1) * 128]
                        .rearrange("g p t -> p g t"),
                        h2b.rearrange("p (g t) -> p g t", t=128),
                        q="pool", reads=h2res, writes=[rH2sp[b]])
                    vec(lambda e, h2b=h2b, mc=mc: e.tensor_reduce(
                        scal[:, mc:mc + 1], h2b, AX.X, ALU.max,
                        apply_absolute_value=True),
                        reads=h2res, writes=[rSC[mc]], narrow=True)
                    vec(lambda e, b=b, mc=mc: e.tensor_tensor(
                        mxacc[:, b:b + 1], mxacc[:, b:b + 1], scal[:, mc:mc + 1],
                        ALU.max),
                        reads=[rSC[mc]], writes=[rMX[b]], narrow=True)

        def scale_pair(mxres, mx_ap, colm, coli):
            pe(lambda e: e.transpose(pR[0:1, 0:128], mx_ap, identf[:, :]),
               reads=[mxres, rId], writes=[rPR], narrow=True)
            vec(lambda e: e.reduce_max(srow[0:1, 8:9], pR[0:1, 0:128], AX.X),
                reads=[rPR], writes=[rSR], narrow=True)
            vec(lambda e: e.tensor_scalar(srow[0:1, 9:10], srow[0:1, 8:9],
                                          1.0 / 32767.0, 1e-12, ALU.mult, ALU.max),
                reads=[rSR], writes=[rSR], narrow=True)
            vec(lambda e: e.reciprocal(srow[0:1, 10:11], srow[0:1, 9:10]),
                reads=[rSR], writes=[rSR], narrow=True)
            bcast(srow[0:1, 9:10], colm, [rSR])
            bcast(srow[0:1, 10:11], coli, [rSR])

        def dqsl(w, g, oc):
            off = w * (NCH * FF) + g * FF + oc * 128
            if off < 16384:
                return arA[:, off:off + 128]
            if off < 32768:
                return arB[:, off - 16384:off - 16384 + 128]
            return arC[:, off - 32768:off - 32768 + 128]

        def lpbq_upgate(raw, w, oc):
            bsl = oc % 2
            dma(FS[bsl], raw[oc * 128:(oc + 1) * 128, :], writes=[rFS[bsl]])
            vec(lambda e, b2=bsl: e.tensor_reduce(
                scal[:, 32:64].rearrange("p (n w) -> p n w", w=1),
                FS[b2].rearrange("p (n w) -> p n w", w=32), AX.X, ALU.max,
                apply_absolute_value=True),
                reads=[rFS[bsl]], writes=[rSC[32]], narrow=True)
            vec(lambda e: e.tensor_scalar(scal[:, 32:64], scal[:, 32:64], 1.0 / 7.0,
                                          1e-12, ALU.mult, ALU.max),
                reads=[rSC[32]], writes=[rSC[32]], narrow=True)
            vec(lambda e: e.reciprocal(fscr2[:, 0:32], scal[:, 32:64]),
                reads=[rSC[32]], writes=[rF2], narrow=True)
            vec(lambda e, b2=bsl: e.tensor_tensor(
                iscr[:, 0:1024].rearrange("p (n w) -> p n w", w=32),
                FS[b2].rearrange("p (n w) -> p n w", w=32),
                fscr2[:, 0:32].unsqueeze(2).broadcast_to([128, 32, 32]), ALU.mult),
                reads=[rFS[bsl], rF2], writes=[rIS])
            vec(lambda e: e.tensor_tensor(
                sqb[:, 0:1024].rearrange("p (n w) -> p n w", w=32),
                iscr[:, 0:1024].rearrange("p (n w) -> p n w", w=32),
                scal[:, 32:64].unsqueeze(2).broadcast_to([128, 32, 32]), ALU.mult),
                reads=[rIS, rSC[32]], writes=[rSQ])
            for g in range(NCH):
                pp = g % 2
                pe(lambda e, g=g, pp=pp: e.transpose(
                    pT[pp][:, 0:128], sqb[:, g * 128:(g + 1) * 128], identb[:, :]),
                   reads=[rSQ, rId], writes=[rPTb[pp]])
                act(lambda e, w=w, g=g, oc=oc, pp=pp: e.copy(
                    dqsl(w, g, oc), pT[pp][:, 0:128]),
                    reads=[rPTb[pp]], writes=[rDQ])

        def ffn_A_load(b, st):
            xoff = (st % 2) * 4096
            rX = rXp[st % 2]
            dma(arD[:, xoff:xoff + 4096].rearrange("p (g t) -> p g t", t=512),
                h2sp[b, :, :, st * 512:(st + 1) * 512].rearrange("g p t -> p g t"),
                reads=[rH2sp[b]], writes=[rX])
            for half in range(4):
                ho = half * 1024
                vec(lambda e, xoff=xoff, ho=ho, rX=rX: e.tensor_scalar(
                    iscr[:, :], arD[:, xoff + ho:xoff + ho + 1024],
                    scal[:, 17:18], None, ALU.mult),
                    reads=[rX, rSC[17]], writes=[rIS])
                vec(lambda e, xoff=xoff, ho=ho, rX=rX: e.tensor_scalar(
                    arD[:, xoff + ho:xoff + ho + 1024], iscr[:, :],
                    scal[:, 16:17], None, ALU.mult),
                    reads=[rIS, rSC[16]], writes=[rX])

        def ffn_A_mm(b, w, st):
            xoff = (st % 2) * 4096
            rX = rXp[st % 2]
            dsp, rdsp, mcol = ((upsp, rUpsp, 2), (gasp, rGasp, 4))[w]
            for oc in range(FCH):
                sp2 = oc % 4
                for g in range(NCH):
                    pe(lambda e, w=w, g=g, oc=oc, xoff=xoff, sp2=sp2: e.matmul(
                        pS[sp2][:, :], dqsl(w, g, oc),
                        arD[:, xoff + g * 512:xoff + (g + 1) * 512],
                        start=(g == 0), stop=(g == NCH - 1)),
                       reads=[rDQ, rX],
                       writes=([rPSb[sp2]] if g in (0, NCH - 1) else []))
                sl = oc % 8
                gr = (oc // 4) % 2
                act(lambda e, sl=sl, sp2=sp2: e.copy(
                    arD[:, 8192 + sl * 512:8704 + sl * 512], pS[sp2][:, :]),
                    reads=[rPSb[sp2]], writes=[rASt[gr]])
                if sl % 4 == 3:
                    go = 8192 + (sl - 3) * 512
                    dma(dsp[b, oc - 3:oc + 1, :, st * 512:(st + 1) * 512]
                        .rearrange("o p t -> p o t"),
                        arD[:, go:go + 2048].rearrange("p (o t) -> p o t", t=512),
                        q="pool", reads=[rASt[gr]], writes=[rdsp[b]])
                    vec(lambda e, go=go: e.tensor_reduce(
                        scal[:, 46:47], arD[:, go:go + 2048], AX.X, ALU.max,
                        apply_absolute_value=True),
                        reads=[rASt[gr]], writes=[rSC[46]], narrow=True)
                    vec(lambda e, b=b, mcol=mcol: e.tensor_tensor(
                        mxacc[:, mcol + b:mcol + b + 1],
                        mxacc[:, mcol + b:mcol + b + 1],
                        scal[:, 46:47], ALU.max),
                        reads=[rSC[46]], writes=[rMX[mcol + b]], narrow=True)

        uzI = uzpad[:, :].bitcast(I32)
        spI = spst[:, :].bitcast(I32)

        def ibuf(h):
            q = h % 4
            src = uzI if q < 2 else spI
            return src[:, (q % 2) * 512:(q % 2) * 512 + 512]
        f2bf = fscr2[:, :].bitcast(BF16)[:, 1024:2048]
        f2bf2 = fscr2[:, :].bitcast(BF16)[:, 0:1024]

        def spB(b):
            scale_pair(rMX[2 + b], mxacc[:, 2 + b:3 + b], 18, 19)
            scale_pair(rMX[4 + b], mxacc[:, 4 + b:5 + b], 20, 21)

        def ffn_B(b, ocs=None):
            ocs = list(range(FCH)) if ocs is None else list(ocs)
            segs = [(oc, sg) for oc in ocs for sg in range(4)]
            n = len(segs)

            def b_load(oc, hf):
                dma(wcast[:, hf * 1024:(hf + 1) * 1024],
                    gasp[b, oc, :, hf * 1024:(hf + 1) * 1024],
                    reads=[rGasp[b]], writes=[rWCh[hf]])

            def b_int(g):
                oc, sg = segs[g]
                so = sg * 512
                ui, ru = ibuf(g), rBIh[g % 4]
                ex = [rSP] if g % 4 >= 2 else []
                vec(lambda e, so=so, ui=ui: e.tensor_scalar(
                    ui, wcast[:, so:so + 512], scal[:, 21:22], None,
                    ALU.mult), reads=[rWCh[sg // 2], rSC[21]] + ex, writes=[ru])

            def b_act(g):
                oc, sg = segs[g]
                so = sg * 512
                hf = sg // 2
                ui, ru = ibuf(g), rBIh[g % 4]
                act(lambda e, so=so, ui=ui: e.activation(
                    fscr[:, 4096 + so:4608 + so], ui, AF.Sigmoid,
                    scale=scal[:, 20:21]),
                    reads=[ru, rSC[20]], writes=[rFS[4 + hf]])
                act(lambda e, so=so, ui=ui: e.activation(
                    sqb[:, so:so + 512], ui, AF.Identity,
                    scale=scal[:, 20:21]),
                    reads=[ru, rSC[20]], writes=[rSQ])

            def b_fq(g):
                oc, sg = segs[g]
                so = sg * 512
                hf = sg // 2
                ui, ru = ibuf(g), rBIh[g % 4]
                vec(lambda e, so=so, ui=ui: e.tensor_scalar(
                    ui, fscr[:, 4096 + so:4608 + so], 65536.0, None,
                    ALU.mult), reads=[rFS[4 + hf]], writes=[ru])
                vec(lambda e, so=so, ui=ui: e.tensor_scalar(
                    qpad[:, so:so + 512], ui, 65535.0,
                    1.0 / 65536.0, ALU.min, ALU.mult),
                    reads=[ru], writes=[rBQ])

            def b_gs(oc, hf):
                ho = hf * 1024
                fb = (f2bf2, f2bf)[hf]
                vec(lambda e, ho=ho, fb=fb: e.tensor_tensor(
                    fb[:, :], sqb[:, ho:ho + 1024], qpad[:, ho:ho + 1024],
                    ALU.mult), reads=[rSQ, rBQ], writes=[rF2h[hf]])
                dma(gssp[b, oc, :, ho:ho + 1024], fb[:, :], q="pool",
                    reads=[rF2h[hf]], writes=[rGssp[b]])
                vec(lambda e, b=b, fb=fb: e.tensor_reduce(
                    mxacc[:, 10 + b:11 + b], fb[:, :], AX.X,
                    ALU.max, apply_absolute_value=True),
                    reads=[rF2h[hf]], writes=[rMX[10 + b]], narrow=True)
                vec(lambda e, b=b: e.tensor_tensor(
                    mxacc[:, 6 + b:7 + b], mxacc[:, 6 + b:7 + b],
                    mxacc[:, 10 + b:11 + b],
                    ALU.max), reads=[rMX[10 + b]], writes=[rMX[6 + b]], narrow=True)

            b_load(segs[0][0], 0)
            b_load(segs[0][0], 1)
            for g in range(n + 2):
                if g < n:
                    b_int(g)
                    oc, sg = segs[g]
                    if sg == 1 and g + 3 < n:
                        b_load(segs[g + 3][0], 0)
                    if sg == 3 and g + 1 < n:
                        b_load(segs[g + 1][0], 1)
                if g >= 1 and g - 1 < n:
                    b_act(g - 1)
                if g >= 2 and g - 2 < n:
                    b_fq(g - 2)
                    oc2, sg2 = segs[g - 2]
                    if sg2 == 1:
                        b_gs(oc2, 0)
                    elif sg2 == 3:
                        b_gs(oc2, 1)

        def spC(b):
            scale_pair(rMX[6 + b], mxacc[:, 6 + b:7 + b], 22, 23)

        fsbf = fscr[:, :].bitcast(BF16)[:, 8192:12288]
        rCG = [R(), R()]
        ALL_RES.extend(rCG)

        def ffn_C(b, ocs=None):
            ocs = list(range(FCH)) if ocs is None else list(ocs)

            def c_gload(i):
                dma(fsbf[:, (i % 2) * 2048:(i % 2) * 2048 + 2048],
                    gssp[b, ocs[i], :, :],
                    reads=[rGssp[b]], writes=[rCG[i % 2], rFS[4], rFS[5]])

            c_gload(0)
            for i, oc in enumerate(ocs):
                gb = fsbf[:, (i % 2) * 2048:(i % 2) * 2048 + 2048]
                rg = rCG[i % 2]
                for hf in range(2):
                    fb = (f2bf2, f2bf)[hf]
                    dma(fb[:, :], upsp[b, oc, :, hf * 1024:(hf + 1) * 1024],
                        reads=[rUpsp[b]], writes=[rF2h[hf], rF2])
                if i + 1 < len(ocs):
                    c_gload(i + 1)

                def cbuf(sg):
                    q = sg % 3
                    if q < 2:
                        return uzI[:, q * 512:q * 512 + 512], rBIh[q]
                    return ibufC[:, :], rBIh[2]

                def c_int1(sg):
                    so = sg * 512
                    ui, ru = cbuf(sg)
                    vec(lambda e, so=so, ui=ui, gb=gb: e.tensor_scalar(
                        ui, gb[:, so:so + 512], scal[:, 23:24], None,
                        ALU.mult), reads=[rg, rSC[23]],
                        writes=[ru])

                def c_act1(sg):
                    so = sg * 512
                    ui, ru = cbuf(sg)
                    vec(lambda e, so=so, ui=ui: e.tensor_scalar(
                        spst[:, so:so + 512], ui, scal[:, 22:23], None,
                        ALU.mult),
                        reads=[ru, rSC[22]], writes=[rSP])

                for sg in range(4):
                    c_int1(sg)
                    if sg >= 2:
                        c_act1(sg - 2)
                c_act1(2)
                c_act1(3)

                def c_int2(sg):
                    so = sg * 512
                    fb = (f2bf2, f2bf)[sg // 2]
                    ui, ru = cbuf(sg)
                    vec(lambda e, so=so, ui=ui, fb=fb, sg=sg: e.tensor_scalar(
                        ui, fb[:, (sg % 2) * 512:(sg % 2) * 512 + 512],
                        scal[:, 19:20], None,
                        ALU.mult), reads=[rF2h[sg // 2], rSC[19]],
                        writes=[ru])

                def c_act2(sg):
                    so = sg * 512
                    ui, ru = cbuf(sg)
                    vec(lambda e, so=so, ui=ui: e.tensor_scalar(
                        qpad[:, so:so + 512], ui, scal[:, 18:19], None,
                        ALU.mult),
                        reads=[ru, rSC[18]], writes=[rBQ])

                for sg in range(4):
                    c_int2(sg)
                    if sg >= 2:
                        c_act2(sg - 2)
                c_act2(2)
                c_act2(3)
                vec(lambda e: e.tensor_tensor(spst[:, :], spst[:, :], qpad[:, :],
                                              ALU.mult),
                    reads=[rBQ], writes=[rSP])
                dma(osp[b, oc, :, :], spst[:, :], q="pool",
                    reads=[rSP], writes=[rOsp[b]])
                vec(lambda e, b=b: e.tensor_reduce(
                    mxacc[:, 12 + b:13 + b], spst[:, :], AX.X,
                    ALU.max, apply_absolute_value=True),
                    reads=[rSP], writes=[rMX[12 + b]], narrow=True)
                vec(lambda e, b=b: e.tensor_tensor(
                    mxacc[:, 8 + b:9 + b], mxacc[:, 8 + b:9 + b],
                    mxacc[:, 12 + b:13 + b],
                    ALU.max), reads=[rMX[12 + b]], writes=[rMX[8 + b]], narrow=True)

        def dqdl(g, co):
            off = co * FF + g * 128
            if off < 16384:
                return arA[:, off:off + 128]
            return arB[:, off - 16384:off - 16384 + 128]

        def lpbq_down(cos=None):
            for co in (range(NCH) if cos is None else cos):
                for third in range(3):
                    tsl = third % 2
                    dma(FS[tsl], w_down[co * 128:(co + 1) * 128,
                                        third * 1024:(third + 1) * 1024],
                        writes=[rFS[tsl]])
                    vec(lambda e, t2=tsl: e.tensor_reduce(
                        scal[:, 32:64].rearrange("p (n w) -> p n w", w=1),
                        FS[t2].rearrange("p (n w) -> p n w", w=32), AX.X, ALU.max,
                        apply_absolute_value=True),
                        reads=[rFS[tsl]], writes=[rSC[32]], narrow=True)
                    vec(lambda e: e.tensor_scalar(scal[:, 32:64], scal[:, 32:64],
                                                  1.0 / 7.0, 1e-12, ALU.mult, ALU.max),
                        reads=[rSC[32]], writes=[rSC[32]], narrow=True)
                    vec(lambda e: e.reciprocal(fscr2[:, 0:32], scal[:, 32:64]),
                        reads=[rSC[32], rF2h[0], rF2h[1]], writes=[rF2],
                        narrow=True)
                    vec(lambda e, t2=tsl: e.tensor_tensor(
                        iscr[:, 0:1024].rearrange("p (n w) -> p n w", w=32),
                        FS[t2].rearrange("p (n w) -> p n w", w=32),
                        fscr2[:, 0:32].unsqueeze(2).broadcast_to([128, 32, 32]),
                        ALU.mult), reads=[rFS[tsl], rF2], writes=[rIS])
                    vec(lambda e: e.tensor_tensor(
                        f2bf[:, 0:1024].rearrange("p (n w) -> p n w", w=32),
                        iscr[:, 0:1024].rearrange("p (n w) -> p n w", w=32),
                        scal[:, 32:64].unsqueeze(2).broadcast_to([128, 32, 32]),
                        ALU.mult), reads=[rIS, rSC[32]], writes=[rF2])
                    for g8 in range(NCH):
                        g = third * 8 + g8
                        pp = g8 % 2
                        pe(lambda e, g8=g8, pp=pp: e.transpose(
                            pT[pp][:, 0:128], f2bf[:, g8 * 128:(g8 + 1) * 128],
                            identb[:, :]),
                           reads=[rF2, rId], writes=[rPTb[pp]])
                        act(lambda e, g=g, co=co, pp=pp: e.copy(
                            dqdl(g, co), pT[pp][:, 0:128]),
                            reads=[rPTb[pp]], writes=[rDQ])

        out_evs = []

        def spD(b):
            scale_pair(rMX[8 + b], mxacc[:, 8 + b:9 + b], 24, 25)

        def ffn_D(b, sts=None):
            for st in (range(NST) if sts is None else sts):
                for grp in range(6):
                    go = grp * 2048
                    rX = rXp[grp % 2]
                    dma(arD[:, go:go + 2048].rearrange("p (g t) -> p g t", t=512),
                        osp[b, grp * 4:(grp + 1) * 4, :, st * 512:(st + 1) * 512]
                        .rearrange("g p t -> p g t"),
                        reads=[rOsp[b], rASt[0], rASt[1]], writes=[rX])
                    for hf in range(2):
                        ho = go + hf * 1024
                        vec(lambda e, ho=ho, rX=rX: e.tensor_scalar(
                            iscr[:, :], arD[:, ho:ho + 1024], scal[:, 25:26], None,
                            ALU.mult), reads=[rX, rSC[25]], writes=[rIS])
                        vec(lambda e, ho=ho, rX=rX: e.tensor_scalar(
                            arD[:, ho:ho + 1024], iscr[:, :], scal[:, 24:25], None,
                            ALU.mult), reads=[rIS, rSC[24]], writes=[rX])
                for co in range(NCH):
                    sp2 = co % 3
                    for g in range(FCH):
                        pe(lambda e, g=g, co=co, sp2=sp2: e.matmul(
                            pS[sp2][:, :], dqdl(g, co),
                            arD[:, g * 512:(g + 1) * 512],
                            start=(g == 0), stop=(g == FCH - 1)),
                           reads=[rDQ, rXp[0], rXp[1]],
                           writes=([rPSb[sp2]] if g in (0, FCH - 1) else []))
                    do = 12288 + co * 512
                    act(lambda e, b=b, co=co, do=do, sp2=sp2: e.activation(
                        arD[:, do:do + 512], pS[sp2][:, :], AF.Identity,
                        scale=mcol(b, 5, co), bias=0.0),
                        reads=[rPSb[sp2], rMod], writes=[rDS[co % 2]])
                for tb in range(4):
                    ti = st * 4 + tb
                    xs = 2 + tb % 2
                    dma(FS[xs], x1sp[b, ti * 128:(ti + 1) * 128, :],
                        reads=[rX1sp[b]], writes=[rFS[xs]])
                    for co in range(NCH):
                        do = 12288 + co * 512
                        pp = co % 2
                        pe(lambda e, do=do, tb=tb, pp=pp, co=co: e.transpose(
                            pT[pp][:, 0:128],
                            arD[:, do + tb * 128:do + (tb + 1) * 128],
                            identb[:, :]),
                           reads=[rDS[co % 2], rId], writes=[rPTb[pp]])
                        vec(lambda e, co=co, xs=xs, pp=pp: e.tensor_tensor(
                            FS[0][:, co * 128:(co + 1) * 128], pT[pp][:, 0:128],
                            FS[xs][:, co * 128:(co + 1) * 128], ALU.add),
                            reads=[rPTb[pp], rFS[xs]], writes=[rFS[0]])
                    ev = dma(out[b, ti * 128:(ti + 1) * 128, :], FS[0], q="pool",
                             reads=[rFS[0]])
                    out_evs.append(ev)

        _dbg = os.environ.get("KERNEL_DEBUG_STOP", "")
        for b in range(BPC):
            ln1_qkv(b)
            attention(b)
            s.fence(ALL_RES + rWCh)
            if _dbg == "attn%d" % b:
                break

        if not _dbg:
          for oc in range(FCH):
            lpbq_upgate(w_up, 0, oc)
          for oc in range(FCH):
            lpbq_upgate(w_gate, 1, oc)
          s.fence(ALL_RES + rWCh)

          scale_pair(rMX[0], mxacc[:, 0:1], 16, 17)
          ffn_A_load(0, 0)
          ffn_A_load(0, 1)
          ffn_A_mm(0, 1, 0)
          ffn_A_load(0, 2)
          ffn_A_mm(0, 1, 1)
          ffn_A_load(0, 3)
          ffn_A_mm(0, 1, 2)
          ffn_A_mm(0, 1, 3)
          scale_pair(rMX[4], mxacc[:, 4:5], 20, 21)
          ffn_A_load(0, 0)
          ffn_A_load(0, 1)
          ffn_A_mm(0, 0, 0)
          ffn_B(0, range(0, 6))
          ffn_A_load(0, 2)
          ffn_A_mm(0, 0, 1)
          ffn_B(0, range(6, 12))
          ffn_A_load(0, 3)
          ffn_A_mm(0, 0, 2)
          ffn_B(0, range(12, 18))
          ffn_A_mm(0, 0, 3)
          ffn_B(0, range(18, 24))
          scale_pair(rMX[2], mxacc[:, 2:3], 18, 19)
          scale_pair(rMX[1], mxacc[:, 1:2], 16, 17)
          ffn_A_load(1, 0)
          ffn_A_load(1, 1)
          ffn_A_mm(1, 1, 0)
          spC(0)
          ffn_C(0, range(0, 6))
          ffn_A_load(1, 2)
          ffn_A_mm(1, 1, 1)
          ffn_C(0, range(6, 12))
          ffn_A_load(1, 3)
          ffn_A_mm(1, 1, 2)
          ffn_C(0, range(12, 18))
          ffn_A_mm(1, 1, 3)
          ffn_C(0, range(18, 24))
          scale_pair(rMX[5], mxacc[:, 5:6], 20, 21)
          ffn_A_load(1, 0)
          ffn_A_load(1, 1)
          ffn_A_mm(1, 0, 0)
          ffn_B(1, range(0, 6))
          ffn_A_load(1, 2)
          ffn_A_mm(1, 0, 1)
          ffn_B(1, range(6, 12))
          ffn_A_load(1, 3)
          ffn_A_mm(1, 0, 2)
          ffn_B(1, range(12, 18))
          ffn_A_mm(1, 0, 3)
          ffn_B(1, range(18, 24))
          scale_pair(rMX[3], mxacc[:, 3:4], 18, 19)
          lpbq_down()
          spD(0)
          ffn_D(0, [0])
          spC(1)
          ffn_C(1, range(0, 6))
          ffn_D(0, [1])
          ffn_C(1, range(6, 12))
          ffn_D(0, [2])
          ffn_C(1, range(12, 18))
          ffn_D(0, [3])
          ffn_C(1, range(18, 24))
          spD(1)
          ffn_D(1)

        final_deps = out_evs[-1:]
        s.step("sync", lambda e: e.nop(), extra=[s.last["pool"], s.last["vec"],
                                                 s.last["act"], s.last["pe"]] + final_deps)

        plans = s.finalize()
        sem_handles = {}
        for i, sid in enumerate(s.sem_ids):
            sem_handles[sid] = ctx.enter_context(nc.semaphore("c%d" % i))

        def runner(eng_name):
            def go(e):
                for waits, f, ev in plans[eng_name]:
                    for sem_eng, val in waits:
                        e.wait_ge(sem_handles[sem_eng], val)
                    ins = f(e)
                    if ev.sem is not None:
                        ins.then_inc(sem_handles[ev.sem], ev.amt)
            return go

        with nc.Block() as blk:
            blk.sync(runner("sync"))
            blk.vector(runner("vec"))
            blk.scalar(runner("act"))
            blk.tensor(runner("pe"))
            blk.gpsimd(runner("pool"))
    return nc


_prog_cache = [None]


def kernel(**inputs):
    x = np.ascontiguousarray(np.asarray(inputs["L0_attn_x"], np.float32))
    cv = np.ascontiguousarray(np.asarray(inputs["L0_c"], np.float32))
    full = {
        "w_attn": np.ascontiguousarray(np.asarray(inputs["L0_attn_W_attn"], np.float32)),
        "b_attn": np.ascontiguousarray(np.asarray(inputs["L0_attn_b_attn"], np.float32)),
        "w_proj": np.ascontiguousarray(np.asarray(inputs["L0_attn_W_proj"], np.float32)),
        "b_proj": np.ascontiguousarray(np.asarray(inputs["L0_attn_b_proj"], np.float32)),
        "temp": np.ascontiguousarray(np.asarray(inputs["L0_attn_temperature"], np.float32)),
        "w_gate": np.ascontiguousarray(np.asarray(inputs["L0_ffn_w_gate"], np.float32)),
        "w_up": np.ascontiguousarray(np.asarray(inputs["L0_ffn_w_up"], np.float32)),
        "w_down": np.ascontiguousarray(np.asarray(inputs["L0_ffn_w_down"], np.float32)),
        "wmod": np.ascontiguousarray(np.asarray(inputs["L0_wmod"], np.float32)),
    }
    if _prog_cache[0] is None:
        _prog_cache[0] = build_program()
    nc = _prog_cache[0]
    in_maps = []
    for core in range(8):
        m = dict(full)
        m["x"] = np.ascontiguousarray(x[core * BPC:(core + 1) * BPC])
        m["cvec"] = np.ascontiguousarray(cv[core * BPC:(core + 1) * BPC])
        in_maps.append(m)
    res = run_bass_kernel_spmd(nc, in_maps, core_ids=list(range(8)))
    return np.concatenate([np.asarray(r["out"], np.float32) for r in res.results],
                          axis=0)


# version 23
# speedup vs baseline: 1.1965x; 1.0006x over previous
import contextlib

import numpy as np
import ml_dtypes

import concourse.bass as bass
import concourse.mybir as mybir
from concourse.bass_utils import run_bass_kernel_spmd

F32 = mybir.dt.float32
BF16 = mybir.dt.bfloat16
I32 = mybir.dt.int32
AF = mybir.ActivationFunctionType
ALU = mybir.AluOpType
AX = mybir.AxisListType

BPC, T, C, H, DH, FF = 2, 2048, 1024, 16, 64, 3072
NCH, FCH, NTIL, NST = 8, 24, 16, 4

ENGINES = ("sync", "vec", "act", "pe", "pool")


class Ev:
    __slots__ = ("eng", "narrow", "amt", "sem", "val", "key")

    def __init__(self, eng, narrow, amt, key=None):
        self.eng, self.narrow, self.amt = eng, narrow, amt
        self.sem = None
        self.val = None
        self.key = key


class Res:
    __slots__ = ("wev", "revs")

    def __init__(self):
        self.wev = None
        self.revs = []


class Sched:
    def __init__(self):
        self.per_eng = {e: [] for e in ENGINES}
        self.last = {e: None for e in ENGINES}
        self.last_comp = {e: None for e in ENGINES}
        self.last_key = {}

    def step(self, eng, f, reads=(), writes=(), extra=(), narrow=False, dma=False,
             key=None):
        deps = []
        for r in reads:
            if r.wev is not None:
                deps.append(r.wev)
        for w in writes:
            deps.extend(w.revs)
            if w.wev is not None:
                deps.append(w.wev)
        for ev0 in extra:
            if ev0 is not None:
                deps.append(ev0)
        if dma and key is None:
            key = id(writes[0]) if writes else "dma_misc"
        ev = Ev(eng, narrow or dma, 16 if dma else 1, key=key)
        self.per_eng[eng].append((f, deps, ev))
        self.last[eng] = ev
        if dma:
            self.last_key[key] = ev
        else:
            self.last_comp[eng] = ev
        for r in reads:
            r.revs.append(ev)
        for w in writes:
            w.wev = ev
            w.revs = []
        return ev

    def fence(self, resources):
        evs = [self.last_comp[e] for e in ENGINES
               if self.last_comp[e] is not None]
        evs += list(self.last_key.values())
        for ev in evs:
            ev.narrow = True
        for r in resources:
            r.wev = None
            r.revs = list(evs)

    def finalize(self):
        needed = set()
        for eng in ENGINES:
            for f, deps, ev in self.per_eng[eng]:
                for d in deps:
                    if d.eng != eng or d.narrow:
                        needed.add(id(d))
        counts = {}
        emit_order = []
        for eng in ENGINES:
            emit_order.extend(self.per_eng[eng])
        for eng in ENGINES:
            for f, deps, ev in self.per_eng[eng]:
                if ev.amt == 16:
                    sem = ("k", ev.key)
                    counts[sem] = counts.get(sem, 0) + 16
                    ev.sem = sem
                    ev.val = counts[sem]
                elif id(ev) in needed:
                    sem = ("e", eng)
                    counts[sem] = counts.get(sem, 0) + 1
                    ev.sem = sem
                    ev.val = counts[sem]
        self.sem_ids = sorted(set(counts.keys()), key=repr)
        out = {}
        for eng in ENGINES:
            lst = []
            hi = {}
            for f, deps, ev in self.per_eng[eng]:
                waits = []
                for d in deps:
                    if d.sem is None or (d.eng == eng and not d.narrow
                                         and d.amt != 16):
                        continue
                    if hi.get(d.sem, -1) >= d.val:
                        continue
                    hi[d.sem] = d.val
                    waits.append((d.sem, d.val))
                lst.append((waits, f, ev))
            out[eng] = lst
        return out


def _alibi_consts():
    slopes = np.array([2.0 ** (-(h + 1) / 2.0) for h in range(H)], np.float32)
    kidx = np.arange(T, dtype=np.float32)
    alF = (slopes[:, None] * kidx[None, :]).astype(ml_dtypes.bfloat16)
    p = np.arange(128, dtype=np.float32)
    alK = np.zeros((128, 256), np.float32)
    for h in range(H):
        for j in range(16):
            alK[:, h * 16 + j] = slopes[h] * (128.0 * j + p)
    pad = np.zeros((128, 272), ml_dtypes.bfloat16)
    pad[0, 0:128] = 1.0
    pad[64, 128:256] = 1.0
    pad[0:64, 256] = 1.0
    pad[64:128, 257] = 1.0
    return np.ascontiguousarray(alF), alK, np.ascontiguousarray(pad)


def build_program():
    import os
    nc = bass.Bass("TRN2", target_bir_lowering=False, detect_race_conditions=False)
    s = Sched()

    x_in = nc.dram_tensor("x", [BPC, T, C], F32, kind="ExternalInput").ap()
    w_attn = nc.dram_tensor("w_attn", [C, 3 * C], F32, kind="ExternalInput").ap()
    b_attn = nc.dram_tensor("b_attn", [3 * C], F32, kind="ExternalInput").ap()
    w_proj = nc.dram_tensor("w_proj", [C, C], F32, kind="ExternalInput").ap()
    b_proj = nc.dram_tensor("b_proj", [C], F32, kind="ExternalInput").ap()
    temp = nc.dram_tensor("temp", [1], F32, kind="ExternalInput").ap()
    w_gate = nc.dram_tensor("w_gate", [FF, C], F32, kind="ExternalInput").ap()
    w_up = nc.dram_tensor("w_up", [FF, C], F32, kind="ExternalInput").ap()
    w_down = nc.dram_tensor("w_down", [C, FF], F32, kind="ExternalInput").ap()
    c_in = nc.dram_tensor("cvec", [BPC, C], F32, kind="ExternalInput").ap()
    wmod = nc.dram_tensor("wmod", [C, 6 * C], F32, kind="ExternalInput").ap()
    out = nc.dram_tensor("out", [BPC, T, C], F32, kind="ExternalOutput").ap()

    x1sp = nc.dram_tensor("x1sp", [BPC, T, C], F32, kind="Internal").ap()
    h2sp = nc.dram_tensor("h2sp", [BPC, NCH, 128, T], BF16, kind="Internal").ap()
    upsp = nc.dram_tensor("upsp", [BPC, FCH, 128, T], BF16, kind="Internal").ap()
    gasp = nc.dram_tensor("gasp", [BPC, FCH, 128, T], BF16, kind="Internal").ap()
    gssp = nc.dram_tensor("gssp", [BPC, FCH, 128, T], BF16, kind="Internal").ap()
    osp = nc.dram_tensor("osp", [BPC, FCH, 128, T], BF16, kind="Internal").ap()

    alF_np, alK_np, pad_np = _alibi_consts()
    c_alF = nc.inline_tensor(alF_np.view(np.uint16), "alibiFc").ap().bitcast(BF16)
    c_alK = nc.inline_tensor(alK_np, "alibiKc").ap()
    c_pad = nc.inline_tensor(pad_np.view(np.uint16), "padonec").ap().bitcast(BF16)
    c_idb = nc.inline_tensor(np.eye(128, dtype=ml_dtypes.bfloat16).view(np.uint16),
                             "identbc").ap().bitcast(BF16)
    c_idf = nc.inline_tensor(np.eye(128, dtype=np.float32), "identfc").ap()
    tri_np = np.where(np.arange(128)[:, None] <= np.arange(128)[None, :],
                      np.float32(1e30), np.float32(-1e30)).astype(np.float32)
    c_tri = nc.inline_tensor(tri_np, "trimaskc").ap()

    ctx = contextlib.ExitStack()
    sbuf = lambda nm, sh, dt=BF16: ctx.enter_context(nc.sbuf_tensor(nm, sh, dt))
    psum = lambda nm, sh, dt=F32: ctx.enter_context(nc.psum_tensor(nm, sh, dt))

    with ctx:
        ctx.enter_context(nc.allow_low_precision(
            "bf16 matmul operands / bf16-rounded normalization by design; "
            "fp32 PSUM accumulation and fp32 statistics throughout"))
        arA = sbuf("arA", [128, 16384])
        arB = sbuf("arB", [128, 16384])
        arC = sbuf("arC", [128, 16384])
        arD = sbuf("arD", [128, 16384])
        wpj = sbuf("wpj", [128, 8192])
        fscr = sbuf("fscr", [128, 6144], F32)
        fscr2 = sbuf("fsc2", [128, 1024], F32)
        sqb = sbuf("sqb", [128, 2048])
        qpad = sbuf("qpad", [128, 2048])
        uzpad = sbuf("uzp", [128, 2048])
        padc = sbuf("padc", [128, 272])
        ibufC = sbuf("ibc", [128, 512], I32)
        alK = sbuf("alK", [128, 256], F32)
        wcast = sbuf("wc", [128, 2048])
        spst = sbuf("spst", [128, 2048])
        iscr = sbuf("iscr", [128, 1024], I32)
        scal = sbuf("scal", [128, 64], F32)
        srow = sbuf("srow", [68, 64], F32)
        modv = sbuf("modv", [128, 96], F32)
        gabv = sbuf("gabv", [128, 16], F32)
        battn = sbuf("batt", [128, 16], F32)
        bprojc = sbuf("bpjc", [128, 8], F32)
        bvrow = sbuf("bvr", [128, 1024])
        onescol = sbuf("onc", [128, 4])
        onesrow = sbuf("onr", [65, 128])
        onesrf = sbuf("onrf", [1, 128], F32)
        identb = sbuf("idb", [128, 128])
        identf = sbuf("idf", [128, 128], F32)
        trimask = sbuf("tri", [128, 128], F32)
        mxacc = sbuf("mxac", [128, 16], F32)
        rrow = sbuf("rrow", [128, 512])
        crow = fscr[0:BPC, 4096:5120]
        siluc = sbuf("silc", [128, 2 * NCH], F32)

        pS = [psum("pS0", [128, 512]), psum("pS1", [128, 512]), psum("pS2", [128, 512])]
        pY = [psum("pY0", [128, 512]), psum("pY1", [128, 512])]
        pR = psum("pR", [128, 512])
        pT = [psum("pT0", [128, 1024], BF16), psum("pT1", [128, 1024], BF16)]
        pS = pS + [pT[0].ap().bitcast(F32), pT[1].ap().bitcast(F32)]

        FS = [fscr[:, i * 1024:(i + 1) * 1024] for i in range(6)]
        sig2k = fscr[:, 4096:6144]
        mview = modv[:, :].rearrange("p (b2 c6 g) -> p b2 c6 g", b2=2, c6=6)

        def mcol(b, comp, g):
            i = b * 48 + comp * 8 + g
            return modv[:, i:i + 1]

        R = Res
        rFS = [R() for _ in range(6)]
        rF2 = R()
        rSQ = R()
        rSP = R()
        rDQ = R()
        rPT = [R(), R(), R(), R(), None]
        rIS = R()
        rSC = [R() for _ in range(64)]
        rSR = R()
        rMX = [R() for _ in range(16)]
        rMod, rGab, rBat, rBpc, rBv, rSil = R(), R(), R(), R(), R(), R()
        rCrow = rFS[4]
        rOnes, rId, rTri = R(), R(), R()
        rQP = [R() for _ in range(4)]
        rUZ = [[R() for _ in range(4)] for _ in range(2)]
        rPadc, rALK = R(), R()
        rA_, rB_, rC_, rD_, rWPJ = R(), R(), R(), R(), R()
        rXp = [R(), R(), R()]
        rRR = R()
        rPSb = [R(), R(), R()]
        rPYb = [R(), R()]
        rPR = R()
        rPTb = [R(), R()]
        rPSb = rPSb + [rPTb[0], rPTb[1]]
        rBI, rBQ = R(), R()
        rBIh = [R(), R(), R(), R()]
        rF2h = [R(), R()]
        rDS = [R(), R()]
        rASt = [R(), R()]
        rH2sp = [R(), R()]
        rX1sp = [R(), R()]
        rUpsp = [R(), R()]
        rGasp = [R(), R()]
        rGssp = [R(), R()]
        rOsp = [R(), R()]

        ALL_RES = [v for v in list(locals().values()) if isinstance(v, Res)]
        for lst in (rFS, rPT, rSC, rMX, rPSb, rPYb, rPTb, rH2sp, rX1sp, rUpsp,
                    rGasp, rGssp, rOsp, rQP, rUZ[0], rUZ[1], rDS,
                    [rBI, rBQ], rBIh, rASt, rF2h):
            ALL_RES.extend(r for r in lst if r is not None)

        def dma(dst, src, q="sync", reads=(), writes=(), extra=(), slow=False):
            return s.step("sync" if q == "sync" else "pool",
                          lambda e: e.dma_start(out=dst, in_=src,
                                                allow_slow_non_contiguous=slow),
                          reads=reads, writes=writes, extra=extra, dma=True)

        def vec(f, **kw):
            return s.step("vec", f, **kw)

        def act(f, **kw):
            return s.step("act", f, **kw)

        def pe(f, **kw):
            return s.step("pe", f, **kw)

        def pool(f, **kw):
            return s.step("pool", f, **kw)

        dma(srow[0:1, 0:1], temp.rearrange("(a b) -> a b", b=1), writes=[rSR])
        dma(identb[:, :], c_idb, writes=[rId])
        dma(identf[:, :], c_idf, writes=[rId])
        dma(trimask[:, :], c_tri, writes=[rTri])
        dma(crow[:, :], c_in, writes=[rCrow])
        dma(fscr[0:1, 0:3072], b_attn.rearrange("(a n) -> a n", a=1),
            writes=[rFS[0], rFS[1], rFS[2]])

        pool(lambda e: e.memset(onescol[:, :], 1.0), writes=[rOnes], narrow=True)
        pool(lambda e: e.memset(onesrow[:, :], 1.0), writes=[rOnes], narrow=True)
        pool(lambda e: e.memset(onesrf[0:1, :], 1.0), writes=[rOnes], narrow=True)
        pool(lambda e: e.memset(qpad[:, :], 0.0), writes=rQP, narrow=True)
        pool(lambda e: e.memset(uzpad[:, :], 0.0),
             writes=rUZ[0] + rUZ[1], narrow=True)
        pool(lambda e: e.memset(bvrow[:, :], 0.0), writes=[rBv], narrow=True)
        pool(lambda e: e.memset(rrow[:, :], 0.0), writes=[rRR], narrow=True)
        dma(padc[:, :], c_pad, writes=[rPadc])
        dma(alK[:, :], c_alK, writes=[rALK])
        pool(lambda e: e.memset(mxacc[:, :], 0.0), writes=list(rMX), narrow=True)

        vec(lambda e: e.tensor_scalar(srow[0:1, 1:2], srow[0:1, 0:1], 8.0, None,
                                      ALU.mult), reads=[rSR], writes=[rSR], narrow=True)
        vec(lambda e: e.reciprocal(srow[0:1, 2:3], srow[0:1, 1:2]),
            reads=[rSR], writes=[rSR], narrow=True)

        def bcast(src_ap, col, rds):
            pe(lambda e: e.matmul(pR[:, 0:1], onesrf[0:1, :], src_ap,
                                  start=True, stop=True),
               reads=[rOnes] + rds, writes=[rPR], narrow=True)
            return act(lambda e: e.copy(scal[:, col:col + 1], pR[:, 0:1]),
                       reads=[rPR], writes=[rSC[col]], narrow=True)

        bcast(srow[0:1, 2:3], 0, [rSR])

        for j in range(16):
            pe(lambda e, j=j: e.transpose(pR[0:128, 0:1],
                                          fscr[0:1, j * 128:(j + 1) * 128],
                                          identf[0:1, 0:1]),
               reads=[rFS[j // 8], rFS[2], rId], writes=[rPR], narrow=True)
            act(lambda e, j=j: e.copy(battn[:, j:j + 1], pR[:, 0:1]),
                reads=[rPR], writes=[rBat], narrow=True)
        vec(lambda e: e.tensor_scalar(battn[:, 8:16], battn[:, 8:16], scal[:, 0:1],
                                      None, ALU.mult),
            reads=[rSC[0], rBat], writes=[rBat], narrow=True)
        vec(lambda e: e.tensor_copy(bvrow[0:1, :], fscr[0:1, 2048:3072]),
            reads=[rFS[2]], writes=[rBv], narrow=True)

        dma(fscr[0:1, 3072:4096], b_proj.rearrange("(a n) -> a n", a=1),
            writes=[rFS[3]])
        for j in range(8):
            pe(lambda e, j=j: e.transpose(pR[0:128, 0:1],
                                          fscr[0:1, 3072 + j * 128:3200 + j * 128],
                                          identf[0:1, 0:1]),
               reads=[rFS[3], rId], writes=[rPR], narrow=True)
            act(lambda e, j=j: e.copy(bprojc[:, j:j + 1], pR[:, 0:1]),
                reads=[rPR], writes=[rBpc], narrow=True)

        act(lambda e: e.activation(crow[:, :], crow[:, :], AF.Silu),
            reads=[rCrow], writes=[rCrow], narrow=True)
        for g in range(NCH):
            pe(lambda e, g=g: e.transpose(pR[0:128, 0:BPC],
                                          crow[0:BPC, g * 128:(g + 1) * 128],
                                          identf[0:BPC, 0:BPC]),
               reads=[rCrow, rId], writes=[rPR], narrow=True)
            vec(lambda e, g=g: e.tensor_copy(siluc[:, g * 2:g * 2 + 2], pR[:, 0:BPC]),
                reads=[rPR], writes=[rSil], narrow=True)

        rWCh = [R(), R()]
        rPT[4] = rWCh[0]
        for mo in range(48):
            bsl = mo % 2
            dma(FS[bsl].rearrange("p (g m) -> p g m", m=128),
                wmod[:, mo * 128:(mo + 1) * 128]
                .rearrange("(g p) m -> p g m", p=128), writes=[rFS[bsl]])
            for g in range(NCH):
                pe(lambda e, b2=bsl, g=g: e.matmul(
                    pY[0][:, 0:BPC],
                    FS[b2][:, g * 128:(g + 1) * 128],
                    siluc[:, g * 2:g * 2 + 2], start=(g == 0), stop=(g == NCH - 1)),
                   reads=[rFS[bsl], rSil],
                   writes=([rPYb[0]] if g in (0, NCH - 1) else []))
            comp, g8 = mo // 8, mo % 8
            dst = modv[:, :].rearrange("p (b2 cg) -> p b2 cg", b2=2)[:, :, comp * 8 + g8]
            act(lambda e, dst=dst: e.copy(dst, pY[0][:, 0:BPC]),
                reads=[rPYb[0]], writes=[rMod], narrow=True)

        for comp in (1, 4):
            vec(lambda e, comp=comp: e.tensor_scalar(
                mview[:, :, comp, :], mview[:, :, comp, :], 1.0, None, ALU.add),
                reads=[rMod], writes=[rMod], narrow=True)
        for comp in (2, 5):
            gsl = mview[:, :, comp, :]
            act(lambda e, gsl=gsl: e.activation(
                scal[:, 8:24].rearrange("p (b2 g) -> p b2 g", b2=2), gsl, AF.Sign),
                reads=[rMod], writes=[rSC[8]], narrow=True)
            act(lambda e, gsl=gsl: e.activation(
                scal[:, 24:40].rearrange("p (b2 g) -> p b2 g", b2=2), gsl, AF.Abs),
                reads=[rMod], writes=[rSC[24]], narrow=True)
            vec(lambda e: e.tensor_scalar(scal[:, 24:40], scal[:, 24:40], 0.25, None,
                                          ALU.max),
                reads=[rSC[24]], writes=[rSC[24]], narrow=True)
            vec(lambda e, gsl=gsl: e.tensor_tensor(
                gsl, scal[:, 8:24].rearrange("p (b2 g) -> p b2 g", b2=2),
                scal[:, 24:40].rearrange("p (b2 g) -> p b2 g", b2=2), ALU.mult),
                reads=[rSC[8], rSC[24]], writes=[rMod], narrow=True)
        for b in range(BPC):
            vec(lambda e, b=b: e.tensor_tensor(
                gabv[:, b * 8:(b + 1) * 8], mview[:, b, 2, :], bprojc[:, :], ALU.mult),
                reads=[rMod, rBpc], writes=[rGab], narrow=True)

        s.fence(ALL_RES + rWCh)

        LN_E = {"sb": 40, "sq": None, "sqr": None, "ln": None, "lnr": None}
        LN_SETS = None

        def layernorm(src_res, src_ap, tmp_ap, tmp_res, par=0):
            P_ = LN_SETS[par]
            sb = P_["sb"]
            vec(lambda e: e.reduce_sum(scal[:, sb:sb + 1], src_ap, AX.X),
                reads=[src_res], writes=[rSC[sb]], narrow=True)
            vec(lambda e: e.tensor_scalar(scal[:, sb + 1:sb + 2],
                                          scal[:, sb:sb + 1], 1.0 / C,
                                          None, ALU.mult),
                reads=[rSC[sb]], writes=[rSC[sb + 1]], narrow=True)
            vec(lambda e: e.tensor_scalar(tmp_ap, src_ap, scal[:, sb + 1:sb + 2],
                                          None, ALU.subtract),
                reads=[src_res, rSC[sb + 1]], writes=[tmp_res])
            vec(lambda e: e.tensor_tensor(P_["sq"], tmp_ap, tmp_ap, ALU.mult),
                reads=[tmp_res], writes=P_["sqr"])
            vec(lambda e: e.reduce_sum(scal[:, sb + 2:sb + 3], P_["sq"], AX.X),
                reads=P_["sqr"], writes=[rSC[sb + 2]], narrow=True)
            vec(lambda e: e.tensor_scalar(scal[:, sb + 3:sb + 4],
                                          scal[:, sb + 2:sb + 3], 1.0 / C,
                                          1e-5, ALU.mult, ALU.add),
                reads=[rSC[sb + 2]], writes=[rSC[sb + 3]], narrow=True)
            act(lambda e: e.activation(scal[:, sb + 4:sb + 5],
                                       scal[:, sb + 3:sb + 4], AF.Sqrt),
                reads=[rSC[sb + 3]], writes=[rSC[sb + 4]], narrow=True)
            vec(lambda e: e.reciprocal(scal[:, sb + 5:sb + 6],
                                       scal[:, sb + 4:sb + 5]),
                reads=[rSC[sb + 4]], writes=[rSC[sb + 5]], narrow=True)
            vec(lambda e: e.tensor_scalar(P_["ln"], tmp_ap,
                                          scal[:, sb + 5:sb + 6],
                                          None, ALU.mult),
                reads=[tmp_res, rSC[sb + 5]], writes=P_["lnr"])

        LN_SETS = [
            {"sb": 40, "sq": sqb[:, 0:1024], "sqr": [rSQ],
             "ln": sqb[:, 1024:2048], "lnr": [rSQ]},
            {"sb": 50, "sq": uzpad[:, 0:1024],
             "sqr": [rUZ[0][0], rUZ[0][1], rUZ[1][0], rUZ[1][1]],
             "ln": uzpad[:, 1024:2048],
             "lnr": [rUZ[0][2], rUZ[0][3], rUZ[1][2], rUZ[1][3]]},
        ]

        def ln1_qkv(b):
            for ti in range(NTIL):
                xs = 2 + ti % 2
                par = ti % 2
                tmp, rtmp = (FS[4], rFS[4]) if par == 0 else (FS[5], rFS[5])
                lnb, lnres = LN_SETS[par]["ln"], LN_SETS[par]["lnr"]
                dma(FS[xs], x_in[b, ti * 128:(ti + 1) * 128, :], writes=[rFS[xs]])
                layernorm(rFS[xs], FS[xs], tmp, rtmp, par)
                for g in range(NCH):
                    pp = g % 2
                    pe(lambda e, g=g, pp=pp, lnb=lnb: e.transpose(
                        pT[pp][:, 0:128], lnb[:, g * 128:(g + 1) * 128],
                        identb[:, :]),
                       reads=lnres + [rId], writes=[rPTb[pp]])
                    act(lambda e, b=b, g=g, ti=ti, pp=pp: e.activation(
                        arD[:, g * T + ti * 128:g * T + (ti + 1) * 128],
                        pT[pp][:, 0:128], AF.Identity,
                        bias=mcol(b, 0, g), scale=mcol(b, 1, g)),
                        reads=[rPTb[pp], rMod], writes=[rD_])
            for oc in range(16):
                bsl = oc % 2
                dma(FS[bsl].rearrange("p (g m) -> p g m", m=128),
                    w_attn[:, oc * 128:(oc + 1) * 128]
                    .rearrange("(g p) m -> p g m", p=128), writes=[rFS[bsl]])
                if oc >= 8:
                    vec(lambda e, b2=bsl: e.tensor_scalar(
                        wcast[:, b2 * 1024:(b2 + 1) * 1024], FS[b2], scal[:, 0:1],
                        None, ALU.mult),
                        reads=[rFS[bsl], rSC[0]], writes=[rWCh[bsl]])
                else:
                    vec(lambda e, b2=bsl: e.tensor_copy(
                        wcast[:, b2 * 1024:(b2 + 1) * 1024], FS[b2]),
                        reads=[rFS[bsl]], writes=[rWCh[bsl]])
                for st in range(NST):
                    sp2 = st % 4
                    for g in range(NCH):
                        pe(lambda e, b2=bsl, g=g, st=st, sp2=sp2: e.matmul(
                            pS[sp2][:, :],
                            wcast[:, b2 * 1024 + g * 128:b2 * 1024 + (g + 1) * 128],
                            arD[:, g * T + st * 512:g * T + (st + 1) * 512],
                            start=(g == 0), stop=(g == NCH - 1)),
                           reads=[rWCh[bsl], rD_],
                           writes=([rPSb[sp2]] if g in (0, NCH - 1) else []))
                    dbuf, dres = (arA, rA_) if oc < 8 else (arB, rB_)
                    og = oc % 8
                    act(lambda e, d=dbuf, og=og, st=st, sp2=sp2, oc=oc: e.activation(
                        d[:, og * T + st * 512:og * T + (st + 1) * 512],
                        pS[sp2][:, :], AF.Identity, bias=battn[:, oc:oc + 1], scale=1.0),
                        reads=[rPSb[sp2], rBat], writes=[dres])
            for vh in range(2):
                for g in range(NCH):
                    bsl = g % 2
                    dma(FS[bsl][:, 0:512], w_attn[g * 128:(g + 1) * 128,
                                                  2048 + vh * 512:2560 + vh * 512],
                        writes=[rFS[bsl]])
                    vec(lambda e, b2=bsl, g=g: e.tensor_copy(
                        wpj[:, g * 512:(g + 1) * 512], FS[b2][:, 0:512]),
                        reads=[rFS[bsl]], writes=[rWPJ])
                for tb in range(NTIL):
                    sp2 = tb % 4
                    for g in range(NCH):
                        pe(lambda e, g=g, tb=tb, sp2=sp2: e.matmul(
                            pS[sp2][:, :], arD[:, g * T + tb * 128:g * T + (tb + 1) * 128],
                            wpj[:, g * 512:(g + 1) * 512], start=(g == 0), stop=False),
                           reads=[rWPJ, rD_], writes=([rPSb[sp2]] if g == 0 else []))
                    pe(lambda e, vh=vh, sp2=sp2: e.matmul(
                        pS[sp2][:, :], padc[:, 0:128],
                        bvrow[0:128, vh * 512:(vh + 1) * 512], start=False, stop=True),
                       reads=[rPadc, rBv], writes=[rPSb[sp2]])
                    act(lambda e, tb=tb, vh=vh, sp2=sp2: e.copy(
                        arC[:, tb * 1024 + vh * 512:tb * 1024 + (vh + 1) * 512],
                        pS[sp2][:, :]), reads=[rPSb[sp2]], writes=[rC_])

        def attention(b):
            for g in range(NCH):
                bsl = g % 2
                dma(FS[bsl], w_proj[g * 128:(g + 1) * 128, :], writes=[rFS[bsl]])
                vec(lambda e, b2=bsl, g=g: e.tensor_copy(
                    wpj[:, g * 1024:(g + 1) * 1024], FS[b2]),
                    reads=[rFS[bsl]], writes=[rWPJ])

            pending_epi = [None]
            for h in range(H):
                hb = (h % 2) * 64
                hc = h // 2
                par = h % 2
                if par == 0:
                    act(lambda e, hc=hc: e.activation(
                        sqb[0:128, 0:2048], arB[0:128, hc * T:(hc + 1) * T],
                        AF.Square), reads=[rB_], writes=[rSQ])
                    for qc in range(4):
                        pe(lambda e, qc=qc: e.matmul(
                            pR[0:1, 0:512], padc[:, 256:257],
                            sqb[0:128, qc * 512:(qc + 1) * 512], start=True, stop=True,
                            tile_position=(0, 0), skip_group_check=True),
                           reads=[rPadc, rSQ], writes=[rPR])
                        pe(lambda e, qc=qc: e.matmul(
                            pR[64:65, 0:512], padc[:, 257:258],
                            sqb[0:128, qc * 512:(qc + 1) * 512], start=True, stop=True,
                            tile_position=(0, 64), skip_group_check=True),
                           reads=[rPadc, rSQ], writes=[rPR])
                        vec(lambda e, qc=qc: e.reduce_max(
                            srow[0:1, 4 + qc:5 + qc], pR[0:1, 0:512], AX.X),
                            reads=[rPR], writes=[rSR], narrow=True)
                        vec(lambda e, qc=qc: e.reduce_max(
                            srow[64:65, 4 + qc:5 + qc], pR[64:65, 0:512], AX.X),
                            reads=[rPR], writes=[rSR], narrow=True)
                    vec(lambda e: e.tensor_reduce(srow[0:1, 3:4], srow[0:1, 4:8], AX.X,
                                                  ALU.max),
                        reads=[rSR], writes=[rSR], narrow=True)
                    vec(lambda e: e.tensor_reduce(srow[64:65, 3:4], srow[64:65, 4:8],
                                                  AX.X, ALU.max),
                        reads=[rSR], writes=[rSR], narrow=True)
                    act(lambda e: e.activation(srow[0:1, 3:4], srow[0:1, 3:4], AF.Sqrt),
                        reads=[rSR], writes=[rSR], narrow=True)
                    act(lambda e: e.activation(srow[64:65, 3:4], srow[64:65, 3:4],
                                               AF.Sqrt),
                        reads=[rSR], writes=[rSR], narrow=True)
                    vec(lambda e: e.tensor_scalar(srow[0:1, 3:4], srow[0:1, 3:4], -1.0,
                                                  None, ALU.mult),
                        reads=[rSR], writes=[rSR], narrow=True)
                    vec(lambda e: e.tensor_scalar(srow[64:65, 3:4], srow[64:65, 3:4],
                                                  -1.0, None, ALU.mult),
                        reads=[rSR], writes=[rSR], narrow=True)
                    act(lambda e, hc=hc: e.activation(
                        sqb[0:128, 0:2048], arA[0:128, hc * T:(hc + 1) * T],
                        AF.Square), reads=[rA_], writes=[rSQ])

                for qc in range(4):
                    pe(lambda e, par=par, hb=hb, qc=qc: e.matmul(
                        pR[hb:hb + 1, 0:512], padc[:, 256 + par:257 + par],
                        sqb[0:128, qc * 512:(qc + 1) * 512], start=True, stop=True,
                        tile_position=(0, hb), skip_group_check=True),
                       reads=[rPadc, rSQ], writes=[rPR])
                    act(lambda e, hb=hb: e.activation(
                        fscr[hb:hb + 1, 5120:5632], pR[hb:hb + 1, 0:512], AF.Sqrt),
                        reads=[rPR], writes=[rFS[5]], narrow=True)
                    dma(uzpad[hb:hb + 1, qc * 512:(qc + 1) * 512],
                        c_alF[h:h + 1, qc * 512:(qc + 1) * 512],
                        writes=[rUZ[par][qc]])
                    vec(lambda e, par=par, hb=hb, qc=qc: e.scalar_tensor_tensor(
                        uzpad[hb:hb + 1, qc * 512:(qc + 1) * 512],
                        fscr[hb:hb + 1, 5120:5632], srow[hb:hb + 1, 3:4],
                        uzpad[hb:hb + 1, qc * 512:(qc + 1) * 512],
                        ALU.mult, ALU.subtract),
                        reads=[rFS[5], rSR, rUZ[par][qc]], writes=[rUZ[par][qc]],
                        narrow=True)

                yrow = hb
                sumrow = 64 - hb

                def emit_fill(q2, par=par, hb=hb, hc=hc):
                    qo2 = par * 1024 + (q2 % 2) * 512
                    vec(lambda e, q2=q2, qo2=qo2: e.tensor_copy(
                        qpad[hb:hb + 64, qo2:qo2 + 512],
                        arA[hb:hb + 64, hc * T + q2 * 512:hc * T + (q2 + 1) * 512]),
                        reads=[rA_], writes=[rQP[par * 2 + q2 % 2]])

                emit_fill(0)
                for qs in range(NST):
                    nk = qs * 4 + 4
                    yb = (h * NST + qs) % 2
                    qoff = par * 1024 + (qs % 2) * 512
                    qres = rQP[par * 2 + qs % 2]

                    def pslot(pp):
                        if pp < 4:
                            return spst[:, pp * 512:(pp + 1) * 512], rPT[pp]
                        return wcast[:, 0:512], rPT[4]

                    def emit_s(j, qs=qs, qoff=qoff, qres=qres):
                        co = max(0, j * 128 - qs * 512)
                        N = 512 - co
                        sp2 = j % 5
                        pe(lambda e, hc=hc, j=j, co=co, N=N, sp2=sp2, qoff=qoff:
                           e.matmul(pS[sp2][:, co:co + N],
                                    arB[0:128,
                                        hc * T + j * 128:hc * T + (j + 1) * 128],
                                    qpad[0:128, qoff + co:qoff + co + N],
                                    start=True, stop=False),
                           reads=[qres, rB_], writes=[rPSb[sp2]])
                        pe(lambda e, par=par, j=j, qs=qs, co=co, N=N, sp2=sp2:
                           e.matmul(pS[sp2][:, co:co + N],
                                    padc[:, par * 128:par * 128 + 128],
                                    uzpad[0:128, qs * 512 + co:qs * 512 + co + N],
                                    start=False, stop=True),
                           reads=[rPadc, rUZ[par][qs]], writes=[rPSb[sp2]])
                        if j * 128 == qs * 512 + co:
                            vec(lambda e, co=co, sp2=sp2: e.tensor_tensor(
                                pS[sp2][:, co:co + 128], pS[sp2][:, co:co + 128],
                                trimask[:, :], ALU.min),
                                reads=[rTri], writes=[rPSb[sp2]])
                        pb, pr = pslot(j % 5)
                        act(lambda e, h=h, j=j, co=co, N=N, sp2=sp2, pb=pb:
                            e.activation(
                                pb[:, 0:N], pS[sp2][:, co:co + N],
                                AF.Exp, bias=alK[:, h * 16 + j:h * 16 + j + 1]),
                            reads=[rPSb[sp2], rALK], writes=[pr])

                    def emit_pv(j, qs=qs, yb=yb, nk=nk):
                        co = max(0, j * 128 - qs * 512)
                        N = 512 - co
                        pb, pr = pslot(j % 5)
                        pe(lambda e, h=h, j=j, co=co, N=N, yb=yb, yrow=yrow, pb=pb,
                           nk=nk: e.matmul(
                               pY[yb][yrow:yrow + 64, co:co + N],
                               arC[:, j * 1024 + h * 64:j * 1024 + (h + 1) * 64],
                               pb[:, 0:N],
                               start=(j == 0), stop=(j == nk - 1),
                               tile_position=(0, yrow), skip_group_check=True),
                           reads=[rC_, pr], writes=[rPYb[yb]])
                        pe(lambda e, j=j, co=co, N=N, yb=yb, sumrow=sumrow, pb=pb,
                           nk=nk: e.matmul(
                               pY[yb][sumrow:sumrow + 1, co:co + N],
                               onescol[:, 1:2], pb[:, 0:N],
                               start=(j == 0), stop=(j == nk - 1),
                               tile_position=(0, sumrow), skip_group_check=True),
                           reads=[rOnes, pr], writes=[rPYb[yb]])

                    DEPTH = int(os.environ.get("KERNEL_DEPTH", "4"))
                    for j0 in range(min(DEPTH, nk)):
                        emit_s(j0)
                    if qs + 1 < NST:
                        emit_fill(qs + 1)
                    for j in range(nk):
                        if j + DEPTH < nk:
                            emit_s(j + DEPTH)
                        emit_pv(j)
                        if j == 1 and pending_epi[0] is not None:
                            pending_epi[0]()
                            pending_epi[0] = None
                    if pending_epi[0] is not None:
                        pending_epi[0]()
                        pending_epi[0] = None

                    def epilogue(yb=yb, sumrow=sumrow, yrow=yrow, hc=hc, qs=qs):
                        vec(lambda e, yb=yb, sumrow=sumrow: e.reciprocal(
                            rrow[sumrow:sumrow + 1, :],
                            pY[yb][sumrow:sumrow + 1, :]),
                            reads=[rPYb[yb]], writes=[rRR], narrow=True)
                        pe(lambda e, sumrow=sumrow, yrow=yrow: e.matmul(
                            pR[yrow:yrow + 64, :],
                            padc[:, (sumrow * 2):(sumrow * 2) + 64],
                            rrow[0:128, :], start=True, stop=True,
                            tile_position=(0, yrow), skip_group_check=True),
                           reads=[rPadc, rRR], writes=[rPR])
                        act(lambda e, yrow=yrow: e.copy(
                            fscr2[yrow:yrow + 64, 512:1024], pR[yrow:yrow + 64, :]),
                            reads=[rPR], writes=[rF2])
                        vec(lambda e, hc=hc, qs=qs, yb=yb, yrow=yrow: e.tensor_tensor(
                            arD[yrow:yrow + 64, hc * T + qs * 512:hc * T + (qs + 1) * 512],
                            pY[yb][yrow:yrow + 64, :], fscr2[yrow:yrow + 64, 512:1024],
                            ALU.mult), reads=[rPYb[yb], rF2], writes=[rD_])

                    if os.environ.get("KERNEL_NO_DEFER"):
                        epilogue()
                    else:
                        pending_epi[0] = epilogue

            if pending_epi[0] is not None:
                pending_epi[0]()
                pending_epi[0] = None

            for qs in range(NST):
                for oc in range(NCH):
                    sp2 = oc % 3
                    for g in range(NCH):
                        pe(lambda e, g=g, oc=oc, qs=qs, sp2=sp2: e.matmul(
                            pS[sp2][:, :],
                            wpj[:, g * 1024 + oc * 128:g * 1024 + (oc + 1) * 128],
                            arD[:, g * T + qs * 512:g * T + (qs + 1) * 512],
                            start=(g == 0), stop=(g == NCH - 1)),
                           reads=[rWPJ, rD_],
                           writes=([rPSb[sp2]] if g in (0, NCH - 1) else []))
                    tgt, trs = (wcast, [rWCh[0], rWCh[1]]) if oc < 4 else (spst, [rSP])
                    to = (oc % 4) * 512
                    act(lambda e, b=b, oc=oc, tgt=tgt, to=to, sp2=sp2: e.activation(
                        tgt[:, to:to + 512], pS[sp2][:, :], AF.Identity,
                        scale=mcol(b, 2, oc),
                        bias=gabv[:, b * 8 + oc:b * 8 + oc + 1]),
                        reads=[rPSb[sp2], rMod, rGab], writes=trs)
                for tb in range(4):
                    ti = qs * 4 + tb
                    xs = 2 + tb % 2
                    par = tb % 2
                    acc, racc = (FS[4], rFS[4]) if par == 0 else (FS[1], rFS[1])
                    tmp, rtmp = (FS[5], rFS[5]) if par == 0 else (FS[0], rFS[0])
                    lnb, lnres = LN_SETS[par]["ln"], LN_SETS[par]["lnr"]
                    h2b, h2res = LN_SETS[par]["sq"], LN_SETS[par]["sqr"]
                    mc = 46 + par * 10
                    dma(FS[xs], x_in[b, ti * 128:(ti + 1) * 128, :], writes=[rFS[xs]])
                    for oc in range(NCH):
                        tgt, trs = (wcast, [rWCh[0], rWCh[1]]) if oc < 4 else (spst, [rSP])
                        to = (oc % 4) * 512
                        pp = oc % 2
                        pe(lambda e, tgt=tgt, to=to, tb=tb, pp=pp: e.transpose(
                            pT[pp][:, 0:128], tgt[:, to + tb * 128:to + (tb + 1) * 128],
                            identb[:, :]),
                           reads=trs + [rId], writes=[rPTb[pp]])
                        vec(lambda e, oc=oc, xs=xs, pp=pp, acc=acc: e.tensor_tensor(
                            acc[:, oc * 128:(oc + 1) * 128], pT[pp][:, 0:128],
                            FS[xs][:, oc * 128:(oc + 1) * 128], ALU.add),
                            reads=[rPTb[pp], rFS[xs]], writes=[racc])
                    dma(x1sp[b, ti * 128:(ti + 1) * 128, :], acc, q="pool",
                        reads=[racc], writes=[rX1sp[b]])
                    layernorm(racc, acc, tmp, rtmp, par)
                    for g in range(NCH):
                        pp = g % 2
                        pe(lambda e, g=g, pp=pp, lnb=lnb: e.transpose(
                            pT[pp][:, 0:128], lnb[:, g * 128:(g + 1) * 128],
                            identb[:, :]),
                           reads=lnres + [rId], writes=[rPTb[pp]])
                        act(lambda e, b=b, g=g, pp=pp, h2b=h2b: e.activation(
                            h2b[:, g * 128:(g + 1) * 128], pT[pp][:, 0:128],
                            AF.Identity,
                            bias=mcol(b, 3, g), scale=mcol(b, 4, g)),
                            reads=[rPTb[pp], rMod], writes=h2res)
                    dma(h2sp[b, :, :, ti * 128:(ti + 1) * 128]
                        .rearrange("g p t -> p g t"),
                        h2b.rearrange("p (g t) -> p g t", t=128),
                        q="pool", reads=h2res, writes=[rH2sp[b]])
                    vec(lambda e, h2b=h2b, mc=mc: e.tensor_reduce(
                        scal[:, mc:mc + 1], h2b, AX.X, ALU.max,
                        apply_absolute_value=True),
                        reads=h2res, writes=[rSC[mc]], narrow=True)
                    vec(lambda e, b=b, mc=mc: e.tensor_tensor(
                        mxacc[:, b:b + 1], mxacc[:, b:b + 1], scal[:, mc:mc + 1],
                        ALU.max),
                        reads=[rSC[mc]], writes=[rMX[b]], narrow=True)

        def scale_pair(mxres, mx_ap, colm, coli):
            pe(lambda e: e.transpose(pR[0:1, 0:128], mx_ap, identf[:, :]),
               reads=[mxres, rId], writes=[rPR], narrow=True)
            vec(lambda e: e.reduce_max(srow[0:1, 8:9], pR[0:1, 0:128], AX.X),
                reads=[rPR], writes=[rSR], narrow=True)
            vec(lambda e: e.tensor_scalar(srow[0:1, 9:10], srow[0:1, 8:9],
                                          1.0 / 32767.0, 1e-12, ALU.mult, ALU.max),
                reads=[rSR], writes=[rSR], narrow=True)
            vec(lambda e: e.reciprocal(srow[0:1, 10:11], srow[0:1, 9:10]),
                reads=[rSR], writes=[rSR], narrow=True)
            bcast(srow[0:1, 9:10], colm, [rSR])
            bcast(srow[0:1, 10:11], coli, [rSR])

        def dqsl(w, g, oc):
            off = w * (NCH * FF) + g * FF + oc * 128
            if off < 16384:
                return arA[:, off:off + 128]
            if off < 32768:
                return arB[:, off - 16384:off - 16384 + 128]
            return arC[:, off - 32768:off - 32768 + 128]

        def lpbq_upgate(raw, w, oc):
            bsl = oc % 2
            dma(FS[bsl], raw[oc * 128:(oc + 1) * 128, :], writes=[rFS[bsl]])
            vec(lambda e, b2=bsl: e.tensor_reduce(
                scal[:, 32:64].rearrange("p (n w) -> p n w", w=1),
                FS[b2].rearrange("p (n w) -> p n w", w=32), AX.X, ALU.max,
                apply_absolute_value=True),
                reads=[rFS[bsl]], writes=[rSC[32]], narrow=True)
            vec(lambda e: e.tensor_scalar(scal[:, 32:64], scal[:, 32:64], 1.0 / 7.0,
                                          1e-12, ALU.mult, ALU.max),
                reads=[rSC[32]], writes=[rSC[32]], narrow=True)
            vec(lambda e: e.reciprocal(fscr2[:, 0:32], scal[:, 32:64]),
                reads=[rSC[32]], writes=[rF2], narrow=True)
            vec(lambda e, b2=bsl: e.tensor_tensor(
                iscr[:, 0:1024].rearrange("p (n w) -> p n w", w=32),
                FS[b2].rearrange("p (n w) -> p n w", w=32),
                fscr2[:, 0:32].unsqueeze(2).broadcast_to([128, 32, 32]), ALU.mult),
                reads=[rFS[bsl], rF2], writes=[rIS])
            vec(lambda e: e.tensor_tensor(
                sqb[:, 0:1024].rearrange("p (n w) -> p n w", w=32),
                iscr[:, 0:1024].rearrange("p (n w) -> p n w", w=32),
                scal[:, 32:64].unsqueeze(2).broadcast_to([128, 32, 32]), ALU.mult),
                reads=[rIS, rSC[32]], writes=[rSQ])
            for g in range(NCH):
                pp = g % 2
                pe(lambda e, g=g, pp=pp: e.transpose(
                    pT[pp][:, 0:128], sqb[:, g * 128:(g + 1) * 128], identb[:, :]),
                   reads=[rSQ, rId], writes=[rPTb[pp]])
                act(lambda e, w=w, g=g, oc=oc, pp=pp: e.copy(
                    dqsl(w, g, oc), pT[pp][:, 0:128]),
                    reads=[rPTb[pp]], writes=[rDQ])

        def ffn_A_load(b, st):
            xoff = (st % 3) * 4096
            rX = rXp[st % 3]
            dma(arD[:, xoff:xoff + 4096].rearrange("p (g t) -> p g t", t=512),
                h2sp[b, :, :, st * 512:(st + 1) * 512].rearrange("g p t -> p g t"),
                reads=[rH2sp[b]], writes=[rX])
            for half in range(4):
                ho = half * 1024
                vec(lambda e, xoff=xoff, ho=ho, rX=rX: e.tensor_scalar(
                    iscr[:, :], arD[:, xoff + ho:xoff + ho + 1024],
                    scal[:, 17:18], None, ALU.mult),
                    reads=[rX, rSC[17]], writes=[rIS])
                vec(lambda e, xoff=xoff, ho=ho, rX=rX: e.tensor_scalar(
                    arD[:, xoff + ho:xoff + ho + 1024], iscr[:, :],
                    scal[:, 16:17], None, ALU.mult),
                    reads=[rIS, rSC[16]], writes=[rX])

        def ffn_A_mm(b, w, st):
            xoff = (st % 3) * 4096
            rX = rXp[st % 3]
            dsp, rdsp, mcol = ((upsp, rUpsp, 2), (gasp, rGasp, 4))[w]
            for oc in range(FCH):
                sp2 = oc % 4
                for g in range(NCH):
                    pe(lambda e, w=w, g=g, oc=oc, xoff=xoff, sp2=sp2: e.matmul(
                        pS[sp2][:, :], dqsl(w, g, oc),
                        arD[:, xoff + g * 512:xoff + (g + 1) * 512],
                        start=(g == 0), stop=(g == NCH - 1)),
                       reads=[rDQ, rX],
                       writes=([rPSb[sp2]] if g in (0, NCH - 1) else []))
                sl = oc % 8
                gr = (oc // 4) % 2
                act(lambda e, sl=sl, sp2=sp2: e.copy(
                    arD[:, 12288 + sl * 512:12800 + sl * 512], pS[sp2][:, :]),
                    reads=[rPSb[sp2]], writes=[rASt[gr]])
                if sl % 4 == 3:
                    go = 12288 + (sl - 3) * 512
                    dma(dsp[b, oc - 3:oc + 1, :, st * 512:(st + 1) * 512]
                        .rearrange("o p t -> p o t"),
                        arD[:, go:go + 2048].rearrange("p (o t) -> p o t", t=512),
                        q="pool", reads=[rASt[gr]], writes=[rdsp[b]])
                    vec(lambda e, go=go: e.tensor_reduce(
                        scal[:, 46:47], arD[:, go:go + 2048], AX.X, ALU.max,
                        apply_absolute_value=True),
                        reads=[rASt[gr]], writes=[rSC[46]], narrow=True)
                    vec(lambda e, b=b, mcol=mcol: e.tensor_tensor(
                        mxacc[:, mcol + b:mcol + b + 1],
                        mxacc[:, mcol + b:mcol + b + 1],
                        scal[:, 46:47], ALU.max),
                        reads=[rSC[46]], writes=[rMX[mcol + b]], narrow=True)

        uzI = uzpad[:, :].bitcast(I32)
        spI = spst[:, :].bitcast(I32)

        def ibuf(h):
            q = h % 4
            src = uzI if q < 2 else spI
            return src[:, (q % 2) * 512:(q % 2) * 512 + 512]
        f2bf = fscr2[:, :].bitcast(BF16)[:, 1024:2048]
        f2bf2 = fscr2[:, :].bitcast(BF16)[:, 0:1024]

        def spB(b):
            scale_pair(rMX[2 + b], mxacc[:, 2 + b:3 + b], 18, 19)
            scale_pair(rMX[4 + b], mxacc[:, 4 + b:5 + b], 20, 21)

        def ffn_B(b, ocs=None):
            ocs = list(range(FCH)) if ocs is None else list(ocs)
            segs = [(oc, sg) for oc in ocs for sg in range(4)]
            n = len(segs)

            def b_load(oc, hf):
                dma(wcast[:, hf * 1024:(hf + 1) * 1024],
                    gasp[b, oc, :, hf * 1024:(hf + 1) * 1024],
                    reads=[rGasp[b]], writes=[rWCh[hf]])

            def b_int(g):
                oc, sg = segs[g]
                so = sg * 512
                ui, ru = ibuf(g), rBIh[g % 4]
                ex = [rSP] if g % 4 >= 2 else []
                vec(lambda e, so=so, ui=ui: e.tensor_scalar(
                    ui, wcast[:, so:so + 512], scal[:, 21:22], None,
                    ALU.mult), reads=[rWCh[sg // 2], rSC[21]] + ex, writes=[ru])

            def b_act(g):
                oc, sg = segs[g]
                so = sg * 512
                hf = sg // 2
                ui, ru = ibuf(g), rBIh[g % 4]
                act(lambda e, so=so, ui=ui: e.activation(
                    fscr[:, 4096 + so:4608 + so], ui, AF.Sigmoid,
                    scale=scal[:, 20:21]),
                    reads=[ru, rSC[20]], writes=[rFS[4 + hf]])
                act(lambda e, so=so, ui=ui: e.activation(
                    sqb[:, so:so + 512], ui, AF.Identity,
                    scale=scal[:, 20:21]),
                    reads=[ru, rSC[20]], writes=[rSQ])

            def b_fq(g):
                oc, sg = segs[g]
                so = sg * 512
                hf = sg // 2
                ui, ru = ibuf(g), rBIh[g % 4]
                vec(lambda e, so=so, ui=ui: e.tensor_scalar(
                    ui, fscr[:, 4096 + so:4608 + so], 65536.0, None,
                    ALU.mult), reads=[rFS[4 + hf]], writes=[ru])
                vec(lambda e, so=so, ui=ui: e.tensor_scalar(
                    qpad[:, so:so + 512], ui, 65535.0,
                    1.0 / 65536.0, ALU.min, ALU.mult),
                    reads=[ru], writes=[rBQ])

            def b_gs(oc, hf):
                ho = hf * 1024
                fb = (f2bf2, f2bf)[hf]
                vec(lambda e, ho=ho, fb=fb: e.tensor_tensor(
                    fb[:, :], sqb[:, ho:ho + 1024], qpad[:, ho:ho + 1024],
                    ALU.mult), reads=[rSQ, rBQ], writes=[rF2h[hf]])
                dma(gssp[b, oc, :, ho:ho + 1024], fb[:, :], q="pool",
                    reads=[rF2h[hf]], writes=[rGssp[b]])
                vec(lambda e, b=b, fb=fb: e.tensor_reduce(
                    mxacc[:, 10 + b:11 + b], fb[:, :], AX.X,
                    ALU.max, apply_absolute_value=True),
                    reads=[rF2h[hf]], writes=[rMX[10 + b]], narrow=True)
                vec(lambda e, b=b: e.tensor_tensor(
                    mxacc[:, 6 + b:7 + b], mxacc[:, 6 + b:7 + b],
                    mxacc[:, 10 + b:11 + b],
                    ALU.max), reads=[rMX[10 + b]], writes=[rMX[6 + b]], narrow=True)

            b_load(segs[0][0], 0)
            b_load(segs[0][0], 1)
            for g in range(n + 2):
                if g < n:
                    b_int(g)
                    oc, sg = segs[g]
                    if sg == 1 and g + 3 < n:
                        b_load(segs[g + 3][0], 0)
                    if sg == 3 and g + 1 < n:
                        b_load(segs[g + 1][0], 1)
                if g >= 1 and g - 1 < n:
                    b_act(g - 1)
                if g >= 2 and g - 2 < n:
                    b_fq(g - 2)
                    oc2, sg2 = segs[g - 2]
                    if sg2 == 1:
                        b_gs(oc2, 0)
                    elif sg2 == 3:
                        b_gs(oc2, 1)

        def spC(b):
            scale_pair(rMX[6 + b], mxacc[:, 6 + b:7 + b], 22, 23)

        fsbf = fscr[:, :].bitcast(BF16)[:, 8192:12288]
        rCG = [R(), R()]
        ALL_RES.extend(rCG)

        def ffn_C(b, ocs=None):
            ocs = list(range(FCH)) if ocs is None else list(ocs)

            def c_gload(i):
                dma(fsbf[:, (i % 2) * 2048:(i % 2) * 2048 + 2048],
                    gssp[b, ocs[i], :, :],
                    reads=[rGssp[b]], writes=[rCG[i % 2], rFS[4], rFS[5]])

            c_gload(0)
            for i, oc in enumerate(ocs):
                gb = fsbf[:, (i % 2) * 2048:(i % 2) * 2048 + 2048]
                rg = rCG[i % 2]
                for hf in range(2):
                    fb = (f2bf2, f2bf)[hf]
                    dma(fb[:, :], upsp[b, oc, :, hf * 1024:(hf + 1) * 1024],
                        reads=[rUpsp[b]], writes=[rF2h[hf], rF2])
                if i + 1 < len(ocs):
                    c_gload(i + 1)

                def cbuf(sg):
                    q = sg % 3
                    if q < 2:
                        return uzI[:, q * 512:q * 512 + 512], rBIh[q]
                    return ibufC[:, :], rBIh[2]

                def c_int1(sg):
                    so = sg * 512
                    ui, ru = cbuf(sg)
                    vec(lambda e, so=so, ui=ui, gb=gb: e.tensor_scalar(
                        ui, gb[:, so:so + 512], scal[:, 23:24], None,
                        ALU.mult), reads=[rg, rSC[23]],
                        writes=[ru])

                def c_act1(sg):
                    so = sg * 512
                    ui, ru = cbuf(sg)
                    vec(lambda e, so=so, ui=ui: e.tensor_scalar(
                        spst[:, so:so + 512], ui, scal[:, 22:23], None,
                        ALU.mult),
                        reads=[ru, rSC[22]], writes=[rSP])

                for sg in range(4):
                    c_int1(sg)
                    if sg >= 2:
                        c_act1(sg - 2)
                c_act1(2)
                c_act1(3)

                def c_int2(sg):
                    so = sg * 512
                    fb = (f2bf2, f2bf)[sg // 2]
                    ui, ru = cbuf(sg)
                    vec(lambda e, so=so, ui=ui, fb=fb, sg=sg: e.tensor_scalar(
                        ui, fb[:, (sg % 2) * 512:(sg % 2) * 512 + 512],
                        scal[:, 19:20], None,
                        ALU.mult), reads=[rF2h[sg // 2], rSC[19]],
                        writes=[ru])

                def c_act2(sg):
                    so = sg * 512
                    ui, ru = cbuf(sg)
                    vec(lambda e, so=so, ui=ui: e.tensor_scalar(
                        qpad[:, so:so + 512], ui, scal[:, 18:19], None,
                        ALU.mult),
                        reads=[ru, rSC[18]], writes=[rBQ])

                for sg in range(4):
                    c_int2(sg)
                    if sg >= 2:
                        c_act2(sg - 2)
                c_act2(2)
                c_act2(3)
                vec(lambda e: e.tensor_tensor(spst[:, :], spst[:, :], qpad[:, :],
                                              ALU.mult),
                    reads=[rBQ], writes=[rSP])
                dma(osp[b, oc, :, :], spst[:, :], q="pool",
                    reads=[rSP], writes=[rOsp[b]])
                vec(lambda e, b=b: e.tensor_reduce(
                    mxacc[:, 12 + b:13 + b], spst[:, :], AX.X,
                    ALU.max, apply_absolute_value=True),
                    reads=[rSP], writes=[rMX[12 + b]], narrow=True)
                vec(lambda e, b=b: e.tensor_tensor(
                    mxacc[:, 8 + b:9 + b], mxacc[:, 8 + b:9 + b],
                    mxacc[:, 12 + b:13 + b],
                    ALU.max), reads=[rMX[12 + b]], writes=[rMX[8 + b]], narrow=True)

        def dqdl(g, co):
            off = co * FF + g * 128
            if off < 16384:
                return arA[:, off:off + 128]
            return arB[:, off - 16384:off - 16384 + 128]

        def lpbq_down(cos=None):
            for co in (range(NCH) if cos is None else cos):
                for third in range(3):
                    tsl = third % 2
                    dma(FS[tsl], w_down[co * 128:(co + 1) * 128,
                                        third * 1024:(third + 1) * 1024],
                        writes=[rFS[tsl]])
                    vec(lambda e, t2=tsl: e.tensor_reduce(
                        scal[:, 32:64].rearrange("p (n w) -> p n w", w=1),
                        FS[t2].rearrange("p (n w) -> p n w", w=32), AX.X, ALU.max,
                        apply_absolute_value=True),
                        reads=[rFS[tsl]], writes=[rSC[32]], narrow=True)
                    vec(lambda e: e.tensor_scalar(scal[:, 32:64], scal[:, 32:64],
                                                  1.0 / 7.0, 1e-12, ALU.mult, ALU.max),
                        reads=[rSC[32]], writes=[rSC[32]], narrow=True)
                    vec(lambda e: e.reciprocal(fscr2[:, 0:32], scal[:, 32:64]),
                        reads=[rSC[32], rF2h[0], rF2h[1]], writes=[rF2],
                        narrow=True)
                    vec(lambda e, t2=tsl: e.tensor_tensor(
                        iscr[:, 0:1024].rearrange("p (n w) -> p n w", w=32),
                        FS[t2].rearrange("p (n w) -> p n w", w=32),
                        fscr2[:, 0:32].unsqueeze(2).broadcast_to([128, 32, 32]),
                        ALU.mult), reads=[rFS[tsl], rF2], writes=[rIS])
                    vec(lambda e: e.tensor_tensor(
                        f2bf[:, 0:1024].rearrange("p (n w) -> p n w", w=32),
                        iscr[:, 0:1024].rearrange("p (n w) -> p n w", w=32),
                        scal[:, 32:64].unsqueeze(2).broadcast_to([128, 32, 32]),
                        ALU.mult), reads=[rIS, rSC[32]], writes=[rF2])
                    for g8 in range(NCH):
                        g = third * 8 + g8
                        pp = g8 % 2
                        pe(lambda e, g8=g8, pp=pp: e.transpose(
                            pT[pp][:, 0:128], f2bf[:, g8 * 128:(g8 + 1) * 128],
                            identb[:, :]),
                           reads=[rF2, rId], writes=[rPTb[pp]])
                        act(lambda e, g=g, co=co, pp=pp: e.copy(
                            dqdl(g, co), pT[pp][:, 0:128]),
                            reads=[rPTb[pp]], writes=[rDQ])

        out_evs = []

        def spD(b):
            scale_pair(rMX[8 + b], mxacc[:, 8 + b:9 + b], 24, 25)

        def ffn_D(b, sts=None):
            for st in (range(NST) if sts is None else sts):
                for grp in range(6):
                    go = grp * 2048
                    rX = rXp[grp % 3]
                    dma(arD[:, go:go + 2048].rearrange("p (g t) -> p g t", t=512),
                        osp[b, grp * 4:(grp + 1) * 4, :, st * 512:(st + 1) * 512]
                        .rearrange("g p t -> p g t"),
                        reads=[rOsp[b], rASt[0], rASt[1]], writes=[rX])
                    for hf in range(2):
                        ho = go + hf * 1024
                        vec(lambda e, ho=ho, rX=rX: e.tensor_scalar(
                            iscr[:, :], arD[:, ho:ho + 1024], scal[:, 25:26], None,
                            ALU.mult), reads=[rX, rSC[25]], writes=[rIS])
                        vec(lambda e, ho=ho, rX=rX: e.tensor_scalar(
                            arD[:, ho:ho + 1024], iscr[:, :], scal[:, 24:25], None,
                            ALU.mult), reads=[rIS, rSC[24]], writes=[rX])
                for co in range(NCH):
                    sp2 = co % 3
                    for g in range(FCH):
                        pe(lambda e, g=g, co=co, sp2=sp2: e.matmul(
                            pS[sp2][:, :], dqdl(g, co),
                            arD[:, g * 512:(g + 1) * 512],
                            start=(g == 0), stop=(g == FCH - 1)),
                           reads=[rDQ, rXp[0], rXp[1]],
                           writes=([rPSb[sp2]] if g in (0, FCH - 1) else []))
                    do = 12288 + co * 512
                    act(lambda e, b=b, co=co, do=do, sp2=sp2: e.activation(
                        arD[:, do:do + 512], pS[sp2][:, :], AF.Identity,
                        scale=mcol(b, 5, co), bias=0.0),
                        reads=[rPSb[sp2], rMod, rASt[0], rASt[1]],
                        writes=[rDS[co % 2]])
                for tb in range(4):
                    ti = st * 4 + tb
                    xs = 2 + tb % 2
                    dma(FS[xs], x1sp[b, ti * 128:(ti + 1) * 128, :],
                        reads=[rX1sp[b]], writes=[rFS[xs]])
                    for co in range(NCH):
                        do = 12288 + co * 512
                        pp = co % 2
                        pe(lambda e, do=do, tb=tb, pp=pp, co=co: e.transpose(
                            pT[pp][:, 0:128],
                            arD[:, do + tb * 128:do + (tb + 1) * 128],
                            identb[:, :]),
                           reads=[rDS[co % 2], rId], writes=[rPTb[pp]])
                        vec(lambda e, co=co, xs=xs, pp=pp: e.tensor_tensor(
                            FS[0][:, co * 128:(co + 1) * 128], pT[pp][:, 0:128],
                            FS[xs][:, co * 128:(co + 1) * 128], ALU.add),
                            reads=[rPTb[pp], rFS[xs]], writes=[rFS[0]])
                    ev = dma(out[b, ti * 128:(ti + 1) * 128, :], FS[0], q="pool",
                             reads=[rFS[0]])
                    out_evs.append(ev)

        _dbg = os.environ.get("KERNEL_DEBUG_STOP", "")
        for b in range(BPC):
            ln1_qkv(b)
            attention(b)
            s.fence(ALL_RES + rWCh)
            if _dbg == "attn%d" % b:
                break

        if not _dbg:
          for oc in range(FCH):
            lpbq_upgate(w_up, 0, oc)
          for oc in range(FCH):
            lpbq_upgate(w_gate, 1, oc)
          s.fence(ALL_RES + rWCh)

          scale_pair(rMX[0], mxacc[:, 0:1], 16, 17)
          ffn_A_load(0, 0)
          ffn_A_load(0, 1)
          ffn_A_mm(0, 1, 0)
          ffn_A_load(0, 2)
          ffn_A_mm(0, 1, 1)
          ffn_A_load(0, 3)
          ffn_A_mm(0, 1, 2)
          ffn_A_mm(0, 1, 3)
          scale_pair(rMX[4], mxacc[:, 4:5], 20, 21)
          ffn_A_load(0, 0)
          ffn_A_load(0, 1)
          ffn_A_mm(0, 0, 0)
          ffn_A_load(0, 2)
          ffn_B(0, range(0, 6))
          ffn_A_mm(0, 0, 1)
          ffn_A_load(0, 3)
          ffn_B(0, range(6, 12))
          ffn_A_mm(0, 0, 2)
          ffn_B(0, range(12, 18))
          ffn_A_mm(0, 0, 3)
          ffn_B(0, range(18, 24))
          scale_pair(rMX[2], mxacc[:, 2:3], 18, 19)
          scale_pair(rMX[1], mxacc[:, 1:2], 16, 17)
          ffn_A_load(1, 0)
          ffn_A_load(1, 1)
          ffn_A_mm(1, 1, 0)
          ffn_A_load(1, 2)
          spC(0)
          ffn_C(0, range(0, 6))
          ffn_A_mm(1, 1, 1)
          ffn_A_load(1, 3)
          ffn_C(0, range(6, 12))
          ffn_A_mm(1, 1, 2)
          ffn_C(0, range(12, 18))
          ffn_A_mm(1, 1, 3)
          ffn_C(0, range(18, 24))
          scale_pair(rMX[5], mxacc[:, 5:6], 20, 21)
          ffn_A_load(1, 0)
          ffn_A_load(1, 1)
          ffn_A_mm(1, 0, 0)
          ffn_A_load(1, 2)
          ffn_B(1, range(0, 6))
          ffn_A_mm(1, 0, 1)
          ffn_A_load(1, 3)
          ffn_B(1, range(6, 12))
          ffn_A_mm(1, 0, 2)
          ffn_B(1, range(12, 18))
          ffn_A_mm(1, 0, 3)
          ffn_B(1, range(18, 24))
          scale_pair(rMX[3], mxacc[:, 3:4], 18, 19)
          lpbq_down()
          spD(0)
          ffn_D(0, [0])
          spC(1)
          ffn_C(1, range(0, 6))
          ffn_D(0, [1])
          ffn_C(1, range(6, 12))
          ffn_D(0, [2])
          ffn_C(1, range(12, 18))
          ffn_D(0, [3])
          ffn_C(1, range(18, 24))
          spD(1)
          ffn_D(1)

        final_deps = out_evs[-1:]
        s.step("sync", lambda e: e.nop(), extra=[s.last["pool"], s.last["vec"],
                                                 s.last["act"], s.last["pe"]] + final_deps)

        plans = s.finalize()
        sem_handles = {}
        for i, sid in enumerate(s.sem_ids):
            sem_handles[sid] = ctx.enter_context(nc.semaphore("c%d" % i))

        def runner(eng_name):
            def go(e):
                for waits, f, ev in plans[eng_name]:
                    for sem_eng, val in waits:
                        e.wait_ge(sem_handles[sem_eng], val)
                    ins = f(e)
                    if ev.sem is not None:
                        ins.then_inc(sem_handles[ev.sem], ev.amt)
            return go

        with nc.Block() as blk:
            blk.sync(runner("sync"))
            blk.vector(runner("vec"))
            blk.scalar(runner("act"))
            blk.tensor(runner("pe"))
            blk.gpsimd(runner("pool"))
    return nc


_prog_cache = [None]


def kernel(**inputs):
    x = np.ascontiguousarray(np.asarray(inputs["L0_attn_x"], np.float32))
    cv = np.ascontiguousarray(np.asarray(inputs["L0_c"], np.float32))
    full = {
        "w_attn": np.ascontiguousarray(np.asarray(inputs["L0_attn_W_attn"], np.float32)),
        "b_attn": np.ascontiguousarray(np.asarray(inputs["L0_attn_b_attn"], np.float32)),
        "w_proj": np.ascontiguousarray(np.asarray(inputs["L0_attn_W_proj"], np.float32)),
        "b_proj": np.ascontiguousarray(np.asarray(inputs["L0_attn_b_proj"], np.float32)),
        "temp": np.ascontiguousarray(np.asarray(inputs["L0_attn_temperature"], np.float32)),
        "w_gate": np.ascontiguousarray(np.asarray(inputs["L0_ffn_w_gate"], np.float32)),
        "w_up": np.ascontiguousarray(np.asarray(inputs["L0_ffn_w_up"], np.float32)),
        "w_down": np.ascontiguousarray(np.asarray(inputs["L0_ffn_w_down"], np.float32)),
        "wmod": np.ascontiguousarray(np.asarray(inputs["L0_wmod"], np.float32)),
    }
    if _prog_cache[0] is None:
        _prog_cache[0] = build_program()
    nc = _prog_cache[0]
    in_maps = []
    for core in range(8):
        m = dict(full)
        m["x"] = np.ascontiguousarray(x[core * BPC:(core + 1) * BPC])
        m["cvec"] = np.ascontiguousarray(cv[core * BPC:(core + 1) * BPC])
        in_maps.append(m)
    res = run_bass_kernel_spmd(nc, in_maps, core_ids=list(range(8)))
    return np.concatenate([np.asarray(r["out"], np.float32) for r in res.results],
                          axis=0)


# version 24 (final)
# speedup vs baseline: 1.1981x; 1.0014x over previous
import contextlib

import numpy as np
import ml_dtypes

import concourse.bass as bass
import concourse.mybir as mybir
from concourse.bass_utils import run_bass_kernel_spmd

F32 = mybir.dt.float32
BF16 = mybir.dt.bfloat16
I32 = mybir.dt.int32
AF = mybir.ActivationFunctionType
ALU = mybir.AluOpType
AX = mybir.AxisListType

BPC, T, C, H, DH, FF = 2, 2048, 1024, 16, 64, 3072
NCH, FCH, NTIL, NST = 8, 24, 16, 4

ENGINES = ("sync", "vec", "act", "pe", "pool")


class Ev:
    __slots__ = ("eng", "narrow", "amt", "sem", "val", "key")

    def __init__(self, eng, narrow, amt, key=None):
        self.eng, self.narrow, self.amt = eng, narrow, amt
        self.sem = None
        self.val = None
        self.key = key


class Res:
    __slots__ = ("wev", "revs")

    def __init__(self):
        self.wev = None
        self.revs = []


class Sched:
    def __init__(self):
        self.per_eng = {e: [] for e in ENGINES}
        self.last = {e: None for e in ENGINES}
        self.last_comp = {e: None for e in ENGINES}
        self.last_key = {}

    def step(self, eng, f, reads=(), writes=(), extra=(), narrow=False, dma=False,
             key=None):
        deps = []
        for r in reads:
            if r.wev is not None:
                deps.append(r.wev)
        for w in writes:
            deps.extend(w.revs)
            if w.wev is not None:
                deps.append(w.wev)
        for ev0 in extra:
            if ev0 is not None:
                deps.append(ev0)
        if dma and key is None:
            key = id(writes[0]) if writes else "dma_misc"
        ev = Ev(eng, narrow or dma, 16 if dma else 1, key=key)
        self.per_eng[eng].append((f, deps, ev))
        self.last[eng] = ev
        if dma:
            self.last_key[key] = ev
        else:
            self.last_comp[eng] = ev
        for r in reads:
            r.revs.append(ev)
        for w in writes:
            w.wev = ev
            w.revs = []
        return ev

    def fence(self, resources):
        evs = [self.last_comp[e] for e in ENGINES
               if self.last_comp[e] is not None]
        evs += list(self.last_key.values())
        for ev in evs:
            ev.narrow = True
        for r in resources:
            r.wev = None
            r.revs = list(evs)

    def finalize(self):
        needed = set()
        for eng in ENGINES:
            for f, deps, ev in self.per_eng[eng]:
                for d in deps:
                    if d.eng != eng or d.narrow:
                        needed.add(id(d))
        counts = {}
        emit_order = []
        for eng in ENGINES:
            emit_order.extend(self.per_eng[eng])
        for eng in ENGINES:
            for f, deps, ev in self.per_eng[eng]:
                if ev.amt == 16:
                    sem = ("k", ev.key)
                    counts[sem] = counts.get(sem, 0) + 16
                    ev.sem = sem
                    ev.val = counts[sem]
                elif id(ev) in needed:
                    sem = ("e", eng)
                    counts[sem] = counts.get(sem, 0) + 1
                    ev.sem = sem
                    ev.val = counts[sem]
        self.sem_ids = sorted(set(counts.keys()), key=repr)
        out = {}
        for eng in ENGINES:
            lst = []
            hi = {}
            for f, deps, ev in self.per_eng[eng]:
                waits = []
                for d in deps:
                    if d.sem is None or (d.eng == eng and not d.narrow
                                         and d.amt != 16):
                        continue
                    if hi.get(d.sem, -1) >= d.val:
                        continue
                    hi[d.sem] = d.val
                    waits.append((d.sem, d.val))
                lst.append((waits, f, ev))
            out[eng] = lst
        return out


def _alibi_consts():
    slopes = np.array([2.0 ** (-(h + 1) / 2.0) for h in range(H)], np.float32)
    kidx = np.arange(T, dtype=np.float32)
    alF = (slopes[:, None] * kidx[None, :]).astype(ml_dtypes.bfloat16)
    p = np.arange(128, dtype=np.float32)
    alK = np.zeros((128, 256), np.float32)
    for h in range(H):
        for j in range(16):
            alK[:, h * 16 + j] = slopes[h] * (128.0 * j + p)
    pad = np.zeros((128, 272), ml_dtypes.bfloat16)
    pad[0, 0:128] = 1.0
    pad[64, 128:256] = 1.0
    pad[0:64, 256] = 1.0
    pad[64:128, 257] = 1.0
    return np.ascontiguousarray(alF), alK, np.ascontiguousarray(pad)


def build_program():
    import os
    nc = bass.Bass("TRN2", target_bir_lowering=False, detect_race_conditions=False)
    s = Sched()

    x_in = nc.dram_tensor("x", [BPC, T, C], F32, kind="ExternalInput").ap()
    w_attn = nc.dram_tensor("w_attn", [C, 3 * C], F32, kind="ExternalInput").ap()
    b_attn = nc.dram_tensor("b_attn", [3 * C], F32, kind="ExternalInput").ap()
    w_proj = nc.dram_tensor("w_proj", [C, C], F32, kind="ExternalInput").ap()
    b_proj = nc.dram_tensor("b_proj", [C], F32, kind="ExternalInput").ap()
    temp = nc.dram_tensor("temp", [1], F32, kind="ExternalInput").ap()
    w_gate = nc.dram_tensor("w_gate", [FF, C], F32, kind="ExternalInput").ap()
    w_up = nc.dram_tensor("w_up", [FF, C], F32, kind="ExternalInput").ap()
    w_down = nc.dram_tensor("w_down", [C, FF], F32, kind="ExternalInput").ap()
    c_in = nc.dram_tensor("cvec", [BPC, C], F32, kind="ExternalInput").ap()
    wmod = nc.dram_tensor("wmod", [C, 6 * C], F32, kind="ExternalInput").ap()
    out = nc.dram_tensor("out", [BPC, T, C], F32, kind="ExternalOutput").ap()

    x1sp = nc.dram_tensor("x1sp", [BPC, T, C], F32, kind="Internal").ap()
    h2sp = nc.dram_tensor("h2sp", [BPC, NCH, 128, T], BF16, kind="Internal").ap()
    upsp = nc.dram_tensor("upsp", [BPC, FCH, 128, T], BF16, kind="Internal").ap()
    gasp = nc.dram_tensor("gasp", [BPC, FCH, 128, T], BF16, kind="Internal").ap()
    gssp = nc.dram_tensor("gssp", [BPC, FCH, 128, T], BF16, kind="Internal").ap()
    osp = nc.dram_tensor("osp", [BPC, FCH, 128, T], BF16, kind="Internal").ap()

    alF_np, alK_np, pad_np = _alibi_consts()
    c_alF = nc.inline_tensor(alF_np.view(np.uint16), "alibiFc").ap().bitcast(BF16)
    c_alK = nc.inline_tensor(alK_np, "alibiKc").ap()
    c_pad = nc.inline_tensor(pad_np.view(np.uint16), "padonec").ap().bitcast(BF16)
    c_idb = nc.inline_tensor(np.eye(128, dtype=ml_dtypes.bfloat16).view(np.uint16),
                             "identbc").ap().bitcast(BF16)
    c_idf = nc.inline_tensor(np.eye(128, dtype=np.float32), "identfc").ap()
    tri_np = np.where(np.arange(128)[:, None] <= np.arange(128)[None, :],
                      np.float32(1e30), np.float32(-1e30)).astype(np.float32)
    c_tri = nc.inline_tensor(tri_np, "trimaskc").ap()

    ctx = contextlib.ExitStack()
    sbuf = lambda nm, sh, dt=BF16: ctx.enter_context(nc.sbuf_tensor(nm, sh, dt))
    psum = lambda nm, sh, dt=F32: ctx.enter_context(nc.psum_tensor(nm, sh, dt))

    with ctx:
        ctx.enter_context(nc.allow_low_precision(
            "bf16 matmul operands / bf16-rounded normalization by design; "
            "fp32 PSUM accumulation and fp32 statistics throughout"))
        arA = sbuf("arA", [128, 16384])
        arB = sbuf("arB", [128, 16384])
        arC = sbuf("arC", [128, 16384])
        arD = sbuf("arD", [128, 16384])
        wpj = sbuf("wpj", [128, 8192])
        fscr = sbuf("fscr", [128, 6144], F32)
        fscr2 = sbuf("fsc2", [128, 1024], F32)
        sqb = sbuf("sqb", [128, 2048])
        qpad = sbuf("qpad", [128, 2048])
        uzpad = sbuf("uzp", [128, 2048])
        padc = sbuf("padc", [128, 272])
        ibufC = sbuf("ibc", [128, 512], I32)
        alK = sbuf("alK", [128, 256], F32)
        wcast = sbuf("wc", [128, 2048])
        spst = sbuf("spst", [128, 2048])
        iscr = sbuf("iscr", [128, 1024], I32)
        scal = sbuf("scal", [128, 64], F32)
        srow = sbuf("srow", [68, 64], F32)
        modv = sbuf("modv", [128, 96], F32)
        gabv = sbuf("gabv", [128, 16], F32)
        battn = sbuf("batt", [128, 16], F32)
        bprojc = sbuf("bpjc", [128, 8], F32)
        bvrow = sbuf("bvr", [128, 1024])
        onescol = sbuf("onc", [128, 4])
        onesrow = sbuf("onr", [65, 128])
        onesrf = sbuf("onrf", [1, 128], F32)
        identb = sbuf("idb", [128, 128])
        identf = sbuf("idf", [128, 128], F32)
        trimask = sbuf("tri", [128, 128], F32)
        mxacc = sbuf("mxac", [128, 16], F32)
        rrow = sbuf("rrow", [128, 512])
        crow = fscr[0:BPC, 4096:5120]
        siluc = sbuf("silc", [128, 2 * NCH], F32)

        pS = [psum("pS0", [128, 512]), psum("pS1", [128, 512]), psum("pS2", [128, 512])]
        pY = [psum("pY0", [128, 512]), psum("pY1", [128, 512])]
        pR = psum("pR", [128, 512])
        pT = [psum("pT0", [128, 1024], BF16), psum("pT1", [128, 1024], BF16)]
        pS = pS + [pT[0].ap().bitcast(F32), pT[1].ap().bitcast(F32)]

        FS = [fscr[:, i * 1024:(i + 1) * 1024] for i in range(6)]
        sig2k = fscr[:, 4096:6144]
        mview = modv[:, :].rearrange("p (b2 c6 g) -> p b2 c6 g", b2=2, c6=6)

        def mcol(b, comp, g):
            i = b * 48 + comp * 8 + g
            return modv[:, i:i + 1]

        R = Res
        rFS = [R() for _ in range(6)]
        rF2 = R()
        rSQ = R()
        rSP = R()
        rDQ = R()
        rPT = [R(), R(), R(), R(), None]
        rIS = R()
        rSC = [R() for _ in range(64)]
        rSR = R()
        rMX = [R() for _ in range(16)]
        rMod, rGab, rBat, rBpc, rBv, rSil = R(), R(), R(), R(), R(), R()
        rCrow = rFS[4]
        rOnes, rId, rTri = R(), R(), R()
        rQP = [R() for _ in range(4)]
        rUZ = [[R() for _ in range(4)] for _ in range(2)]
        rPadc, rALK = R(), R()
        rA_, rB_, rC_, rD_, rWPJ = R(), R(), R(), R(), R()
        rXp = [R(), R(), R()]
        rRR = R()
        rPSb = [R(), R(), R()]
        rPYb = [R(), R()]
        rPR = R()
        rPTb = [R(), R()]
        rPSb = rPSb + [rPTb[0], rPTb[1]]
        rBI, rBQ = R(), R()
        rBIh = [R(), R(), R(), R()]
        rF2h = [R(), R()]
        rDS = [R(), R()]
        rASt = [R(), R()]
        rH2sp = [R(), R()]
        rX1sp = [R(), R()]
        rUpsp = [R(), R()]
        rGasp = [R(), R()]
        rGssp = [R(), R()]
        rOsp = [R(), R()]

        ALL_RES = [v for v in list(locals().values()) if isinstance(v, Res)]
        for lst in (rFS, rPT, rSC, rMX, rPSb, rPYb, rPTb, rH2sp, rX1sp, rUpsp,
                    rGasp, rGssp, rOsp, rQP, rUZ[0], rUZ[1], rDS,
                    [rBI, rBQ], rBIh, rASt, rF2h):
            ALL_RES.extend(r for r in lst if r is not None)

        def dma(dst, src, q="sync", reads=(), writes=(), extra=(), slow=False):
            return s.step("sync" if q == "sync" else "pool",
                          lambda e: e.dma_start(out=dst, in_=src,
                                                allow_slow_non_contiguous=slow),
                          reads=reads, writes=writes, extra=extra, dma=True)

        def vec(f, **kw):
            return s.step("vec", f, **kw)

        def act(f, **kw):
            return s.step("act", f, **kw)

        def pe(f, **kw):
            return s.step("pe", f, **kw)

        def pool(f, **kw):
            return s.step("pool", f, **kw)

        dma(srow[0:1, 0:1], temp.rearrange("(a b) -> a b", b=1), writes=[rSR])
        dma(identb[:, :], c_idb, writes=[rId])
        dma(identf[:, :], c_idf, writes=[rId])
        dma(trimask[:, :], c_tri, writes=[rTri])
        dma(crow[:, :], c_in, writes=[rCrow])
        dma(fscr[0:1, 0:3072], b_attn.rearrange("(a n) -> a n", a=1),
            writes=[rFS[0], rFS[1], rFS[2]])

        pool(lambda e: e.memset(onescol[:, :], 1.0), writes=[rOnes], narrow=True)
        pool(lambda e: e.memset(onesrow[:, :], 1.0), writes=[rOnes], narrow=True)
        pool(lambda e: e.memset(onesrf[0:1, :], 1.0), writes=[rOnes], narrow=True)
        pool(lambda e: e.memset(qpad[:, :], 0.0), writes=rQP, narrow=True)
        pool(lambda e: e.memset(uzpad[:, :], 0.0),
             writes=rUZ[0] + rUZ[1], narrow=True)
        pool(lambda e: e.memset(bvrow[:, :], 0.0), writes=[rBv], narrow=True)
        pool(lambda e: e.memset(rrow[:, :], 0.0), writes=[rRR], narrow=True)
        dma(padc[:, :], c_pad, writes=[rPadc])
        dma(alK[:, :], c_alK, writes=[rALK])
        pool(lambda e: e.memset(mxacc[:, :], 0.0), writes=list(rMX), narrow=True)

        vec(lambda e: e.tensor_scalar(srow[0:1, 1:2], srow[0:1, 0:1], 8.0, None,
                                      ALU.mult), reads=[rSR], writes=[rSR], narrow=True)
        vec(lambda e: e.reciprocal(srow[0:1, 2:3], srow[0:1, 1:2]),
            reads=[rSR], writes=[rSR], narrow=True)

        def bcast(src_ap, col, rds):
            pe(lambda e: e.matmul(pR[:, 0:1], onesrf[0:1, :], src_ap,
                                  start=True, stop=True),
               reads=[rOnes] + rds, writes=[rPR], narrow=True)
            return act(lambda e: e.copy(scal[:, col:col + 1], pR[:, 0:1]),
                       reads=[rPR], writes=[rSC[col]], narrow=True)

        bcast(srow[0:1, 2:3], 0, [rSR])

        for j in range(16):
            pe(lambda e, j=j: e.transpose(pR[0:128, 0:1],
                                          fscr[0:1, j * 128:(j + 1) * 128],
                                          identf[0:1, 0:1]),
               reads=[rFS[j // 8], rFS[2], rId], writes=[rPR], narrow=True)
            act(lambda e, j=j: e.copy(battn[:, j:j + 1], pR[:, 0:1]),
                reads=[rPR], writes=[rBat], narrow=True)
        vec(lambda e: e.tensor_scalar(battn[:, 8:16], battn[:, 8:16], scal[:, 0:1],
                                      None, ALU.mult),
            reads=[rSC[0], rBat], writes=[rBat], narrow=True)
        vec(lambda e: e.tensor_copy(bvrow[0:1, :], fscr[0:1, 2048:3072]),
            reads=[rFS[2]], writes=[rBv], narrow=True)

        dma(fscr[0:1, 3072:4096], b_proj.rearrange("(a n) -> a n", a=1),
            writes=[rFS[3]])
        for j in range(8):
            pe(lambda e, j=j: e.transpose(pR[0:128, 0:1],
                                          fscr[0:1, 3072 + j * 128:3200 + j * 128],
                                          identf[0:1, 0:1]),
               reads=[rFS[3], rId], writes=[rPR], narrow=True)
            act(lambda e, j=j: e.copy(bprojc[:, j:j + 1], pR[:, 0:1]),
                reads=[rPR], writes=[rBpc], narrow=True)

        act(lambda e: e.activation(crow[:, :], crow[:, :], AF.Silu),
            reads=[rCrow], writes=[rCrow], narrow=True)
        for g in range(NCH):
            pe(lambda e, g=g: e.transpose(pR[0:128, 0:BPC],
                                          crow[0:BPC, g * 128:(g + 1) * 128],
                                          identf[0:BPC, 0:BPC]),
               reads=[rCrow, rId], writes=[rPR], narrow=True)
            vec(lambda e, g=g: e.tensor_copy(siluc[:, g * 2:g * 2 + 2], pR[:, 0:BPC]),
                reads=[rPR], writes=[rSil], narrow=True)

        rWCh = [R(), R()]
        rPT[4] = rWCh[0]
        for mo in range(48):
            bsl = mo % 2
            dma(FS[bsl].rearrange("p (g m) -> p g m", m=128),
                wmod[:, mo * 128:(mo + 1) * 128]
                .rearrange("(g p) m -> p g m", p=128), writes=[rFS[bsl]])
            for g in range(NCH):
                pe(lambda e, b2=bsl, g=g: e.matmul(
                    pY[0][:, 0:BPC],
                    FS[b2][:, g * 128:(g + 1) * 128],
                    siluc[:, g * 2:g * 2 + 2], start=(g == 0), stop=(g == NCH - 1)),
                   reads=[rFS[bsl], rSil],
                   writes=([rPYb[0]] if g in (0, NCH - 1) else []))
            comp, g8 = mo // 8, mo % 8
            dst = modv[:, :].rearrange("p (b2 cg) -> p b2 cg", b2=2)[:, :, comp * 8 + g8]
            act(lambda e, dst=dst: e.copy(dst, pY[0][:, 0:BPC]),
                reads=[rPYb[0]], writes=[rMod], narrow=True)

        for comp in (1, 4):
            vec(lambda e, comp=comp: e.tensor_scalar(
                mview[:, :, comp, :], mview[:, :, comp, :], 1.0, None, ALU.add),
                reads=[rMod], writes=[rMod], narrow=True)
        for comp in (2, 5):
            gsl = mview[:, :, comp, :]
            act(lambda e, gsl=gsl: e.activation(
                scal[:, 8:24].rearrange("p (b2 g) -> p b2 g", b2=2), gsl, AF.Sign),
                reads=[rMod], writes=[rSC[8]], narrow=True)
            act(lambda e, gsl=gsl: e.activation(
                scal[:, 24:40].rearrange("p (b2 g) -> p b2 g", b2=2), gsl, AF.Abs),
                reads=[rMod], writes=[rSC[24]], narrow=True)
            vec(lambda e: e.tensor_scalar(scal[:, 24:40], scal[:, 24:40], 0.25, None,
                                          ALU.max),
                reads=[rSC[24]], writes=[rSC[24]], narrow=True)
            vec(lambda e, gsl=gsl: e.tensor_tensor(
                gsl, scal[:, 8:24].rearrange("p (b2 g) -> p b2 g", b2=2),
                scal[:, 24:40].rearrange("p (b2 g) -> p b2 g", b2=2), ALU.mult),
                reads=[rSC[8], rSC[24]], writes=[rMod], narrow=True)
        for b in range(BPC):
            vec(lambda e, b=b: e.tensor_tensor(
                gabv[:, b * 8:(b + 1) * 8], mview[:, b, 2, :], bprojc[:, :], ALU.mult),
                reads=[rMod, rBpc], writes=[rGab], narrow=True)

        s.fence(ALL_RES + rWCh)

        LN_E = {"sb": 40, "sq": None, "sqr": None, "ln": None, "lnr": None}
        LN_SETS = None

        def layernorm(src_res, src_ap, tmp_ap, tmp_res, par=0):
            P_ = LN_SETS[par]
            sb = P_["sb"]
            vec(lambda e: e.reduce_sum(scal[:, sb:sb + 1], src_ap, AX.X),
                reads=[src_res], writes=[rSC[sb]], narrow=True)
            vec(lambda e: e.tensor_scalar(scal[:, sb + 1:sb + 2],
                                          scal[:, sb:sb + 1], 1.0 / C,
                                          None, ALU.mult),
                reads=[rSC[sb]], writes=[rSC[sb + 1]], narrow=True)
            vec(lambda e: e.tensor_scalar(tmp_ap, src_ap, scal[:, sb + 1:sb + 2],
                                          None, ALU.subtract),
                reads=[src_res, rSC[sb + 1]], writes=[tmp_res])
            vec(lambda e: e.tensor_tensor(P_["sq"], tmp_ap, tmp_ap, ALU.mult),
                reads=[tmp_res], writes=P_["sqr"])
            vec(lambda e: e.reduce_sum(scal[:, sb + 2:sb + 3], P_["sq"], AX.X),
                reads=P_["sqr"], writes=[rSC[sb + 2]], narrow=True)
            vec(lambda e: e.tensor_scalar(scal[:, sb + 3:sb + 4],
                                          scal[:, sb + 2:sb + 3], 1.0 / C,
                                          1e-5, ALU.mult, ALU.add),
                reads=[rSC[sb + 2]], writes=[rSC[sb + 3]], narrow=True)
            act(lambda e: e.activation(scal[:, sb + 4:sb + 5],
                                       scal[:, sb + 3:sb + 4], AF.Sqrt),
                reads=[rSC[sb + 3]], writes=[rSC[sb + 4]], narrow=True)
            vec(lambda e: e.reciprocal(scal[:, sb + 5:sb + 6],
                                       scal[:, sb + 4:sb + 5]),
                reads=[rSC[sb + 4]], writes=[rSC[sb + 5]], narrow=True)
            vec(lambda e: e.tensor_scalar(P_["ln"], tmp_ap,
                                          scal[:, sb + 5:sb + 6],
                                          None, ALU.mult),
                reads=[tmp_res, rSC[sb + 5]], writes=P_["lnr"])

        LN_SETS = [
            {"sb": 40, "sq": sqb[:, 0:1024], "sqr": [rSQ],
             "ln": sqb[:, 1024:2048], "lnr": [rSQ]},
            {"sb": 50, "sq": uzpad[:, 0:1024],
             "sqr": [rUZ[0][0], rUZ[0][1], rUZ[1][0], rUZ[1][1]],
             "ln": uzpad[:, 1024:2048],
             "lnr": [rUZ[0][2], rUZ[0][3], rUZ[1][2], rUZ[1][3]]},
        ]

        def ln1_qkv(b):
            for ti in range(NTIL):
                xs = 2 + ti % 2
                par = ti % 2
                tmp, rtmp = (FS[4], rFS[4]) if par == 0 else (FS[5], rFS[5])
                lnb, lnres = LN_SETS[par]["ln"], LN_SETS[par]["lnr"]
                dma(FS[xs], x_in[b, ti * 128:(ti + 1) * 128, :], writes=[rFS[xs]])
                layernorm(rFS[xs], FS[xs], tmp, rtmp, par)
                for g in range(NCH):
                    pp = g % 2
                    pe(lambda e, g=g, pp=pp, lnb=lnb: e.transpose(
                        pT[pp][:, 0:128], lnb[:, g * 128:(g + 1) * 128],
                        identb[:, :]),
                       reads=lnres + [rId], writes=[rPTb[pp]])
                    act(lambda e, b=b, g=g, ti=ti, pp=pp: e.activation(
                        arD[:, g * T + ti * 128:g * T + (ti + 1) * 128],
                        pT[pp][:, 0:128], AF.Identity,
                        bias=mcol(b, 0, g), scale=mcol(b, 1, g)),
                        reads=[rPTb[pp], rMod], writes=[rD_])
            for oc in range(16):
                bsl = oc % 2
                dma(FS[bsl].rearrange("p (g m) -> p g m", m=128),
                    w_attn[:, oc * 128:(oc + 1) * 128]
                    .rearrange("(g p) m -> p g m", p=128), writes=[rFS[bsl]])
                if oc >= 8:
                    vec(lambda e, b2=bsl: e.tensor_scalar(
                        wcast[:, b2 * 1024:(b2 + 1) * 1024], FS[b2], scal[:, 0:1],
                        None, ALU.mult),
                        reads=[rFS[bsl], rSC[0]], writes=[rWCh[bsl]])
                else:
                    vec(lambda e, b2=bsl: e.tensor_copy(
                        wcast[:, b2 * 1024:(b2 + 1) * 1024], FS[b2]),
                        reads=[rFS[bsl]], writes=[rWCh[bsl]])
                for st in range(NST):
                    sp2 = st % 4
                    for g in range(NCH):
                        pe(lambda e, b2=bsl, g=g, st=st, sp2=sp2: e.matmul(
                            pS[sp2][:, :],
                            wcast[:, b2 * 1024 + g * 128:b2 * 1024 + (g + 1) * 128],
                            arD[:, g * T + st * 512:g * T + (st + 1) * 512],
                            start=(g == 0), stop=(g == NCH - 1)),
                           reads=[rWCh[bsl], rD_],
                           writes=([rPSb[sp2]] if g in (0, NCH - 1) else []))
                    dbuf, dres = (arA, rA_) if oc < 8 else (arB, rB_)
                    og = oc % 8
                    act(lambda e, d=dbuf, og=og, st=st, sp2=sp2, oc=oc: e.activation(
                        d[:, og * T + st * 512:og * T + (st + 1) * 512],
                        pS[sp2][:, :], AF.Identity, bias=battn[:, oc:oc + 1], scale=1.0),
                        reads=[rPSb[sp2], rBat], writes=[dres])
            for vh in range(2):
                for g in range(NCH):
                    bsl = g % 2
                    dma(FS[bsl][:, 0:512], w_attn[g * 128:(g + 1) * 128,
                                                  2048 + vh * 512:2560 + vh * 512],
                        writes=[rFS[bsl]])
                    vec(lambda e, b2=bsl, g=g: e.tensor_copy(
                        wpj[:, g * 512:(g + 1) * 512], FS[b2][:, 0:512]),
                        reads=[rFS[bsl]], writes=[rWPJ])
                for tb in range(NTIL):
                    sp2 = tb % 4
                    for g in range(NCH):
                        pe(lambda e, g=g, tb=tb, sp2=sp2: e.matmul(
                            pS[sp2][:, :], arD[:, g * T + tb * 128:g * T + (tb + 1) * 128],
                            wpj[:, g * 512:(g + 1) * 512], start=(g == 0), stop=False),
                           reads=[rWPJ, rD_], writes=([rPSb[sp2]] if g == 0 else []))
                    pe(lambda e, vh=vh, sp2=sp2: e.matmul(
                        pS[sp2][:, :], padc[:, 0:128],
                        bvrow[0:128, vh * 512:(vh + 1) * 512], start=False, stop=True),
                       reads=[rPadc, rBv], writes=[rPSb[sp2]])
                    act(lambda e, tb=tb, vh=vh, sp2=sp2: e.copy(
                        arC[:, tb * 1024 + vh * 512:tb * 1024 + (vh + 1) * 512],
                        pS[sp2][:, :]), reads=[rPSb[sp2]], writes=[rC_])

        def attention(b):
            for g in range(NCH):
                bsl = g % 2
                dma(FS[bsl], w_proj[g * 128:(g + 1) * 128, :], writes=[rFS[bsl]])
                vec(lambda e, b2=bsl, g=g: e.tensor_copy(
                    wpj[:, g * 1024:(g + 1) * 1024], FS[b2]),
                    reads=[rFS[bsl]], writes=[rWPJ])

            pending_epi = [None]
            for h in range(H):
                hb = (h % 2) * 64
                hc = h // 2
                par = h % 2
                if par == 0:
                    act(lambda e, hc=hc: e.activation(
                        sqb[0:128, 0:2048], arB[0:128, hc * T:(hc + 1) * T],
                        AF.Square), reads=[rB_], writes=[rSQ])
                    for qc in range(4):
                        pe(lambda e, qc=qc: e.matmul(
                            pR[0:1, 0:512], padc[:, 256:257],
                            sqb[0:128, qc * 512:(qc + 1) * 512], start=True, stop=True,
                            tile_position=(0, 0), skip_group_check=True),
                           reads=[rPadc, rSQ], writes=[rPR])
                        pe(lambda e, qc=qc: e.matmul(
                            pR[64:65, 0:512], padc[:, 257:258],
                            sqb[0:128, qc * 512:(qc + 1) * 512], start=True, stop=True,
                            tile_position=(0, 64), skip_group_check=True),
                           reads=[rPadc, rSQ], writes=[rPR])
                        vec(lambda e, qc=qc: e.reduce_max(
                            srow[0:1, 4 + qc:5 + qc], pR[0:1, 0:512], AX.X),
                            reads=[rPR], writes=[rSR], narrow=True)
                        vec(lambda e, qc=qc: e.reduce_max(
                            srow[64:65, 4 + qc:5 + qc], pR[64:65, 0:512], AX.X),
                            reads=[rPR], writes=[rSR], narrow=True)
                    vec(lambda e: e.tensor_reduce(srow[0:1, 3:4], srow[0:1, 4:8], AX.X,
                                                  ALU.max),
                        reads=[rSR], writes=[rSR], narrow=True)
                    vec(lambda e: e.tensor_reduce(srow[64:65, 3:4], srow[64:65, 4:8],
                                                  AX.X, ALU.max),
                        reads=[rSR], writes=[rSR], narrow=True)
                    act(lambda e: e.activation(srow[0:1, 3:4], srow[0:1, 3:4], AF.Sqrt),
                        reads=[rSR], writes=[rSR], narrow=True)
                    act(lambda e: e.activation(srow[64:65, 3:4], srow[64:65, 3:4],
                                               AF.Sqrt),
                        reads=[rSR], writes=[rSR], narrow=True)
                    vec(lambda e: e.tensor_scalar(srow[0:1, 3:4], srow[0:1, 3:4], -1.0,
                                                  None, ALU.mult),
                        reads=[rSR], writes=[rSR], narrow=True)
                    vec(lambda e: e.tensor_scalar(srow[64:65, 3:4], srow[64:65, 3:4],
                                                  -1.0, None, ALU.mult),
                        reads=[rSR], writes=[rSR], narrow=True)
                    act(lambda e, hc=hc: e.activation(
                        sqb[0:128, 0:2048], arA[0:128, hc * T:(hc + 1) * T],
                        AF.Square), reads=[rA_], writes=[rSQ])

                for qc in range(4):
                    pe(lambda e, par=par, hb=hb, qc=qc: e.matmul(
                        pR[hb:hb + 1, 0:512], padc[:, 256 + par:257 + par],
                        sqb[0:128, qc * 512:(qc + 1) * 512], start=True, stop=True,
                        tile_position=(0, hb), skip_group_check=True),
                       reads=[rPadc, rSQ], writes=[rPR])
                    act(lambda e, hb=hb: e.activation(
                        fscr[hb:hb + 1, 5120:5632], pR[hb:hb + 1, 0:512], AF.Sqrt),
                        reads=[rPR], writes=[rFS[5]], narrow=True)
                    dma(uzpad[hb:hb + 1, qc * 512:(qc + 1) * 512],
                        c_alF[h:h + 1, qc * 512:(qc + 1) * 512],
                        writes=[rUZ[par][qc]])
                    vec(lambda e, par=par, hb=hb, qc=qc: e.scalar_tensor_tensor(
                        uzpad[hb:hb + 1, qc * 512:(qc + 1) * 512],
                        fscr[hb:hb + 1, 5120:5632], srow[hb:hb + 1, 3:4],
                        uzpad[hb:hb + 1, qc * 512:(qc + 1) * 512],
                        ALU.mult, ALU.subtract),
                        reads=[rFS[5], rSR, rUZ[par][qc]], writes=[rUZ[par][qc]],
                        narrow=True)

                yrow = hb
                sumrow = 64 - hb

                def emit_fill(q2, par=par, hb=hb, hc=hc):
                    qo2 = par * 1024 + (q2 % 2) * 512
                    vec(lambda e, q2=q2, qo2=qo2: e.tensor_copy(
                        qpad[hb:hb + 64, qo2:qo2 + 512],
                        arA[hb:hb + 64, hc * T + q2 * 512:hc * T + (q2 + 1) * 512]),
                        reads=[rA_], writes=[rQP[par * 2 + q2 % 2]])

                emit_fill(0)
                for qs in range(NST):
                    nk = qs * 4 + 4
                    yb = (h * NST + qs) % 2
                    qoff = par * 1024 + (qs % 2) * 512
                    qres = rQP[par * 2 + qs % 2]

                    def pslot(pp):
                        if pp < 4:
                            return spst[:, pp * 512:(pp + 1) * 512], rPT[pp]
                        return wcast[:, 0:512], rPT[4]

                    def emit_s(j, qs=qs, qoff=qoff, qres=qres):
                        co = max(0, j * 128 - qs * 512)
                        N = 512 - co
                        sp2 = j % 5
                        pe(lambda e, hc=hc, j=j, co=co, N=N, sp2=sp2, qoff=qoff:
                           e.matmul(pS[sp2][:, co:co + N],
                                    arB[0:128,
                                        hc * T + j * 128:hc * T + (j + 1) * 128],
                                    qpad[0:128, qoff + co:qoff + co + N],
                                    start=True, stop=False),
                           reads=[qres, rB_], writes=[rPSb[sp2]])
                        pe(lambda e, par=par, j=j, qs=qs, co=co, N=N, sp2=sp2:
                           e.matmul(pS[sp2][:, co:co + N],
                                    padc[:, par * 128:par * 128 + 128],
                                    uzpad[0:128, qs * 512 + co:qs * 512 + co + N],
                                    start=False, stop=True),
                           reads=[rPadc, rUZ[par][qs]], writes=[rPSb[sp2]])
                        if j * 128 == qs * 512 + co:
                            vec(lambda e, co=co, sp2=sp2: e.tensor_tensor(
                                pS[sp2][:, co:co + 128], pS[sp2][:, co:co + 128],
                                trimask[:, :], ALU.min),
                                reads=[rTri], writes=[rPSb[sp2]])
                        pb, pr = pslot(j % 5)
                        act(lambda e, h=h, j=j, co=co, N=N, sp2=sp2, pb=pb:
                            e.activation(
                                pb[:, 0:N], pS[sp2][:, co:co + N],
                                AF.Exp, bias=alK[:, h * 16 + j:h * 16 + j + 1]),
                            reads=[rPSb[sp2], rALK], writes=[pr])

                    def emit_pv(j, qs=qs, yb=yb, nk=nk):
                        co = max(0, j * 128 - qs * 512)
                        N = 512 - co
                        pb, pr = pslot(j % 5)
                        pe(lambda e, h=h, j=j, co=co, N=N, yb=yb, yrow=yrow, pb=pb,
                           nk=nk: e.matmul(
                               pY[yb][yrow:yrow + 64, co:co + N],
                               arC[:, j * 1024 + h * 64:j * 1024 + (h + 1) * 64],
                               pb[:, 0:N],
                               start=(j == 0), stop=(j == nk - 1),
                               tile_position=(0, yrow), skip_group_check=True),
                           reads=[rC_, pr], writes=[rPYb[yb]])
                        pe(lambda e, j=j, co=co, N=N, yb=yb, sumrow=sumrow, pb=pb,
                           nk=nk: e.matmul(
                               pY[yb][sumrow:sumrow + 1, co:co + N],
                               onescol[:, 1:2], pb[:, 0:N],
                               start=(j == 0), stop=(j == nk - 1),
                               tile_position=(0, sumrow), skip_group_check=True),
                           reads=[rOnes, pr], writes=[rPYb[yb]])

                    DEPTH = int(os.environ.get("KERNEL_DEPTH", "4"))
                    for j0 in range(min(DEPTH, nk)):
                        emit_s(j0)
                    if qs + 1 < NST:
                        emit_fill(qs + 1)
                    for j in range(nk):
                        if j + DEPTH < nk:
                            emit_s(j + DEPTH)
                        emit_pv(j)
                        if j == 1 and pending_epi[0] is not None:
                            pending_epi[0]()
                            pending_epi[0] = None
                    if pending_epi[0] is not None:
                        pending_epi[0]()
                        pending_epi[0] = None

                    def epilogue(yb=yb, sumrow=sumrow, yrow=yrow, hc=hc, qs=qs):
                        vec(lambda e, yb=yb, sumrow=sumrow: e.reciprocal(
                            rrow[sumrow:sumrow + 1, :],
                            pY[yb][sumrow:sumrow + 1, :]),
                            reads=[rPYb[yb]], writes=[rRR], narrow=True)
                        pe(lambda e, sumrow=sumrow, yrow=yrow: e.matmul(
                            pR[yrow:yrow + 64, :],
                            padc[:, (sumrow * 2):(sumrow * 2) + 64],
                            rrow[0:128, :], start=True, stop=True,
                            tile_position=(0, yrow), skip_group_check=True),
                           reads=[rPadc, rRR], writes=[rPR])
                        act(lambda e, yrow=yrow: e.copy(
                            fscr2[yrow:yrow + 64, 512:1024], pR[yrow:yrow + 64, :]),
                            reads=[rPR], writes=[rF2])
                        vec(lambda e, hc=hc, qs=qs, yb=yb, yrow=yrow: e.tensor_tensor(
                            arD[yrow:yrow + 64, hc * T + qs * 512:hc * T + (qs + 1) * 512],
                            pY[yb][yrow:yrow + 64, :], fscr2[yrow:yrow + 64, 512:1024],
                            ALU.mult), reads=[rPYb[yb], rF2], writes=[rD_])

                    if os.environ.get("KERNEL_NO_DEFER"):
                        epilogue()
                    else:
                        pending_epi[0] = epilogue

            if pending_epi[0] is not None:
                pending_epi[0]()
                pending_epi[0] = None

            for qs in range(NST):
                for oc in range(NCH):
                    sp2 = oc % 3
                    for g in range(NCH):
                        pe(lambda e, g=g, oc=oc, qs=qs, sp2=sp2: e.matmul(
                            pS[sp2][:, :],
                            wpj[:, g * 1024 + oc * 128:g * 1024 + (oc + 1) * 128],
                            arD[:, g * T + qs * 512:g * T + (qs + 1) * 512],
                            start=(g == 0), stop=(g == NCH - 1)),
                           reads=[rWPJ, rD_],
                           writes=([rPSb[sp2]] if g in (0, NCH - 1) else []))
                    tgt, trs = (wcast, [rWCh[0], rWCh[1]]) if oc < 4 else (spst, [rSP])
                    to = (oc % 4) * 512
                    act(lambda e, b=b, oc=oc, tgt=tgt, to=to, sp2=sp2: e.activation(
                        tgt[:, to:to + 512], pS[sp2][:, :], AF.Identity,
                        scale=mcol(b, 2, oc),
                        bias=gabv[:, b * 8 + oc:b * 8 + oc + 1]),
                        reads=[rPSb[sp2], rMod, rGab], writes=trs)
                for tb in range(4):
                    ti = qs * 4 + tb
                    xs = 2 + tb % 2
                    par = tb % 2
                    acc, racc = (FS[4], rFS[4]) if par == 0 else (FS[1], rFS[1])
                    tmp, rtmp = (FS[5], rFS[5]) if par == 0 else (FS[0], rFS[0])
                    lnb, lnres = LN_SETS[par]["ln"], LN_SETS[par]["lnr"]
                    h2b, h2res = LN_SETS[par]["sq"], LN_SETS[par]["sqr"]
                    mc = 46 + par * 10
                    dma(FS[xs], x_in[b, ti * 128:(ti + 1) * 128, :], writes=[rFS[xs]])
                    for oc in range(NCH):
                        tgt, trs = (wcast, [rWCh[0], rWCh[1]]) if oc < 4 else (spst, [rSP])
                        to = (oc % 4) * 512
                        pp = oc % 2
                        pe(lambda e, tgt=tgt, to=to, tb=tb, pp=pp: e.transpose(
                            pT[pp][:, 0:128], tgt[:, to + tb * 128:to + (tb + 1) * 128],
                            identb[:, :]),
                           reads=trs + [rId], writes=[rPTb[pp]])
                        vec(lambda e, oc=oc, xs=xs, pp=pp, acc=acc: e.tensor_tensor(
                            acc[:, oc * 128:(oc + 1) * 128], pT[pp][:, 0:128],
                            FS[xs][:, oc * 128:(oc + 1) * 128], ALU.add),
                            reads=[rPTb[pp], rFS[xs]], writes=[racc])
                    dma(x1sp[b, ti * 128:(ti + 1) * 128, :], acc, q="pool",
                        reads=[racc], writes=[rX1sp[b]])
                    layernorm(racc, acc, tmp, rtmp, par)
                    for g in range(NCH):
                        pp = g % 2
                        pe(lambda e, g=g, pp=pp, lnb=lnb: e.transpose(
                            pT[pp][:, 0:128], lnb[:, g * 128:(g + 1) * 128],
                            identb[:, :]),
                           reads=lnres + [rId], writes=[rPTb[pp]])
                        act(lambda e, b=b, g=g, pp=pp, h2b=h2b: e.activation(
                            h2b[:, g * 128:(g + 1) * 128], pT[pp][:, 0:128],
                            AF.Identity,
                            bias=mcol(b, 3, g), scale=mcol(b, 4, g)),
                            reads=[rPTb[pp], rMod], writes=h2res)
                    dma(h2sp[b, :, :, ti * 128:(ti + 1) * 128]
                        .rearrange("g p t -> p g t"),
                        h2b.rearrange("p (g t) -> p g t", t=128),
                        q="pool", reads=h2res, writes=[rH2sp[b]])
                    vec(lambda e, h2b=h2b, mc=mc: e.tensor_reduce(
                        scal[:, mc:mc + 1], h2b, AX.X, ALU.max,
                        apply_absolute_value=True),
                        reads=h2res, writes=[rSC[mc]], narrow=True)
                    vec(lambda e, b=b, mc=mc: e.tensor_tensor(
                        mxacc[:, b:b + 1], mxacc[:, b:b + 1], scal[:, mc:mc + 1],
                        ALU.max),
                        reads=[rSC[mc]], writes=[rMX[b]], narrow=True)

        def scale_pair(mxres, mx_ap, colm, coli):
            pe(lambda e: e.transpose(pR[0:1, 0:128], mx_ap, identf[:, :]),
               reads=[mxres, rId], writes=[rPR], narrow=True)
            vec(lambda e: e.reduce_max(srow[0:1, 8:9], pR[0:1, 0:128], AX.X),
                reads=[rPR], writes=[rSR], narrow=True)
            vec(lambda e: e.tensor_scalar(srow[0:1, 9:10], srow[0:1, 8:9],
                                          1.0 / 32767.0, 1e-12, ALU.mult, ALU.max),
                reads=[rSR], writes=[rSR], narrow=True)
            vec(lambda e: e.reciprocal(srow[0:1, 10:11], srow[0:1, 9:10]),
                reads=[rSR], writes=[rSR], narrow=True)
            bcast(srow[0:1, 9:10], colm, [rSR])
            bcast(srow[0:1, 10:11], coli, [rSR])

        def dqsl(w, g, oc):
            off = w * (NCH * FF) + g * FF + oc * 128
            if off < 16384:
                return arA[:, off:off + 128]
            if off < 32768:
                return arB[:, off - 16384:off - 16384 + 128]
            return arC[:, off - 32768:off - 32768 + 128]

        def lpbq_upgate(raw, w, oc):
            bsl = oc % 2
            dma(FS[bsl], raw[oc * 128:(oc + 1) * 128, :], writes=[rFS[bsl]])
            vec(lambda e, b2=bsl: e.tensor_reduce(
                scal[:, 32:64].rearrange("p (n w) -> p n w", w=1),
                FS[b2].rearrange("p (n w) -> p n w", w=32), AX.X, ALU.max,
                apply_absolute_value=True),
                reads=[rFS[bsl]], writes=[rSC[32]], narrow=True)
            vec(lambda e: e.tensor_scalar(scal[:, 32:64], scal[:, 32:64], 1.0 / 7.0,
                                          1e-12, ALU.mult, ALU.max),
                reads=[rSC[32]], writes=[rSC[32]], narrow=True)
            vec(lambda e: e.reciprocal(fscr2[:, 0:32], scal[:, 32:64]),
                reads=[rSC[32]], writes=[rF2], narrow=True)
            vec(lambda e, b2=bsl: e.tensor_tensor(
                iscr[:, 0:1024].rearrange("p (n w) -> p n w", w=32),
                FS[b2].rearrange("p (n w) -> p n w", w=32),
                fscr2[:, 0:32].unsqueeze(2).broadcast_to([128, 32, 32]), ALU.mult),
                reads=[rFS[bsl], rF2], writes=[rIS])
            vec(lambda e: e.tensor_tensor(
                sqb[:, 0:1024].rearrange("p (n w) -> p n w", w=32),
                iscr[:, 0:1024].rearrange("p (n w) -> p n w", w=32),
                scal[:, 32:64].unsqueeze(2).broadcast_to([128, 32, 32]), ALU.mult),
                reads=[rIS, rSC[32]], writes=[rSQ])
            for g in range(NCH):
                pp = g % 2
                pe(lambda e, g=g, pp=pp: e.transpose(
                    pT[pp][:, 0:128], sqb[:, g * 128:(g + 1) * 128], identb[:, :]),
                   reads=[rSQ, rId], writes=[rPTb[pp]])
                act(lambda e, w=w, g=g, oc=oc, pp=pp: e.copy(
                    dqsl(w, g, oc), pT[pp][:, 0:128]),
                    reads=[rPTb[pp]], writes=[rDQ])

        def ffn_A_load(b, st):
            xoff = (st % 3) * 4096
            rX = rXp[st % 3]
            dma(arD[:, xoff:xoff + 4096].rearrange("p (g t) -> p g t", t=512),
                h2sp[b, :, :, st * 512:(st + 1) * 512].rearrange("g p t -> p g t"),
                reads=[rH2sp[b]], writes=[rX])
            for half in range(4):
                ho = half * 1024
                vec(lambda e, xoff=xoff, ho=ho, rX=rX: e.tensor_scalar(
                    iscr[:, :], arD[:, xoff + ho:xoff + ho + 1024],
                    scal[:, 17:18], None, ALU.mult),
                    reads=[rX, rSC[17]], writes=[rIS])
                vec(lambda e, xoff=xoff, ho=ho, rX=rX: e.tensor_scalar(
                    arD[:, xoff + ho:xoff + ho + 1024], iscr[:, :],
                    scal[:, 16:17], None, ALU.mult),
                    reads=[rIS, rSC[16]], writes=[rX])

        def ffn_A_mm(b, w, st):
            xoff = (st % 3) * 4096
            rX = rXp[st % 3]
            dsp, rdsp, mcol = ((upsp, rUpsp, 2), (gasp, rGasp, 4))[w]
            for oc in range(FCH):
                sp2 = oc % 4
                for g in range(NCH):
                    pe(lambda e, w=w, g=g, oc=oc, xoff=xoff, sp2=sp2: e.matmul(
                        pS[sp2][:, :], dqsl(w, g, oc),
                        arD[:, xoff + g * 512:xoff + (g + 1) * 512],
                        start=(g == 0), stop=(g == NCH - 1)),
                       reads=[rDQ, rX],
                       writes=([rPSb[sp2]] if g in (0, NCH - 1) else []))
                sl = oc % 8
                gr = (oc // 4) % 2
                if w == 1:
                    act(lambda e, sl=sl, sp2=sp2: e.copy(
                        arD[:, 12288 + sl * 512:12800 + sl * 512], pS[sp2][:, :]),
                        reads=[rPSb[sp2]], writes=[rASt[gr]])
                else:
                    vec(lambda e, sl=sl, sp2=sp2: e.tensor_copy(
                        arD[:, 12288 + sl * 512:12800 + sl * 512], pS[sp2][:, :]),
                        reads=[rPSb[sp2]], writes=[rASt[gr]])
                if sl % 4 == 3:
                    go = 12288 + (sl - 3) * 512
                    dma(dsp[b, oc - 3:oc + 1, :, st * 512:(st + 1) * 512]
                        .rearrange("o p t -> p o t"),
                        arD[:, go:go + 2048].rearrange("p (o t) -> p o t", t=512),
                        q="pool", reads=[rASt[gr]], writes=[rdsp[b]])
                    vec(lambda e, go=go: e.tensor_reduce(
                        scal[:, 46:47], arD[:, go:go + 2048], AX.X, ALU.max,
                        apply_absolute_value=True),
                        reads=[rASt[gr]], writes=[rSC[46]], narrow=True)
                    vec(lambda e, b=b, mcol=mcol: e.tensor_tensor(
                        mxacc[:, mcol + b:mcol + b + 1],
                        mxacc[:, mcol + b:mcol + b + 1],
                        scal[:, 46:47], ALU.max),
                        reads=[rSC[46]], writes=[rMX[mcol + b]], narrow=True)

        uzI = uzpad[:, :].bitcast(I32)
        spI = spst[:, :].bitcast(I32)

        def ibuf(h):
            q = h % 4
            src = uzI if q < 2 else spI
            return src[:, (q % 2) * 512:(q % 2) * 512 + 512]
        f2bf = fscr2[:, :].bitcast(BF16)[:, 1024:2048]
        f2bf2 = fscr2[:, :].bitcast(BF16)[:, 0:1024]

        def spB(b):
            scale_pair(rMX[2 + b], mxacc[:, 2 + b:3 + b], 18, 19)
            scale_pair(rMX[4 + b], mxacc[:, 4 + b:5 + b], 20, 21)

        def ffn_B(b, ocs=None):
            ocs = list(range(FCH)) if ocs is None else list(ocs)
            segs = [(oc, sg) for oc in ocs for sg in range(4)]
            n = len(segs)

            def b_load(oc, hf):
                dma(wcast[:, hf * 1024:(hf + 1) * 1024],
                    gasp[b, oc, :, hf * 1024:(hf + 1) * 1024],
                    reads=[rGasp[b]], writes=[rWCh[hf]])

            def b_int(g):
                oc, sg = segs[g]
                so = sg * 512
                ui, ru = ibuf(g), rBIh[g % 4]
                ex = [rSP] if g % 4 >= 2 else []
                vec(lambda e, so=so, ui=ui: e.tensor_scalar(
                    ui, wcast[:, so:so + 512], scal[:, 21:22], None,
                    ALU.mult), reads=[rWCh[sg // 2], rSC[21]] + ex, writes=[ru])

            def b_act(g):
                oc, sg = segs[g]
                so = sg * 512
                hf = sg // 2
                ui, ru = ibuf(g), rBIh[g % 4]
                act(lambda e, so=so, ui=ui: e.activation(
                    fscr[:, 4096 + so:4608 + so], ui, AF.Sigmoid,
                    scale=scal[:, 20:21]),
                    reads=[ru, rSC[20]], writes=[rFS[4 + hf]])
                act(lambda e, so=so, ui=ui: e.activation(
                    sqb[:, so:so + 512], ui, AF.Identity,
                    scale=scal[:, 20:21]),
                    reads=[ru, rSC[20]], writes=[rSQ])

            def b_fq(g):
                oc, sg = segs[g]
                so = sg * 512
                hf = sg // 2
                ui, ru = ibuf(g), rBIh[g % 4]
                vec(lambda e, so=so, ui=ui: e.tensor_scalar(
                    ui, fscr[:, 4096 + so:4608 + so], 65536.0, None,
                    ALU.mult), reads=[rFS[4 + hf]], writes=[ru])
                vec(lambda e, so=so, ui=ui: e.tensor_scalar(
                    qpad[:, so:so + 512], ui, 65535.0,
                    1.0 / 65536.0, ALU.min, ALU.mult),
                    reads=[ru], writes=[rBQ])

            def b_gs(oc, hf):
                ho = hf * 1024
                fb = (f2bf2, f2bf)[hf]
                vec(lambda e, ho=ho, fb=fb: e.tensor_tensor(
                    fb[:, :], sqb[:, ho:ho + 1024], qpad[:, ho:ho + 1024],
                    ALU.mult), reads=[rSQ, rBQ], writes=[rF2h[hf]])
                dma(gssp[b, oc, :, ho:ho + 1024], fb[:, :], q="pool",
                    reads=[rF2h[hf]], writes=[rGssp[b]])
                vec(lambda e, b=b, fb=fb: e.tensor_reduce(
                    mxacc[:, 10 + b:11 + b], fb[:, :], AX.X,
                    ALU.max, apply_absolute_value=True),
                    reads=[rF2h[hf]], writes=[rMX[10 + b]], narrow=True)
                vec(lambda e, b=b: e.tensor_tensor(
                    mxacc[:, 6 + b:7 + b], mxacc[:, 6 + b:7 + b],
                    mxacc[:, 10 + b:11 + b],
                    ALU.max), reads=[rMX[10 + b]], writes=[rMX[6 + b]], narrow=True)

            b_load(segs[0][0], 0)
            b_load(segs[0][0], 1)
            for g in range(n + 2):
                if g < n:
                    b_int(g)
                    oc, sg = segs[g]
                    if sg == 1 and g + 3 < n:
                        b_load(segs[g + 3][0], 0)
                    if sg == 3 and g + 1 < n:
                        b_load(segs[g + 1][0], 1)
                if g >= 1 and g - 1 < n:
                    b_act(g - 1)
                if g >= 2 and g - 2 < n:
                    b_fq(g - 2)
                    oc2, sg2 = segs[g - 2]
                    if sg2 == 1:
                        b_gs(oc2, 0)
                    elif sg2 == 3:
                        b_gs(oc2, 1)

        def spC(b):
            scale_pair(rMX[6 + b], mxacc[:, 6 + b:7 + b], 22, 23)

        fsbf = fscr[:, :].bitcast(BF16)[:, 8192:12288]
        rCG = [R(), R()]
        ALL_RES.extend(rCG)

        def ffn_C(b, ocs=None):
            ocs = list(range(FCH)) if ocs is None else list(ocs)

            def c_gload(i):
                dma(fsbf[:, (i % 2) * 2048:(i % 2) * 2048 + 2048],
                    gssp[b, ocs[i], :, :],
                    reads=[rGssp[b]], writes=[rCG[i % 2], rFS[4], rFS[5]])

            c_gload(0)
            for i, oc in enumerate(ocs):
                gb = fsbf[:, (i % 2) * 2048:(i % 2) * 2048 + 2048]
                rg = rCG[i % 2]
                for hf in range(2):
                    fb = (f2bf2, f2bf)[hf]
                    dma(fb[:, :], upsp[b, oc, :, hf * 1024:(hf + 1) * 1024],
                        reads=[rUpsp[b]], writes=[rF2h[hf], rF2])
                if i + 1 < len(ocs):
                    c_gload(i + 1)

                def cbuf(sg):
                    q = sg % 3
                    if q < 2:
                        return uzI[:, q * 512:q * 512 + 512], rBIh[q]
                    return ibufC[:, :], rBIh[2]

                def c_int1(sg):
                    so = sg * 512
                    ui, ru = cbuf(sg)
                    vec(lambda e, so=so, ui=ui, gb=gb: e.tensor_scalar(
                        ui, gb[:, so:so + 512], scal[:, 23:24], None,
                        ALU.mult), reads=[rg, rSC[23]],
                        writes=[ru])

                def c_act1(sg):
                    so = sg * 512
                    ui, ru = cbuf(sg)
                    vec(lambda e, so=so, ui=ui: e.tensor_scalar(
                        spst[:, so:so + 512], ui, scal[:, 22:23], None,
                        ALU.mult),
                        reads=[ru, rSC[22]], writes=[rSP])

                for sg in range(4):
                    c_int1(sg)
                    if sg >= 2:
                        c_act1(sg - 2)
                c_act1(2)
                c_act1(3)

                def c_int2(sg):
                    so = sg * 512
                    fb = (f2bf2, f2bf)[sg // 2]
                    ui, ru = cbuf(sg)
                    vec(lambda e, so=so, ui=ui, fb=fb, sg=sg: e.tensor_scalar(
                        ui, fb[:, (sg % 2) * 512:(sg % 2) * 512 + 512],
                        scal[:, 19:20], None,
                        ALU.mult), reads=[rF2h[sg // 2], rSC[19]],
                        writes=[ru])

                def c_act2(sg):
                    so = sg * 512
                    ui, ru = cbuf(sg)
                    vec(lambda e, so=so, ui=ui: e.tensor_scalar(
                        qpad[:, so:so + 512], ui, scal[:, 18:19], None,
                        ALU.mult),
                        reads=[ru, rSC[18]], writes=[rBQ])

                for sg in range(4):
                    c_int2(sg)
                    if sg >= 2:
                        c_act2(sg - 2)
                c_act2(2)
                c_act2(3)
                vec(lambda e: e.tensor_tensor(spst[:, :], spst[:, :], qpad[:, :],
                                              ALU.mult),
                    reads=[rBQ], writes=[rSP])
                dma(osp[b, oc, :, :], spst[:, :], q="pool",
                    reads=[rSP], writes=[rOsp[b]])
                vec(lambda e, b=b: e.tensor_reduce(
                    mxacc[:, 12 + b:13 + b], spst[:, :], AX.X,
                    ALU.max, apply_absolute_value=True),
                    reads=[rSP], writes=[rMX[12 + b]], narrow=True)
                vec(lambda e, b=b: e.tensor_tensor(
                    mxacc[:, 8 + b:9 + b], mxacc[:, 8 + b:9 + b],
                    mxacc[:, 12 + b:13 + b],
                    ALU.max), reads=[rMX[12 + b]], writes=[rMX[8 + b]], narrow=True)

        def dqdl(g, co):
            off = co * FF + g * 128
            if off < 16384:
                return arA[:, off:off + 128]
            return arB[:, off - 16384:off - 16384 + 128]

        def lpbq_down(cos=None):
            for co in (range(NCH) if cos is None else cos):
                for third in range(3):
                    tsl = third % 2
                    dma(FS[tsl], w_down[co * 128:(co + 1) * 128,
                                        third * 1024:(third + 1) * 1024],
                        writes=[rFS[tsl]])
                    vec(lambda e, t2=tsl: e.tensor_reduce(
                        scal[:, 32:64].rearrange("p (n w) -> p n w", w=1),
                        FS[t2].rearrange("p (n w) -> p n w", w=32), AX.X, ALU.max,
                        apply_absolute_value=True),
                        reads=[rFS[tsl]], writes=[rSC[32]], narrow=True)
                    vec(lambda e: e.tensor_scalar(scal[:, 32:64], scal[:, 32:64],
                                                  1.0 / 7.0, 1e-12, ALU.mult, ALU.max),
                        reads=[rSC[32]], writes=[rSC[32]], narrow=True)
                    vec(lambda e: e.reciprocal(fscr2[:, 0:32], scal[:, 32:64]),
                        reads=[rSC[32], rF2h[0], rF2h[1]], writes=[rF2],
                        narrow=True)
                    vec(lambda e, t2=tsl: e.tensor_tensor(
                        iscr[:, 0:1024].rearrange("p (n w) -> p n w", w=32),
                        FS[t2].rearrange("p (n w) -> p n w", w=32),
                        fscr2[:, 0:32].unsqueeze(2).broadcast_to([128, 32, 32]),
                        ALU.mult), reads=[rFS[tsl], rF2], writes=[rIS])
                    vec(lambda e: e.tensor_tensor(
                        f2bf[:, 0:1024].rearrange("p (n w) -> p n w", w=32),
                        iscr[:, 0:1024].rearrange("p (n w) -> p n w", w=32),
                        scal[:, 32:64].unsqueeze(2).broadcast_to([128, 32, 32]),
                        ALU.mult), reads=[rIS, rSC[32]], writes=[rF2])
                    for g8 in range(NCH):
                        g = third * 8 + g8
                        pp = g8 % 2
                        pe(lambda e, g8=g8, pp=pp: e.transpose(
                            pT[pp][:, 0:128], f2bf[:, g8 * 128:(g8 + 1) * 128],
                            identb[:, :]),
                           reads=[rF2, rId], writes=[rPTb[pp]])
                        act(lambda e, g=g, co=co, pp=pp: e.copy(
                            dqdl(g, co), pT[pp][:, 0:128]),
                            reads=[rPTb[pp]], writes=[rDQ])

        out_evs = []

        def spD(b):
            scale_pair(rMX[8 + b], mxacc[:, 8 + b:9 + b], 24, 25)

        def ffn_D(b, sts=None):
            for st in (range(NST) if sts is None else sts):
                for grp in range(6):
                    go = grp * 2048
                    rX = rXp[grp % 3]
                    dma(arD[:, go:go + 2048].rearrange("p (g t) -> p g t", t=512),
                        osp[b, grp * 4:(grp + 1) * 4, :, st * 512:(st + 1) * 512]
                        .rearrange("g p t -> p g t"),
                        reads=[rOsp[b], rASt[0], rASt[1]], writes=[rX])
                    for hf in range(2):
                        ho = go + hf * 1024
                        vec(lambda e, ho=ho, rX=rX: e.tensor_scalar(
                            iscr[:, :], arD[:, ho:ho + 1024], scal[:, 25:26], None,
                            ALU.mult), reads=[rX, rSC[25]], writes=[rIS])
                        vec(lambda e, ho=ho, rX=rX: e.tensor_scalar(
                            arD[:, ho:ho + 1024], iscr[:, :], scal[:, 24:25], None,
                            ALU.mult), reads=[rIS, rSC[24]], writes=[rX])
                for co in range(NCH):
                    sp2 = co % 3
                    for g in range(FCH):
                        pe(lambda e, g=g, co=co, sp2=sp2: e.matmul(
                            pS[sp2][:, :], dqdl(g, co),
                            arD[:, g * 512:(g + 1) * 512],
                            start=(g == 0), stop=(g == FCH - 1)),
                           reads=[rDQ, rXp[0], rXp[1]],
                           writes=([rPSb[sp2]] if g in (0, FCH - 1) else []))
                    do = 12288 + co * 512
                    act(lambda e, b=b, co=co, do=do, sp2=sp2: e.activation(
                        arD[:, do:do + 512], pS[sp2][:, :], AF.Identity,
                        scale=mcol(b, 5, co), bias=0.0),
                        reads=[rPSb[sp2], rMod, rASt[0], rASt[1]],
                        writes=[rDS[co % 2]])
                for tb in range(4):
                    ti = st * 4 + tb
                    xs = 2 + tb % 2
                    dma(FS[xs], x1sp[b, ti * 128:(ti + 1) * 128, :],
                        reads=[rX1sp[b]], writes=[rFS[xs]])
                    for co in range(NCH):
                        do = 12288 + co * 512
                        pp = co % 2
                        pe(lambda e, do=do, tb=tb, pp=pp, co=co: e.transpose(
                            pT[pp][:, 0:128],
                            arD[:, do + tb * 128:do + (tb + 1) * 128],
                            identb[:, :]),
                           reads=[rDS[co % 2], rId], writes=[rPTb[pp]])
                        vec(lambda e, co=co, xs=xs, pp=pp: e.tensor_tensor(
                            FS[0][:, co * 128:(co + 1) * 128], pT[pp][:, 0:128],
                            FS[xs][:, co * 128:(co + 1) * 128], ALU.add),
                            reads=[rPTb[pp], rFS[xs]], writes=[rFS[0]])
                    ev = dma(out[b, ti * 128:(ti + 1) * 128, :], FS[0], q="pool",
                             reads=[rFS[0]])
                    out_evs.append(ev)

        _dbg = os.environ.get("KERNEL_DEBUG_STOP", "")
        for b in range(BPC):
            ln1_qkv(b)
            attention(b)
            s.fence(ALL_RES + rWCh)
            if _dbg == "attn%d" % b:
                break

        if not _dbg:
          for oc in range(FCH):
            lpbq_upgate(w_up, 0, oc)
          for oc in range(FCH):
            lpbq_upgate(w_gate, 1, oc)
          s.fence(ALL_RES + rWCh)

          scale_pair(rMX[0], mxacc[:, 0:1], 16, 17)
          ffn_A_load(0, 0)
          ffn_A_load(0, 1)
          ffn_A_mm(0, 1, 0)
          ffn_A_load(0, 2)
          ffn_A_mm(0, 1, 1)
          ffn_A_load(0, 3)
          ffn_A_mm(0, 1, 2)
          ffn_A_mm(0, 1, 3)
          scale_pair(rMX[4], mxacc[:, 4:5], 20, 21)
          ffn_A_load(0, 0)
          ffn_A_load(0, 1)
          ffn_A_mm(0, 0, 0)
          ffn_A_load(0, 2)
          ffn_B(0, range(0, 6))
          ffn_A_mm(0, 0, 1)
          ffn_A_load(0, 3)
          ffn_B(0, range(6, 12))
          ffn_A_mm(0, 0, 2)
          ffn_B(0, range(12, 18))
          ffn_A_mm(0, 0, 3)
          ffn_B(0, range(18, 24))
          scale_pair(rMX[2], mxacc[:, 2:3], 18, 19)
          scale_pair(rMX[1], mxacc[:, 1:2], 16, 17)
          ffn_A_load(1, 0)
          ffn_A_load(1, 1)
          ffn_A_mm(1, 1, 0)
          ffn_A_load(1, 2)
          spC(0)
          ffn_C(0, range(0, 6))
          ffn_A_mm(1, 1, 1)
          ffn_A_load(1, 3)
          ffn_C(0, range(6, 12))
          ffn_A_mm(1, 1, 2)
          ffn_C(0, range(12, 18))
          ffn_A_mm(1, 1, 3)
          ffn_C(0, range(18, 24))
          scale_pair(rMX[5], mxacc[:, 5:6], 20, 21)
          ffn_A_load(1, 0)
          ffn_A_load(1, 1)
          ffn_A_mm(1, 0, 0)
          ffn_A_load(1, 2)
          ffn_B(1, range(0, 6))
          ffn_A_mm(1, 0, 1)
          ffn_A_load(1, 3)
          ffn_B(1, range(6, 12))
          ffn_A_mm(1, 0, 2)
          ffn_B(1, range(12, 18))
          ffn_A_mm(1, 0, 3)
          ffn_B(1, range(18, 24))
          scale_pair(rMX[3], mxacc[:, 3:4], 18, 19)
          lpbq_down()
          spD(0)
          ffn_D(0, [0])
          spC(1)
          ffn_C(1, range(0, 6))
          ffn_D(0, [1])
          ffn_C(1, range(6, 12))
          ffn_D(0, [2])
          ffn_C(1, range(12, 18))
          ffn_D(0, [3])
          ffn_C(1, range(18, 24))
          spD(1)
          ffn_D(1)

        final_deps = out_evs[-1:]
        s.step("sync", lambda e: e.nop(), extra=[s.last["pool"], s.last["vec"],
                                                 s.last["act"], s.last["pe"]] + final_deps)

        plans = s.finalize()
        sem_handles = {}
        for i, sid in enumerate(s.sem_ids):
            sem_handles[sid] = ctx.enter_context(nc.semaphore("c%d" % i))

        def runner(eng_name):
            def go(e):
                for waits, f, ev in plans[eng_name]:
                    for sem_eng, val in waits:
                        e.wait_ge(sem_handles[sem_eng], val)
                    ins = f(e)
                    if ev.sem is not None:
                        ins.then_inc(sem_handles[ev.sem], ev.amt)
            return go

        with nc.Block() as blk:
            blk.sync(runner("sync"))
            blk.vector(runner("vec"))
            blk.scalar(runner("act"))
            blk.tensor(runner("pe"))
            blk.gpsimd(runner("pool"))
    return nc


_prog_cache = [None]


def kernel(**inputs):
    x = np.ascontiguousarray(np.asarray(inputs["L0_attn_x"], np.float32))
    cv = np.ascontiguousarray(np.asarray(inputs["L0_c"], np.float32))
    full = {
        "w_attn": np.ascontiguousarray(np.asarray(inputs["L0_attn_W_attn"], np.float32)),
        "b_attn": np.ascontiguousarray(np.asarray(inputs["L0_attn_b_attn"], np.float32)),
        "w_proj": np.ascontiguousarray(np.asarray(inputs["L0_attn_W_proj"], np.float32)),
        "b_proj": np.ascontiguousarray(np.asarray(inputs["L0_attn_b_proj"], np.float32)),
        "temp": np.ascontiguousarray(np.asarray(inputs["L0_attn_temperature"], np.float32)),
        "w_gate": np.ascontiguousarray(np.asarray(inputs["L0_ffn_w_gate"], np.float32)),
        "w_up": np.ascontiguousarray(np.asarray(inputs["L0_ffn_w_up"], np.float32)),
        "w_down": np.ascontiguousarray(np.asarray(inputs["L0_ffn_w_down"], np.float32)),
        "wmod": np.ascontiguousarray(np.asarray(inputs["L0_wmod"], np.float32)),
    }
    if _prog_cache[0] is None:
        _prog_cache[0] = build_program()
    nc = _prog_cache[0]
    in_maps = []
    for core in range(8):
        m = dict(full)
        m["x"] = np.ascontiguousarray(x[core * BPC:(core + 1) * BPC])
        m["cvec"] = np.ascontiguousarray(cv[core * BPC:(core + 1) * BPC])
        in_maps.append(m)
    res = run_bass_kernel_spmd(nc, in_maps, core_ids=list(range(8)))
    return np.concatenate([np.asarray(r["out"], np.float32) for r in res.results],
                          axis=0)
